# version 09
# baseline (speedup 1.0000x reference)
import numpy as np
import ml_dtypes
import concourse.bass as bass
import concourse.mybir as mybir
from concourse.bass_utils import run_bass_kernel_spmd
from contextlib import ExitStack

F32 = mybir.dt.float32
BF16 = mybir.dt.bfloat16
AF = mybir.ActivationFunctionType
ALU = mybir.AluOpType
AX = mybir.AxisListType
BF = ml_dtypes.bfloat16

N_CORES = 8
B_LOC = 2
S = 2048
D = 2048
T = B_LOC * S
LN_EPS = 1e-5

MAGIC = 3 * 2.0 ** 22


class Sem:

    def __init__(self, nc, stack, name):
        self.h = stack.enter_context(nc.semaphore(name))
        self.v = 0

    def inc(self, inst, k=1):
        inst.then_inc(self.h, k)
        self.v += k
        return self.v


def cdiv(a, b):
    return (a + b - 1) // b


def lp(nc, fn, reason="bf16 transpose data movement (no accumulation)"):
    def wrapped(eng):
        with nc.allow_low_precision(reason=reason):
            return fn(eng)
    return wrapped



ENGINES = ("sync", "scalar", "vector", "tensor", "gpsimd")


class Prog:
    def __init__(self, nc, stack, pfx, pool_size=44):
        self.nc = nc
        self.stack = stack
        self.pfx = pfx
        self.ops = {e: [] for e in ENGINES}
        self.sems = {e: Sem(nc, stack, f"{pfx}_{e}") for e in ENGINES}
        self.cnt = {e: self.sems[e].v for e in ENGINES}
        self.pool = [Sem(nc, stack, f"{pfx}_d{i}") for i in range(28)]
        self.pool_sw = [Sem(nc, stack, f"{pfx}_s{i}") for i in range(10)]
        self.pool_cp = [Sem(nc, stack, f"{pfx}_c{i}") for i in range(6)]
        self.pool_r = [Sem(nc, stack, f"{pfx}_r{i}") for i in range(18)]
        self.binding = {}

    def phase_reset(self):
        self.binding = {k: v for k, v in self.binding.items()
                        if k.startswith("R_")}

    def _sem(self, name, kind="hw"):
        if name not in self.binding:
            pool = (self.pool_r if name.startswith("R_") else
                    {"hw": self.pool, "sw": self.pool_sw, "cp": self.pool_cp}[kind])
            used = set(id(s) for s in self.binding.values())
            free = [s for s in pool if id(s) not in used]
            assert free, f"slot sem pool exhausted ({kind}:{name})"
            self.binding[name] = free[0]
        return self.binding[name]

    phase = "init"

    def op(self, engine, fn, deps=(), inc=0, dma=False, sem=None):
        if dma and inc:
            inc = 16
            assert sem is not None, "DMA op with inc needs a slot semaphore name"
        if sem is not None and inc:
            kind = ("sw" if engine == "gpsimd" else "hw") if dma else "cp"
            s = self._sem(sem, kind)
            s.v += inc
            ev = (s.h, s.v)
            rec = [fn, tuple(d for d in deps if d is not None), inc, s.h,
                   self.phase]
            if getattr(self, "_side_mode", False):
                rec.append(self._side_gidx)
                self._side_gidx += 1
            else:
                self._g = getattr(self, "_g", 0) + 1
                rec.append(-self._g)
            self.ops[engine].append(tuple(rec))
            return ev
        self.cnt[engine] += inc
        ev = (self.sems[engine].h, self.cnt[engine]) if inc else None
        rec = [fn, tuple(d for d in deps if d is not None), inc, None,
               self.phase]
        if getattr(self, "_side_mode", False):
            rec.append(self._side_gidx)
            self._side_gidx += 1
        else:
            self._g = getattr(self, "_g", 0) + 1
            rec.append(-self._g)
        self.ops[engine].append(tuple(rec))
        return ev

    def mark(self, label):
        self.marks = getattr(self, "marks", {})
        self.marks[label] = {e: len(self.ops[e]) for e in ENGINES}
        self.marks[label]["_g"] = getattr(self, "_g", 0)

    def side_begin(self):
        self._main_ops = self.ops
        self.ops = {e: [] for e in ENGINES}
        self._side_mode = True
        self._side_gidx = 0

    def side_end(self):
        side = self.ops
        self.ops = self._main_ops
        self._side_mode = False
        return side

    def splice(self, side, label_a, label_b):
        self._splices = getattr(self, "_splices", [])
        self._splices.append((side, label_a, label_b))

    def apply_splices(self):
        for side, la, lb in sorted(
                getattr(self, "_splices", []),
                key=lambda t: -self.marks[t[1]]["vector"]):
            total = max(1, sum(len(v) for v in side.values()))
            a_g, b_g = self.marks[la]["_g"], self.marks[lb]["_g"]
            for e in ENGINES:
                s = side[e]
                if not s:
                    continue
                a, b = self.marks[la][e], self.marks[lb][e]
                main = self.ops[e]
                n = len(s)
                out = main[:a]
                si = 0
                for k in range(a, b):
                    g_main = -main[k][5]
                    while si < n and a_g + (s[si][5] * (b_g - a_g)) // total <= g_main:
                        out.append(s[si][:5])
                        si += 1
                    out.append(main[k])
                while si < n:
                    out.append(s[si][:5])
                    si += 1
                out.extend(main[b:])
                self.ops[e] = out
                for lbl, m in self.marks.items():
                    if m[e] >= b:
                        m[e] += n

    def r_totals(self):
        return [(s.h, s.v) for k, s in self.binding.items()
                if k.startswith("R_") and s.v > 0]

    def barrier_all(self):
        rset = set(id(s) for k, s in self.binding.items() if k.startswith("R_"))
        evs = [(self.sems[e].h, self.cnt[e]) for e in ENGINES
               if self.cnt[e] > self.sems[e].v]
        evs += [(s.h, s.v) for s in self.pool + self.pool_sw + self.pool_cp
                if s.v > 0 and id(s) not in rset]
        return evs

    def emit(self, block):
        watermark = {e: {} for e in ENGINES}

        def make(engine):
            ops = self.ops[engine]
            mysem = self.sems[engine]
            wm = watermark[engine]

            def body(eng):
                cur = [None, 0]
                for rec in ops:
                    fn, deps, inc, semh, phase = rec[:5]
                    if phase != cur[0]:
                        if cur[0] is not None:
                            self.nc.leave_named_scope(cur[0], cur[1], False)
                        cur[1] = self.nc.enter_named_scope(phase, False)[0]
                        cur[0] = phase
                    for (h, val) in deps:
                        if wm.get(h, 0) < val:
                            eng.wait_ge(h, val)
                            wm[h] = val
                    inst = fn(eng)
                    if inc:
                        inst.then_inc(semh if semh is not None else mysem.h, inc)
                if cur[0] is not None:
                    self.nc.leave_named_scope(cur[0], cur[1], False)
            return body

        for engine in ENGINES:
            if not self.ops[engine]:
                continue
            getattr(block, engine)(make(engine))
        for e in ENGINES:
            self.sems[e].v = self.cnt[e]

    def drain(self, block):
        vals = [(self.sems[e].h, self.cnt[e]) for e in ENGINES if self.cnt[e] > 0]
        vals += [(s.h, s.v) for s in
                 self.pool + self.pool_sw + self.pool_cp + self.pool_r
                 if s.v > 0]

        def body(eng):
            for h, v in vals:
                eng.wait_ge(h, v)
        block.sync(body)



CONST_COLS = {
    "eps_ln": (0, LN_EPS),
    "inv_d": (1, 1.0 / D),
    "one": (2, 1.0),
    "qk_scale": (3, 1.0 / np.sqrt(128.0)),
    "eps_gn": (4, 1e-5),
    "inv_128": (5, 1.0 / 128.0),
    "half": (6, 0.5),
}


def emit_consts(nc, block, stack):
    ct = stack.enter_context(nc.sbuf_tensor("consts", [128, 16], F32))
    cinit = Sem(nc, stack, "cinit")

    @block.gpsimd
    def _(gpsimd):
        items = list(CONST_COLS.items())
        for name, (col, val) in items[:-1]:
            gpsimd.memset(ct[:, col:col + 1], float(val))
        _, (col, val) = items[-1]
        cinit.inc(gpsimd.memset(ct[:, col:col + 1], float(val)))

    def cc(name):
        return ct[:, CONST_COLS[name][0]:CONST_COLS[name][0] + 1]

    return ct, cc, cinit


def emit_ln(nc, stack, prog, cc, cinit_ev, xin, xout, pfx="ln", n_tok=T, d=D,
            entry=()):
    n_tiles = n_tok // 128
    NB = 3
    xt = [stack.enter_context(nc.sbuf_tensor(f"{pfx}_x{i}", [128, d], F32)) for i in range(NB)]
    sq = [stack.enter_context(nc.sbuf_tensor(f"{pfx}_sq{i}", [128, d], F32)) for i in range(NB)]
    ot = [stack.enter_context(nc.sbuf_tensor(f"{pfx}_o{i}", [128, d], BF16)) for i in range(NB)]
    st = [stack.enter_context(nc.sbuf_tensor(f"{pfx}_st{i}", [128, 8], F32)) for i in range(NB)]
    hist = {}

    def H(k):
        v = hist.get(k)
        return [v] if v is not None else list(entry)

    for i in range(n_tiles):
        b = i % NB
        lx = prog.op("sync", lambda s, i=i, b=b: s.dma_start(
            xt[b][:], xin[i * 128:(i + 1) * 128, :]),
            deps=H(("xt", b)), inc=16, dma=True, sem=f"{pfx}x{b}")
        sA = prog.op("vector", lambda v, b=b: v.reduce_sum(
            st[b][:, 0:1], xt[b][:], axis=AX.X), deps=[lx] + H(("stw", b)), inc=1)
        sB = prog.op("scalar", lambda s, b=b: s.activation(
            sq[b][:], xt[b][:], AF.Square), deps=[lx] + H(("sq", b)), inc=1)
        sC = prog.op("vector", lambda v, b=b: v.reduce_sum(
            st[b][:, 1:2], sq[b][:], axis=AX.X), deps=[sB], inc=1)
        hist[("sq", b)] = sC
        sD = prog.op("scalar", lambda s, b=b: s.activation(
            st[b][:, 3:4], st[b][:, 0:1], AF.Square, scale=cc("inv_d")),
            deps=[sA, cinit_ev], inc=1)
        sE = prog.op("vector", lambda v, b=b: v.tensor_scalar_mul(
            st[b][:, 2:3], st[b][:, 0:1], 1.0 / d), deps=[sA], inc=1)
        sF = prog.op("vector", lambda v, b=b: v.scalar_tensor_tensor(
            st[b][:, 4:5], st[b][:, 1:2], 1.0 / d, st[b][:, 3:4],
            ALU.mult, ALU.subtract), deps=[sC, sD], inc=1)
        sG = prog.op("scalar", lambda s, b=b: s.activation(
            st[b][:, 5:6], st[b][:, 4:5], AF.Sqrt, bias=cc("eps_ln")),
            deps=[sF], inc=1)
        sH = prog.op("vector", lambda v, b=b: v.reciprocal(
            st[b][:, 6:7], st[b][:, 5:6]), deps=[sG], inc=1)
        sI = prog.op("vector", lambda v, b=b: v.tensor_scalar(
            ot[b][:], xt[b][:], st[b][:, 2:3], st[b][:, 6:7],
            ALU.subtract, ALU.mult), deps=[sH] + H(("ot", b)), inc=1)
        hist[("xt", b)] = sI
        hist[("stw", b)] = sI
        so = prog.op("scalar", lambda g, i=i, b=b: g.dma_start(
            xout[i * 128:(i + 1) * 128, :], ot[b][:]), deps=[sI], inc=16,
            dma=True, sem=f"{pfx}o{b}")
        hist[("ot", b)] = so
    return prog.barrier_all()


def host_strips(wT, pad_to=None):
    K, M = wT.shape
    mtiles = cdiv(M, 128)
    if M % 128:
        wT = np.concatenate([wT, np.zeros((K, mtiles * 128 - M), wT.dtype)], axis=1)
    ktiles = K // 128
    return np.ascontiguousarray(
        wT.reshape(ktiles, 128, mtiles, 128).transpose(2, 1, 0, 3).reshape(mtiles, 128, K))


def emit_actT_load(nc, stack, prog, pfx, src, d_tiles, n_tok, entry=(),
                   tok_lo=0, buf=None):
    res = buf if buf is not None else stack.enter_context(
        nc.sbuf_tensor(f"{pfx}_actT", [128, d_tiles * n_tok], BF16))
    ev = None
    for kt in range(d_tiles):
        ev = prog.op("sync", lambda s, kt=kt: s.dma_start(
            res[:, kt * n_tok:(kt + 1) * n_tok],
            src[tok_lo:tok_lo + n_tok, kt * 128:(kt + 1) * 128], transpose=True),
            deps=(list(entry) if kt == 0 else ()), inc=16, dma=True,
            sem=f"{pfx}_at")
    return (lambda kt: res[:, kt * n_tok:(kt + 1) * n_tok]), ev


def emit_dense_mm(nc, stack, prog, pfx, *, wT, M_tot, K_tot, n_tok,
                  act_tile, act_ready, out_store, NCH=512,
                  epilogue=None, w_queue="gpsimd", n_banks=4, res=None,
                  entry=()):
    ktiles = K_tot // 128
    mtiles = cdiv(M_tot, 128)
    nch = cdiv(n_tok, NCH)
    if res is None:
        wb = [stack.enter_context(nc.sbuf_tensor(f"{pfx}_wb{i}", [128, ktiles * 128], BF16))
              for i in range(2)]
        ps = [stack.enter_context(nc.psum_tensor(f"{pfx}_ps{i}", [128, NCH], F32))
              for i in range(n_banks)]
        ot = [stack.enter_context(nc.sbuf_tensor(f"{pfx}_ot{i}", [128, NCH], BF16))
              for i in range(n_banks)]
    else:
        wb, ps, ot = res["wb"], res["ps"], res["ot"]
        n_banks = len(ps)

    guard = res.get("guard") if res else None
    wload = [None] * mtiles
    group_pe = []
    group_ep = []
    store_ev = []
    g = 0
    for mt in range(mtiles):
        msz = min(128, M_tot - mt * 128)
        par = mt % 2
        pe_dep = group_pe[(mt - 2) * nch + nch - 1] if mt >= 2 else guard
        wload[mt] = prog.op(w_queue, lambda e, mt=mt, par=par: e.dma_start(
            wb[par][:], wT[mt, :, :]),
            deps=[pe_dep] + (list(entry) if mt < 2 else []), inc=16, dma=True,
            sem=f"{pfx}_w{par}")
        for ci in range(nch):
            lo = ci * NCH
            sz = min(NCH, n_tok - lo)
            bank = g % n_banks
            deps = [wload[mt], act_ready]
            if g >= n_banks:
                deps.append(group_ep[g - n_banks])
            elif guard is not None:
                deps.append(guard)
            pe_ev = None
            for kt in range(ktiles):
                pe_ev = prog.op("tensor", lambda t, kt=kt, par=par, msz=msz, bank=bank, lo=lo, sz=sz: t.matmul(
                    ps[bank][0:msz, 0:sz], wb[par][:, kt * 128:kt * 128 + msz],
                    act_tile(kt, lo, sz),
                    start=(kt == 0), stop=(kt == ktiles - 1)),
                    deps=(deps if kt == 0 else ()),
                    inc=(1 if kt == ktiles - 1 else 0))
            group_pe.append(pe_ev)
            st_dep = store_ev[g - n_banks] if g >= n_banks else (
                res.get("guard_st") if res else None)
            if epilogue is None:
                ep = prog.op("scalar", lambda s, bank=bank, msz=msz, sz=sz: s.copy(
                    ot[bank][0:msz, 0:sz], ps[bank][0:msz, 0:sz]),
                    deps=[pe_ev, st_dep], inc=1)
            else:
                ep = epilogue(ps[bank][0:msz, 0:sz], ot[bank][0:msz, 0:sz], [pe_ev, st_dep])
            group_ep.append(ep)
            sev = out_store(mt, msz, ci, lo, sz, ot[bank][0:msz, 0:sz], ep)
            store_ev.append(sev)
            g += 1
    return group_pe, group_ep, store_ev


def emit_attn_consts(nc, stack, prog, pfx="ac"):
    ident = stack.enter_context(nc.sbuf_tensor(f"{pfx}_id", [128, 128], BF16))
    onec = stack.enter_context(nc.sbuf_tensor(f"{pfx}_onec", [128, 1], BF16))
    oner = stack.enter_context(nc.sbuf_tensor(f"{pfx}_oner", [1, 128], F32))
    onerb = stack.enter_context(nc.sbuf_tensor(f"{pfx}_onerb", [1, 128], BF16))
    masks = stack.enter_context(nc.sbuf_tensor(f"{pfx}_mask", [128, 4 * 512], BF16))
    onecf = stack.enter_context(nc.sbuf_tensor(f"{pfx}_onecf", [128, 1], F32))
    identf = stack.enter_context(nc.sbuf_tensor(f"{pfx}_idf", [128, 128], F32))
    e1 = prog.op("gpsimd", lambda g: g.memset(ident[:], 1.0), inc=1)
    prog.op("gpsimd", lambda g: g.memset(onecf[:], 1.0), inc=1)
    e1f = prog.op("gpsimd", lambda g: g.memset(identf[:], 1.0), inc=1)
    prog.op("gpsimd", lambda g: g.affine_select(
        identf[:], identf[:], [[1, 128]], ALU.is_equal, 0.0, channel_multiplier=-1),
        deps=[e1f], inc=1)
    e2 = prog.op("gpsimd", lambda g: g.affine_select(
        ident[:], ident[:], [[1, 128]], ALU.is_equal, 0.0, channel_multiplier=-1),
        deps=[e1], inc=1)
    prog.op("gpsimd", lambda g: g.memset(onec[:], 1.0), inc=1)
    prog.op("gpsimd", lambda g: g.memset(oner[:], 1.0), inc=1)
    prog.op("gpsimd", lambda g: g.memset(onerb[:], 1.0), inc=1)
    e3 = prog.op("gpsimd", lambda g: g.memset(masks[:], 1.0), inc=1)
    ev = e3
    for r in range(4):
        ev = prog.op("gpsimd", lambda g, r=r: g.affine_select(
            masks[:, r * 512:(r + 1) * 512], masks[:, r * 512:(r + 1) * 512],
            [[1, 512]], ALU.is_ge, 0.0, base=-128 * r, channel_multiplier=-1),
            deps=[e3 if r == 0 else ev], inc=1)
    ready = prog.barrier_all()
    return dict(ident=ident, identf=identf, onec=onec, oner=oner, onerb=onerb,
                onecf=onecf, masks=masks, ready=ev)


def emit_attn(nc, stack, prog, cc, cinit_ev, ac, qkvT, s0T, wprojT, entry=()):
    NH = 16
    QC = 4
    qt = [stack.enter_context(nc.sbuf_tensor(f"at_q{i}", [128, S], BF16)) for i in range(2)]
    kt_ = [stack.enter_context(nc.sbuf_tensor(f"at_k{i}", [128, S], BF16)) for i in range(2)]
    vt = [stack.enter_context(nc.sbuf_tensor(f"at_v{i}", [128, S], BF16)) for i in range(2)]
    vtm = [stack.enter_context(nc.sbuf_tensor(f"at_vtm{i}", [128, S], BF16)) for i in range(2)]
    pT = stack.enter_context(nc.sbuf_tensor("at_p", [128, 16 * 512], BF16))
    rec = stack.enter_context(nc.sbuf_tensor("at_rec", [1, 512], F32))
    recb = stack.enter_context(nc.sbuf_tensor("at_recb", [128, 512], F32))
    att = stack.enter_context(nc.sbuf_tensor("at_att", [128, NH * S], BF16))

    ps_s = [stack.enter_context(nc.psum_tensor(f"at_pss{i}", [128, 512], F32)) for i in range(2)]
    ps_av = [stack.enter_context(nc.psum_tensor(f"at_psav{i}", [128, 512], F32)) for i in range(2)]
    ps_db = stack.enter_context(nc.psum_tensor("at_psdb", [128, 512], F32))
    ps_vt = stack.enter_context(nc.psum_tensor("at_psvt", [128, 1024], BF16))

    ident, onec, oner, masks = ac["ident"], ac["onec"], ac["oner"], ac["masks"]
    acr = ac["ready"]

    hist = {}
    last_exp = {}
    last_av = {}
    last_av_pe = None
    vtm_evac = {}
    last_recip = None
    last_bevac = None
    att_ev_final = None
    proj_pe = None
    proj_res = None

    for b in range(B_LOC):
        for h in range(NH):
            gh = b * NH + h
            par = gh % 2
            prev_done = hist.get(("head_done", gh - 2))
            extra_entry = list(entry) if gh < 2 else []
            lq = prog.op("sync", lambda s, h=h, b=b, par=par: s.dma_start(
                qt[par][:], qkvT[h * 128:(h + 1) * 128, b * S:(b + 1) * S]),
                deps=[prev_done] + extra_entry, inc=16, dma=True, sem=f"at_q{par}")
            lk = prog.op("sync", lambda s, h=h, b=b, par=par: s.dma_start(
                kt_[par][:], qkvT[2048 + h * 128:2048 + (h + 1) * 128, b * S:(b + 1) * S]),
                deps=[prev_done] + extra_entry, inc=16, dma=True, sem=f"at_k{par}")
            lv = prog.op("sync", lambda s, h=h, b=b, par=par: s.dma_start(
                vt[par][:], qkvT[4096 + h * 128:4096 + (h + 1) * 128, b * S:(b + 1) * S]),
                deps=[prev_done] + extra_entry, inc=16, dma=True, sem=f"at_v{par}")
            vtm_ready = []
            for half in range(2):
                pe_ev = None
                for j in range(8):
                    ktile = half * 8 + j
                    deps = ()
                    if j == 0:
                        deps = [lv, acr, vtm_evac.get(half),
                                hist.get(("vtm_used", gh - 2))]
                        if half == 1:
                            deps.append(vtm_ready[0])
                    pe_ev = prog.op("tensor", lp(nc, lambda t, par=par, j=j, ktile=ktile: t.transpose(
                        ps_vt[:, j * 128:(j + 1) * 128],
                        vt[par][:, ktile * 128:(ktile + 1) * 128], ident[:])),
                        deps=deps, inc=(1 if j == 7 else 0))
                ev = prog.op("scalar", lambda s, par=par, half=half: s.copy(
                    vtm[par][:, half * 1024:(half + 1) * 1024], ps_vt[:]),
                    deps=[pe_ev, hist.get(("vtm_used", gh - 2))], inc=1)
                vtm_evac[half] = ev
                vtm_ready.append(ev)

            for qc in range(QC):
                ktmax = 4 * (qc + 1)
                exp_evs = []
                for ktk in range(ktmax):
                    bank = ktk % 2
                    deps = [lq, lk, last_exp.get(bank)]
                    pe_ev = prog.op("tensor", lambda t, par=par, ktk=ktk, qc=qc, bank=bank: t.matmul(
                        ps_s[bank][:], kt_[par][:, ktk * 128:(ktk + 1) * 128],
                        qt[par][:, qc * 512:(qc + 1) * 512], start=True, stop=True),
                        deps=deps, inc=1)
                    eev = prog.op("scalar", lambda s, ktk=ktk, bank=bank: s.activation(
                        pT[:, ktk * 512:(ktk + 1) * 512], ps_s[bank][:],
                        AF.Exp, scale=cc("qk_scale")),
                        deps=[pe_ev, cinit_ev, last_av_pe], inc=1)
                    last_exp[bank] = eev
                    if ktk >= 4 * qc:
                        r = ktk - 4 * qc
                        eev = prog.op("vector", lambda v, ktk=ktk, r=r: v.tensor_mul(
                            pT[:, ktk * 512:(ktk + 1) * 512],
                            pT[:, ktk * 512:(ktk + 1) * 512],
                            masks[:, r * 512:(r + 1) * 512]),
                            deps=[eev, acr], inc=1)
                    exp_evs.append(eev)
                d_pe = None
                for ktk in range(ktmax):
                    deps = [exp_evs[ktk]]
                    if ktk == 0:
                        deps += [last_recip, last_bevac, acr]
                    d_pe = prog.op("tensor", lambda t, ktk=ktk, ktmax=ktmax: t.matmul(
                        ps_db[0:1, :], onec[:], pT[:, ktk * 512:(ktk + 1) * 512],
                        start=(ktk == 0), stop=(ktk == ktmax - 1)),
                        deps=deps, inc=(1 if ktk == ktmax - 1 else 0))
                last_recip = prog.op("vector", lambda v: v.reciprocal(rec[:], ps_db[0:1, :]),
                                     deps=[d_pe], inc=1)
                b_pe = prog.op("tensor", lambda t: t.matmul(
                    ps_db[:], oner[:], rec[:], start=True, stop=True),
                    deps=[last_recip, last_bevac], inc=1)
                last_bevac = prog.op("scalar", lambda s: s.copy(recb[:], ps_db[:]),
                                     deps=[b_pe], inc=1)
                avbank = qc % 2
                av_pe = None
                for ktk in range(ktmax):
                    deps = [vtm_ready[0], vtm_ready[1], exp_evs[ktk]]
                    if ktk == 0:
                        deps.append(last_av.get(avbank))
                    av_pe = prog.op("tensor", lambda t, par=par, ktk=ktk, avbank=avbank: t.matmul(
                        ps_av[avbank][:], vtm[par][:, ktk * 128:(ktk + 1) * 128],
                        pT[:, ktk * 512:(ktk + 1) * 512],
                        start=(ktk == 0), stop=(ktk == ktmax - 1)),
                        deps=deps, inc=(1 if ktk == ktmax - 1 else 0))
                last_av_pe = av_pe
                deps = [av_pe, last_bevac]
                if proj_pe is not None:
                    deps.append(proj_pe[15 * 4 + qc])
                am = prog.op("vector", lambda v, h=h, qc=qc, avbank=avbank: v.tensor_tensor(
                    att[:, h * S + qc * 512: h * S + (qc + 1) * 512],
                    ps_av[avbank][:], recb[:], ALU.mult),
                    deps=deps, inc=1)
                last_av[avbank] = am
                att_ev_final = am
            hist[("head_done", gh)] = att_ev_final
            hist[("vtm_used", gh)] = last_av_pe

        if proj_res is None:
            proj_res = {
                "wb": [stack.enter_context(nc.sbuf_tensor(f"pj_wb{i}", [128, 16 * 128], BF16))
                       for i in range(2)],
                "ps": [stack.enter_context(nc.psum_tensor(f"pj_ps{i}", [128, 512], F32))
                       for i in range(2)],
                "ot": [stack.enter_context(nc.sbuf_tensor(f"pj_ot{i}", [128, 512], BF16))
                       for i in range(2)],
            }
        proj_res["guard"] = att_ev_final

        def out_store(mt, msz, ci, lo, sz, src, dep, b=b):
            g = mt * 4 + ci
            return prog.op("sync", lambda s: s.dma_start(
                s0T[mt * 128:mt * 128 + msz, b * S + lo:b * S + lo + sz], src),
                deps=[dep], inc=16, dma=True, sem=f"pj_st{g % 2}")

        proj_pe, proj_ep, proj_st = emit_dense_mm(
            nc, stack, prog, f"proj{b}", wT=wprojT, M_tot=D, K_tot=D, n_tok=S,
            act_tile=lambda kt, lo, sz: att[:, kt * S + lo:lo + kt * S + sz],
            act_ready=att_ev_final, out_store=out_store, res=proj_res,
            entry=entry)
        proj_res["guard_st"] = proj_st[-1]



def emit_gamma(nc, stack, prog, ac, wTs, entry=()):
    nw = len(wTs)
    maxC = max(c for _, _, c in wTs)
    parts = stack.enter_context(nc.sbuf_tensor("gm_parts", [128, 64], F32))
    tot = stack.enter_context(nc.sbuf_tensor("gm_tot", [128, nw], F32))
    thr = stack.enter_context(nc.sbuf_tensor("gm_thr", [1, nw], F32))
    thrb = stack.enter_context(nc.sbuf_tensor("gm_thrb", [128, 2 * nw], F32))

    onecf, onerf = ac["onecf"], ac["oner"]
    acr = ac["ready"]
    ready = None
    with ExitStack() as local:
        buf = [local.enter_context(nc.sbuf_tensor(f"gm_b{i}", [128, maxC], F32))
               for i in range(3)]
        ps_g = local.enter_context(nc.psum_tensor("gm_ps", [128, 16], F32))
        bufap = [b[:] for b in buf]
        last_red = {}
        for w, (ap, nrb, C) in enumerate(wTs):
            rev = None
            for rb in range(nrb):
                par = rb % 3
                dep = [last_red.get(par)] if par in last_red else list(entry)
                src = ap[rb, :, 0:C] if ap.ndim == 3 else ap[rb * 128:(rb + 1) * 128, 0:C]
                dst = bufap[par][:, 0:C] if C != maxC else bufap[par]
                lev = prog.op("scalar", lambda s, src=src, dst=dst: s.dma_start(dst, src),
                              deps=dep, inc=16, dma=True, sem=f"gm_l{par}")
                pdst = parts[:, rb:rb + 1]
                rev = prog.op("vector", lambda v, dst=dst, pdst=pdst: v.reduce_sum(
                    pdst, dst, axis=AX.X, apply_absolute_value=True),
                    deps=[lev], inc=1)
                last_red[par] = rev
            sev = prog.op("vector", lambda v, nrb=nrb, w=w: v.reduce_sum(
                tot[:, w:w + 1], parts[:, 0:nrb], axis=AX.X),
                deps=[rev], inc=1)
            pg1 = ps_g[0:1, w:w + 1]
            pg2 = ps_g[0:128, 8 + w:9 + w]
            mm1 = prog.op("tensor", lambda t, pg1=pg1, w=w: t.matmul(
                pg1, onecf[:], tot[:, w:w + 1], start=True, stop=True),
                deps=[sev, acr], inc=1)
            N = nrb * 128 * C
            tev = prog.op("vector", lambda v, pg1=pg1, w=w, N=N: v.tensor_scalar(
                thr[:, w:w + 1], pg1, 0.5 / N, 0.5e-8, ALU.mult, ALU.add),
                deps=[mm1], inc=1)
            mm2 = prog.op("tensor", lambda t, pg2=pg2, w=w: t.matmul(
                pg2, onerf[:], thr[:, w:w + 1], start=True, stop=True),
                deps=[tev], inc=1)
            cev = prog.op("vector", lambda v, pg2=pg2, w=w: v.tensor_copy(
                thrb[:, 2 * w:2 * w + 1], pg2),
                deps=[mm2], inc=1)
            ready = prog.op("vector", lambda v, w=w: v.tensor_scalar_mul(
                thrb[:, 2 * w + 1:2 * w + 2], thrb[:, 2 * w:2 * w + 1], -1.0),
                deps=[cev], inc=1)
    return thrb, ready


def emit_quant(nc, stack, prog, wf32_ap, sgn_ap, msk_ap, wq_ap, thrb, wi, deps):
    e1 = prog.op("vector", lambda v: v.tensor_scalar(
        sgn_ap, wf32_ap, thrb[:, 2 * wi:2 * wi + 1], None, ALU.is_gt),
        deps=deps, inc=1)
    e2 = prog.op("vector", lambda v: v.tensor_scalar(
        msk_ap, wf32_ap, thrb[:, 2 * wi + 1:2 * wi + 2], None, ALU.is_lt),
        deps=[e1] + list(deps), inc=1)
    e3 = prog.op("vector", lambda v: v.tensor_sub(wq_ap, sgn_ap, msk_ap),
                 deps=[e1, e2], inc=1)
    return e3


def emit_bitlinear_hid(nc, stack, prog, ac, cc, cinit_ev, thrb, thr_ev,
                       w1T, w2T, xn0, hidT, entry=(), tok_lo=0, n_tok=T,
                       pfx="bl", bufs=None):
    H = 5632
    ktiles = 16
    mtiles = H // 128
    nch = n_tok // 512
    if bufs is None:
        bufs = {}
        bufs["actT"] = stack.enter_context(
            nc.sbuf_tensor(f"{pfx}_actT", [128, (D // 128) * n_tok], BF16))
        bufs["wf"] = [stack.enter_context(nc.sbuf_tensor(f"{pfx}_wf{i}", [128, 2048], F32)) for i in range(2)]
        bufs["sgn"] = [stack.enter_context(nc.sbuf_tensor(f"{pfx}_sg{i}", [128, 2048], BF16)) for i in range(2)]
        bufs["msk"] = [stack.enter_context(nc.sbuf_tensor(f"{pfx}_mk{i}", [128, 2048], BF16)) for i in range(2)]
        bufs["wq1"] = [stack.enter_context(nc.sbuf_tensor(f"{pfx}_q1{i}", [128, 2048], BF16)) for i in range(2)]
        bufs["wq2"] = [stack.enter_context(nc.sbuf_tensor(f"{pfx}_q2{i}", [128, 2048], BF16)) for i in range(2)]
        bufs["sig"] = [stack.enter_context(nc.sbuf_tensor(f"{pfx}_si{i}", [128, 512], BF16)) for i in range(4)]
        bufs["ot"] = [stack.enter_context(nc.sbuf_tensor(f"{pfx}_ot{i}", [128, 512], BF16)) for i in range(4)]
        bufs["psA"] = [stack.enter_context(nc.psum_tensor(f"{pfx}_psA{i}", [128, 512], F32)) for i in range(2)]
        bufs["psB"] = [stack.enter_context(nc.psum_tensor(f"{pfx}_psB{i}", [128, 512], F32)) for i in range(2)]
    act, act_ev = emit_actT_load(nc, stack, prog, f"{pfx}{tok_lo // S}", xn0,
                                 D // 128, n_tok, entry=entry, tok_lo=tok_lo,
                                 buf=bufs["actT"])
    wf, sgn, msk = bufs["wf"], bufs["sgn"], bufs["msk"]
    wq1, wq2, sig, ot = bufs["wq1"], bufs["wq2"], bufs["sig"], bufs["ot"]
    psA, psB = bufs["psA"], bufs["psB"]

    qev = {}
    mm_last = {}
    epA_hist = [None, None]
    epB_hist = [None, None]
    sig_hist = [None] * 4
    st_hist = [None] * 4
    for mt in range(mtiles):
        par = mt % 2
        for which, wT, wq in ((0, w1T, wq1), (1, w2T, wq2)):
            wpar = (2 * mt + which) % 2
            ldep = [qev[(0, wpar)] if (0, wpar) in qev else None,
                    qev[(1, wpar)] if (1, wpar) in qev else None]
            ldep = [d for d in ldep if d] or list(entry)
            lev = prog.op("sync", lambda s, wT=wT, mt=mt, wpar=wpar: s.dma_start(
                wf[wpar][:], wT[mt, :, :]),
                deps=ldep + [mm_last.get((which, par))], inc=16, dma=True,
                sem=f"{pfx}_w{wpar}")
            qev[(which, wpar)] = emit_quant(
                nc, stack, prog, wf[wpar][:], sgn[wpar][:], msk[wpar][:],
                wq[par][:], thrb, which,
                [lev, thr_ev, mm_last.get((which, par))])
        for ci in range(nch):
            lo = ci * 512
            bank = ci % 2
            slot = ci % 4
            depsA = [qev[(0, (2 * mt) % 2)], act_ev]
            depsB = [qev[(1, (2 * mt + 1) % 2)], act_ev]
            if epA_hist[bank] is not None:
                depsA.append(epA_hist[bank])
            if epB_hist[bank] is not None:
                depsB.append(epB_hist[bank])
            peA = None
            peB = None
            for kt in range(ktiles):
                peA = prog.op("tensor", lambda t, kt=kt, par=par, bank=bank, lo=lo: t.matmul(
                    psA[bank][:], wq1[par][:, kt * 128:(kt + 1) * 128],
                    act(kt)[:, lo:lo + 512], start=(kt == 0), stop=(kt == ktiles - 1)),
                    deps=(depsA if kt == 0 else ()), inc=(1 if kt == ktiles - 1 else 0))
            for kt in range(ktiles):
                peB = prog.op("tensor", lambda t, kt=kt, par=par, bank=bank, lo=lo: t.matmul(
                    psB[bank][:], wq2[par][:, kt * 128:(kt + 1) * 128],
                    act(kt)[:, lo:lo + 512], start=(kt == 0), stop=(kt == ktiles - 1)),
                    deps=(depsB if kt == 0 else ()), inc=(1 if kt == ktiles - 1 else 0))
            mm_last[(0, par)] = peB
            mm_last[(1, par)] = peB
            sev = prog.op("scalar", lambda s, bank=bank, slot=slot: s.activation(
                sig[slot][:], psA[bank][:], AF.Sigmoid),
                deps=[peA, sig_hist[slot]], inc=1)
            epA_hist[bank] = sev
            hev = prog.op("vector", lambda v, bank=bank, slot=slot: v.tensor_tensor(
                ot[slot][:], psB[bank][:], sig[slot][:], ALU.mult),
                deps=[peB, sev, st_hist[slot]], inc=1)
            sig_hist[slot] = hev
            epB_hist[bank] = hev
            stv = prog.op("scalar", lambda g, mt=mt, lo=lo, slot=slot: g.dma_start(
                hidT[mt * 128:(mt + 1) * 128, tok_lo + lo:tok_lo + lo + 512],
                ot[slot][:]),
                deps=[hev], inc=16, dma=True, sem=f"{pfx}_st{slot}")
            st_hist[slot] = stv
    return prog.barrier_all(), bufs


def emit_w3_quant(nc, stack, prog, thrb, thr_ev, w3T, w3q, entry=()):
    H = 5632
    ktiles = H // 128
    wf = [stack.enter_context(nc.sbuf_tensor(f"w3_wf{i}", [128, ktiles * 128], F32)) for i in range(2)]
    sgn = [stack.enter_context(nc.sbuf_tensor(f"w3_sg{i}", [128, ktiles * 128], BF16)) for i in range(2)]
    msk = [stack.enter_context(nc.sbuf_tensor(f"w3_mk{i}", [128, ktiles * 128], BF16)) for i in range(2)]
    wq = [stack.enter_context(nc.sbuf_tensor(f"w3_q{i}", [128, ktiles * 128], BF16)) for i in range(2)]
    qhist = {}
    shist = {}
    for mt in range(16):
        par = mt % 2
        lev = prog.op("sync", lambda s, mt=mt, par=par: s.dma_start(
            wf[par][:], w3T[mt, :, :]),
            deps=[qhist.get(par)] if par in qhist else list(entry), inc=16,
            dma=True, sem=f"w3_l{par}")
        qv = emit_quant(nc, stack, prog, wf[par][:], sgn[par][:], msk[par][:],
                        wq[par][:], thrb, 2, [lev, thr_ev, shist.get(par)])
        qhist[par] = qv
        sv = prog.op("scalar", lambda g, par=par, mt=mt: g.dma_start(
            w3q[mt, :, :], wq[par][:]),
            deps=[qv], inc=16, dma=True, sem=f"w3_s{par}")
        shist[par] = sv
    return prog.barrier_all()


def emit_w3_mm(nc, stack, prog, cc, cinit_ev, w3q, hidT, s0T, sgT, entry=()):
    H = 5632
    ktiles = H // 128
    hb = stack.enter_context(nc.sbuf_tensor("w3m_hb", [128, ktiles * 1024], BF16))
    wqb = [stack.enter_context(nc.sbuf_tensor(f"w3m_wq{i}", [128, ktiles * 128], BF16)) for i in range(2)]
    s0b = [stack.enter_context(nc.sbuf_tensor(f"w3m_s0{i}", [128, 512], BF16)) for i in range(4)]
    gb = [stack.enter_context(nc.sbuf_tensor(f"w3m_g{i}", [128, 512], BF16)) for i in range(4)]
    ot = [stack.enter_context(nc.sbuf_tensor(f"w3m_ot{i}", [128, 512], BF16)) for i in range(4)]
    ps = [stack.enter_context(nc.psum_tensor(f"w3m_ps{i}", [128, 512], F32)) for i in range(4)]

    hload = [None] * ktiles
    wload = {}
    g = 0
    ep_hist = {}
    st_hist = {}
    sg_hist = {}
    s0_hist = {}
    mm_hist = {}
    hfree = None
    for c4 in range(4):
        tlo = c4 * 1024
        for kt in range(ktiles):
            hload[kt] = prog.op("scalar", lambda s, kt=kt, tlo=tlo: s.dma_start(
                hb[:, kt * 1024:(kt + 1) * 1024],
                hidT[kt * 128:(kt + 1) * 128, tlo:tlo + 1024]),
                deps=[hfree] if hfree else list(entry), inc=16, dma=True,
                sem="w3m_h")
        hready = hload[-1]
        for mt in range(16):
            par = mt % 2
            wload[mt] = prog.op("sync", lambda s, mt=mt, par=par: s.dma_start(
                wqb[par][:], w3q[mt, :, :]),
                deps=[mm_hist.get(par)] if par in mm_hist else list(entry),
                inc=16, dma=True, sem=f"w3m_w{par}")
            for half in range(2):
                lo = tlo + half * 512
                bank = g % 4
                slot = g % 4
                deps = [wload[mt], hready]
                if bank in ep_hist:
                    deps.append(ep_hist[bank])
                pe = None
                for kt in range(ktiles):
                    pe = prog.op("tensor", lambda t, kt=kt, par=par, bank=bank, lo=lo: t.matmul(
                        ps[bank][:], wqb[par][:, kt * 128:(kt + 1) * 128],
                        hb[:, kt * 1024 + (lo % 1024):kt * 1024 + (lo % 1024) + 512],
                        start=(kt == 0), stop=(kt == ktiles - 1)),
                        deps=(deps if kt == 0 else ()), inc=(1 if kt == ktiles - 1 else 0))
                mm_hist[par] = pe
                sl = prog.op("sync", lambda gg, mt=mt, lo=lo, slot=slot: gg.dma_start(
                    s0b[slot][:], s0T[mt * 128:(mt + 1) * 128, lo:lo + 512]),
                    deps=[s0_hist.get(slot)] if slot in s0_hist else list(entry),
                    inc=16, dma=True, sem=f"w3m_s0{slot}")
                gv = prog.op("scalar", lambda s, bank=bank, slot=slot: s.activation(
                    gb[slot][:], ps[bank][:], AF.Silu),
                    deps=[pe, sg_hist.get(slot)], inc=1)
                ep_hist[bank] = gv
                hv = prog.op("vector", lambda v, slot=slot: v.tensor_mul(
                    ot[slot][:], gb[slot][:], s0b[slot][:]),
                    deps=[gv, sl, st_hist.get(slot)], inc=1)
                sg_hist[slot] = hv
                s0_hist[slot] = hv
                stv = prog.op("scalar", lambda gg, mt=mt, lo=lo, slot=slot: gg.dma_start(
                    sgT[mt * 128:(mt + 1) * 128, lo:lo + 512], ot[slot][:]),
                    deps=[hv], inc=16, dma=True, sem=f"w3m_st{slot}")
                st_hist[slot] = stv
                g += 1
        hfree = pe
    return prog.barrier_all()


def emit_residual(nc, stack, prog, x_in, sgT, x1_out, entry=(), n_tok=T, d=D,
                  pfx="rs"):
    n_tiles = n_tok // 128
    xb = [stack.enter_context(nc.sbuf_tensor(f"{pfx}_x{i}", [128, d], F32)) for i in range(3)]
    sb = [stack.enter_context(nc.sbuf_tensor(f"{pfx}_s{i}", [128, d], BF16)) for i in range(3)]
    ob = [stack.enter_context(nc.sbuf_tensor(f"{pfx}_o{i}", [128, d], F32)) for i in range(3)]
    add_h = {}
    st_h = {}
    for i in range(n_tiles):
        b = i % 3
        dep0 = [add_h.get(b)] if b in add_h else list(entry)
        lx = prog.op("sync", lambda s, i=i, b=b: s.dma_start(
            xb[b][:], x_in[i * 128:(i + 1) * 128, :]), deps=dep0, inc=16, dma=True,
            sem=f"{pfx}x{b}")
        ls = prog.op("sync", lambda s, i=i, b=b: s.dma_start(
            sb[b][:], sgT[0:d, i * 128:(i + 1) * 128], transpose=True),
            deps=dep0, inc=16, dma=True, sem=f"rs_s{b}")
        av = prog.op("vector", lambda v, b=b: v.tensor_add(ob[b][:], xb[b][:], sb[b][:]),
                     deps=[lx, ls, st_h.get(b)], inc=1)
        add_h[b] = av
        sv = prog.op("scalar", lambda g, i=i, b=b: g.dma_start(
            x1_out[i * 128:(i + 1) * 128, :], ob[b][:]), deps=[av], inc=16,
            dma=True, sem=f"{pfx}o{b}")
        st_h[b] = sv
    return prog.barrier_all()



MB = dict(D_IN=4096, D_XB=1024, D_ST=128, HD=64, NH=64, NG=32, XG=8,
          CH=128, NCK=16, CONV=4, CDIM=6144, DPROJ=10304)


def emit_inproj(nc, stack, prog, zT, xbcT, dtT, xn1, winT, entry=(),
                tok_lo=0, n_tok=T, pfx="ip", bufs=None):
    nch = n_tok // 512
    ktiles = D // 128
    if bufs is None:
        bufs = {}
        bufs["actT"] = stack.enter_context(
            nc.sbuf_tensor(f"{pfx}_actT", [128, (D // 128) * n_tok], BF16))
        bufs["dts"] = [stack.enter_context(nc.sbuf_tensor(f"{pfx}_dt{i}", [64, 512], F32)) for i in range(2)]
        bufs["wb"] = [stack.enter_context(nc.sbuf_tensor(f"{pfx}_wb{i}", [128, 16 * 128], BF16))
                      for i in range(2)]
        bufs["ps"] = [stack.enter_context(nc.psum_tensor(f"{pfx}_ps{i}", [128, 512], F32)) for i in range(4)]
        bufs["ot"] = [stack.enter_context(nc.sbuf_tensor(f"{pfx}_ot{i}", [128, 512], BF16)) for i in range(4)]
    act, act_ev = emit_actT_load(nc, stack, prog, f"{pfx}{tok_lo // S}", xn1,
                                 D // 128, n_tok, entry=entry, tok_lo=tok_lo,
                                 buf=bufs["actT"])
    dts = bufs["dts"]
    dt_state = {}

    def out_store(mt, msz, ci, lo, sz, src, dep):
        lo = lo + tok_lo
        if mt < 32:
            dst = zT[mt * 128:mt * 128 + msz, lo:lo + sz]
        elif mt < 80:
            dst = xbcT[(mt - 32) * 128:(mt - 32) * 128 + msz, lo:lo + sz]
        else:
            dst = dtT[0:64, lo:lo + sz]
        return prog.op("scalar", lambda g: g.dma_start(dst, src), deps=[dep],
                       inc=16, dma=True, sem=f"{pfx}_st{out_store.slot}")

    def epilogue(ps_ap, ot_ap, deps, _mt=[0]):
        mt, ci = epilogue.pos
        if mt == 80:
            slot = ci % 2
            ev = prog.op("scalar", lambda s, slot=slot: s.copy(dts[slot][:], ps_ap),
                         deps=list(deps) + [dt_state.get(slot)], inc=1)
            dt_state[slot] = ev
            epilogue.last_src = dts[slot][:]
            return ev
        ev = prog.op("scalar", lambda s: s.copy(ot_ap, ps_ap), deps=deps, inc=1)
        epilogue.last_src = None
        return ev

    mtiles = cdiv(MB["DPROJ"], 128)
    wb, ps, ot = bufs["wb"], bufs["ps"], bufs["ot"]
    group_pe, group_ep, store_ev = [], [], []
    g = 0
    for mt in range(mtiles):
        msz = min(128, MB["DPROJ"] - mt * 128)
        par = mt % 2
        pe_dep = group_pe[(mt - 2) * nch + nch - 1] if mt >= 2 else None
        wl = prog.op("gpsimd", lambda e, mt=mt, par=par: e.dma_start(
            wb[par][:], winT[mt, :, :]), deps=[pe_dep] if pe_dep else list(entry),
            inc=16, dma=True, sem=f"{pfx}_w{par}")
        for ci in range(nch):
            lo = ci * 512
            bank = g % 4
            deps = [wl, act_ev]
            if g >= 4:
                deps.append(group_ep[g - 4])
            pe = None
            for kt in range(ktiles):
                pe = prog.op("tensor", lambda t, kt=kt, par=par, msz=msz, bank=bank, lo=lo: t.matmul(
                    ps[bank][0:msz, :], wb[par][:, kt * 128:kt * 128 + msz],
                    act(kt)[:, lo:lo + 512],
                    start=(kt == 0), stop=(kt == ktiles - 1)),
                    deps=(deps if kt == 0 else ()), inc=(1 if kt == ktiles - 1 else 0))
            group_pe.append(pe)
            st_dep = store_ev[g - 4] if g >= 4 else None
            epilogue.pos = (mt, ci)
            ep = epilogue(ps[bank][0:msz, :], ot[bank][0:msz, :], [pe, st_dep])
            group_ep.append(ep)
            src = epilogue.last_src if epilogue.last_src is not None else ot[bank][0:msz, :]
            out_store.slot = g % 4
            sev = out_store(mt, msz, ci, lo, 512, src, ep)
            store_ev.append(sev)
            g += 1
    return prog.barrier_all(), bufs


def emit_conv(nc, stack, prog, ac, xbcT, convT, convw, convb, entry=()):
    cw = stack.enter_context(nc.sbuf_tensor("cv_w", [128, 48 * 4], F32))
    cb = stack.enter_context(nc.sbuf_tensor("cv_b", [128, 48], F32))
    xb = [stack.enter_context(nc.sbuf_tensor(f"cv_x{i}", [128, S + 3], BF16)) for i in range(3)]
    acc = [stack.enter_context(nc.sbuf_tensor(f"cv_a{i}", [128, S], F32)) for i in range(3)]
    ob = [stack.enter_context(nc.sbuf_tensor(f"cv_o{i}", [128, S], BF16)) for i in range(3)]
    lw = prog.op("sync", lambda s: s.dma_start(cw[:], convw[:, :]),
                 deps=list(entry), inc=16, dma=True, sem="cv_w")
    lb = prog.op("sync", lambda s: s.dma_start(cb[:], convb[:, :]),
                 deps=list(entry), inc=16, dma=True, sem="cv_b")
    hist = {}
    for it in range(96):
        ct_, b = divmod(it, 2)
        sl = it % 3
        dep0 = [hist.get(("st", sl))] if ("st", sl) in hist else list(entry)
        z0 = prog.op("gpsimd", lambda g, sl=sl: g.memset(xb[sl][:, 0:3], 0.0),
                     deps=dep0 + [hist.get(("a4", sl))], inc=1)
        lx = prog.op("sync", lambda s, ct_=ct_, b=b, sl=sl: s.dma_start(
            xb[sl][:, 3:S + 3],
            xbcT[ct_ * 128:(ct_ + 1) * 128, b * S:(b + 1) * S]),
            deps=dep0 + [hist.get(("a4", sl))], inc=16, dma=True, sem=f"cv_x{sl}")
        a1 = prog.op("vector", lambda v, ct_=ct_, sl=sl: v.tensor_scalar(
            acc[sl][:], xb[sl][:, 0:S], cw[:, ct_ * 4:ct_ * 4 + 1],
            cb[:, ct_:ct_ + 1], ALU.mult, ALU.add),
            deps=[lx, lw, lb, z0, hist.get(("sil", sl))], inc=1)
        a2 = prog.op("vector", lambda v, ct_=ct_, sl=sl: v.scalar_tensor_tensor(
            acc[sl][:], xb[sl][:, 1:S + 1], cw[:, ct_ * 4 + 1:ct_ * 4 + 2],
            acc[sl][:], ALU.mult, ALU.add), deps=[a1], inc=1)
        a3 = prog.op("vector", lambda v, ct_=ct_, sl=sl: v.scalar_tensor_tensor(
            acc[sl][:], xb[sl][:, 2:S + 2], cw[:, ct_ * 4 + 2:ct_ * 4 + 3],
            acc[sl][:], ALU.mult, ALU.add), deps=[a2], inc=1)
        a4 = prog.op("vector", lambda v, ct_=ct_, sl=sl: v.scalar_tensor_tensor(
            acc[sl][:], xb[sl][:, 3:S + 3], cw[:, ct_ * 4 + 3:ct_ * 4 + 4],
            acc[sl][:], ALU.mult, ALU.add), deps=[a3], inc=1)
        hist[("a4", sl)] = a4
        sil = prog.op("scalar", lambda s, sl=sl: s.activation(
            ob[sl][:], acc[sl][:], AF.Silu), deps=[a4, hist.get(("st", sl))], inc=1)
        hist[("sil", sl)] = sil
        st = prog.op("scalar", lambda g, ct_=ct_, b=b, sl=sl: g.dma_start(
            convT[ct_ * 128:(ct_ + 1) * 128, b * S:(b + 1) * S], ob[sl][:]),
            deps=[sil], inc=16, dma=True, sem=f"cv_o{sl}")
        hist[("st", sl)] = st
    return prog.barrier_all()


def emit_scan(nc, stack, prog, ac, cc, cinit_ev, convT, dtT, yTd,
              dt_bias, A_log, Dvec, entry=(), nck=None, nb=B_LOC, stage=9,
              tscr=None):
    NH, HD, NG, XG, CH = MB["NH"], MB["HD"], MB["NG"], MB["XG"], MB["CH"]
    ident = ac["ident"]
    identf = ac["identf"]

    oh = stack.enter_context(nc.sbuf_tensor("sc_oh", [64, 64 * 128], BF16))
    e1 = prog.op("gpsimd", lambda g: g.memset(oh[:], 1.0), deps=list(entry), inc=1)
    ohev = prog.op("gpsimd", lambda g: g.affine_select(
        oh[:], oh[:], [[1, 64], [0, 128]], ALU.is_equal, 0.0, channel_multiplier=-1),
        deps=[e1], inc=1)
    mbias = stack.enter_context(nc.sbuf_tensor("sc_mb", [128, 128], F32))
    e2 = prog.op("gpsimd", lambda g: g.memset(mbias[:], 0.0), deps=list(entry), inc=1)
    mbev = prog.op("gpsimd", lambda g: g.affine_select(
        mbias[:], mbias[:], [[1, 128]], ALU.is_ge, -1e30, channel_multiplier=-1),
        deps=[e2], inc=1)
    dtb = stack.enter_context(nc.sbuf_tensor("sc_dtb", [64, 1], F32))
    alog = stack.enter_context(nc.sbuf_tensor("sc_al", [64, 1], F32))
    negA = stack.enter_context(nc.sbuf_tensor("sc_na", [64, 1], F32))
    drow = stack.enter_context(nc.sbuf_tensor("sc_dr", [1, 64], F32))
    Db = stack.enter_context(nc.sbuf_tensor("sc_db", [128, 64], F32))
    l1 = prog.op("sync", lambda s: s.dma_start(dtb[:], dt_bias[:, :]),
                 deps=list(entry), inc=16, dma=True, sem="sc_c1")
    l2 = prog.op("sync", lambda s: s.dma_start(alog[:], A_log[:, :]),
                 deps=list(entry), inc=16, dma=True, sem="sc_c2")
    l3 = prog.op("sync", lambda s: s.dma_start(drow[:], Dvec[:, :]),
                 deps=list(entry), inc=16, dma=True, sem="sc_c3")
    e3 = prog.op("scalar", lambda s: s.activation(negA[:], alog[:], AF.Exp),
                 deps=[l2], inc=1)
    negA_ev = prog.op("vector", lambda v: v.tensor_scalar_mul(negA[:], negA[:], -1.0),
                      deps=[e3], inc=1)

    xs = stack.enter_context(nc.sbuf_tensor("sc_xs", [128, XG * CH], BF16))
    bs = stack.enter_context(nc.sbuf_tensor("sc_bs", [128, XG * CH], BF16))
    cs_t = stack.enter_context(nc.sbuf_tensor("sc_cs", [128, NG * CH], BF16))
    dts = stack.enter_context(nc.sbuf_tensor("sc_dts", [64, CH], F32))
    dtsp = stack.enter_context(nc.sbuf_tensor("sc_dtsp", [64, CH], F32))
    dA = stack.enter_context(nc.sbuf_tensor("sc_dA", [64, CH], F32))
    csum_ = stack.enter_context(nc.sbuf_tensor("sc_csum", [64, CH], F32))
    ncs = stack.enter_context(nc.sbuf_tensor("sc_ncs", [64, CH], F32))
    csbf = stack.enter_context(nc.sbuf_tensor("sc_csbf", [64, CH], BF16))
    tsum = stack.enter_context(nc.sbuf_tensor("sc_ts", [64, 1], F32))
    ncsT = stack.enter_context(nc.sbuf_tensor("sc_ncsT", [128, 64], F32))
    dtsT = stack.enter_context(nc.sbuf_tensor("sc_dtsT", [128, 64], F32))
    tsrow = stack.enter_context(nc.sbuf_tensor("sc_tsr", [1, 64], F32))
    dec = stack.enter_context(nc.sbuf_tensor("sc_dec", [128, 64], F32))
    cdb = stack.enter_context(nc.sbuf_tensor("sc_cdb", [128, 64], F32))
    ddec = stack.enter_context(nc.sbuf_tensor("sc_ddec", [128, 64], F32))
    x_tm = stack.enter_context(nc.sbuf_tensor("sc_xtm", [128, XG * CH], BF16))
    b_tm = stack.enter_context(nc.sbuf_tensor("sc_btm", [128, XG * CH], BF16))
    xdt = stack.enter_context(nc.sbuf_tensor("sc_xdt", [128, NH * HD], BF16))
    xdtd = stack.enter_context(nc.sbuf_tensor("sc_xdtd", [128, NH * HD], BF16))
    lt = [stack.enter_context(nc.sbuf_tensor(f"sc_lt{i}", [128, CH], F32)) for i in range(2)]
    lt_blk = stack.enter_context(nc.sbuf_tensor("sc_ltb", [128, 32 * CH], BF16))
    glt_blk = stack.enter_context(nc.sbuf_tensor("sc_gltb", [128, 32 * CH], BF16))
    cpr_blk = stack.enter_context(nc.sbuf_tensor("sc_cprb", [128, 32 * CH], BF16))
    ecs_blk = stack.enter_context(nc.sbuf_tensor("sc_ecsb", [128, 32 * CH], BF16))
    yall = stack.enter_context(nc.sbuf_tensor("sc_y", [128, NG * CH], BF16))
    carry_bf = stack.enter_context(nc.sbuf_tensor("sc_cbf", [128, NH * HD], BF16))

    ps_csb = stack.enter_context(nc.psum_tensor("sc_pcsb", [128, 512], F32))[:, 0:CH]
    ps_g = [stack.enter_context(nc.psum_tensor(f"sc_pg{i}", [128, 512], F32))[:, 0:CH]
            for i in range(2)]
    ps_y = [stack.enter_context(nc.psum_tensor(f"sc_py{i}", [128, 512], F32))[0:64, 0:CH]
            for i in range(2)]
    ps_st = stack.enter_context(nc.psum_tensor("sc_pst", [128, 512], F32))[:, 0:HD]
    ps_pre = stack.enter_context(nc.psum_tensor("sc_pre", [128, 512], F32))[:, 0:256]
    ps_bt = stack.enter_context(nc.psum_tensor("sc_pbt", [128, 1024], BF16))

    nck = MB["NCK"] if nck is None else nck
    if tscr is None:
        tscr = nc.dram_tensor(f"sc_tscr_{id(prog) % 997}", [2, 64, 1], F32,
                              kind="Internal")[:]
    hist = {}
    dmm = prog.op("tensor", lambda t: t.matmul(
        ps_pre[0:128, 0:64], ac["oner"][:], drow[:], start=True, stop=True),
        deps=[l3, ac["ready"]], inc=1)
    db_ev = prog.op("vector", lambda v: v.tensor_copy(Db[:], ps_pre[0:128, 0:64]),
                    deps=[dmm], inc=1)
    hist["pre1_used"] = db_ev

    def H(key):
        v = hist.get(key)
        return [v] if v is not None else list(entry)

    for b in range(nb):
        zcb = prog.op("gpsimd", lambda g: g.memset(carry_bf[:], 0.0),
                      deps=H("carry_used"), inc=1)
        for c in range(nck):
            tlo = b * S + c * CH
            pb = hist
            pdep = [pb.get("tiles_used_pe"), pb.get("tiles_used_ve"),
                    pb.get("tiles_used_po")]
            pdep = [d for d in pdep if d is not None] or list(entry)
            for gx in range(XG):
                xl = prog.op("sync", lambda s, gx=gx, tlo=tlo: s.dma_start(
                    xs[:, gx * CH:(gx + 1) * CH],
                    convT[gx * 128:(gx + 1) * 128, tlo:tlo + CH]),
                    deps=pdep, inc=16, dma=True, sem="sc_lx")
                bl = prog.op("sync", lambda s, gx=gx, tlo=tlo: s.dma_start(
                    bs[:, gx * CH:(gx + 1) * CH],
                    convT[1024 + gx * 128:1024 + (gx + 1) * 128, tlo:tlo + CH]),
                    deps=pdep, inc=16, dma=True, sem="sc_lb")
            for gc in range(NG):
                cl = prog.op("scalar", lambda s, gc=gc, tlo=tlo: s.dma_start(
                    cs_t[:, gc * CH:(gc + 1) * CH],
                    convT[2048 + gc * 128:2048 + (gc + 1) * 128, tlo:tlo + CH]),
                    deps=pdep, inc=16, dma=True, sem="sc_lc")
            xlev = [xl] * XG
            blev = [bl] * XG
            clev = [cl] * NG
            ldt = prog.op("sync", lambda s, tlo=tlo: s.dma_start(
                dts[:], dtT[0:64, tlo:tlo + CH]),
                deps=H("dts_used"), inc=16, dma=True, sem="sc_ldt")
            if stage < 2:
                continue
            p1 = prog.op("scalar", lambda s: s.activation(
                dtsp[:], dts[:], AF.Exp, bias=dtb[:]),
                deps=[ldt, l1] + H("dtsp_used"), inc=1)
            p2 = prog.op("scalar", lambda s: s.activation(
                dtsp[:], dtsp[:], AF.Ln, bias=cc("one")[0:64, :]),
                deps=[p1, cinit_ev], inc=1)
            p3 = prog.op("vector", lambda v: v.tensor_scalar_mul(
                dA[:], dtsp[:], negA[:]), deps=[p2, negA_ev] + H("dA_used"), inc=1)
            p4 = prog.op("vector", lambda v: v.tensor_tensor_scan(
                csum_[:], dA[:], dA[:], 0.0, ALU.add, ALU.bypass),
                deps=[p3] + H("cs_used"), inc=1)
            p5 = prog.op("vector", lambda v: v.reduce_sum(
                tsum[:], dA[:], axis=AX.X), deps=[p3] + H("tsum_used"), inc=1)
            p6 = prog.op("gpsimd", lambda g: g.tensor_copy(csbf[:], csum_[:]),
                         deps=[p4] + H("csbf_used"), inc=1)
            p7 = prog.op("vector", lambda v: v.tensor_scalar_mul(ncs[:], csum_[:], -1.0),
                         deps=[p4] + H("ncs_used"), inc=1)
            if stage == 21:
                continue
            t1 = prog.op("tensor", lambda t: t.matmul(
                ps_pre[0:CH, 0:64], ncs[:], identf[0:64, 0:64],
                start=True, stop=True),
                deps=[p7, ac["ready"]] + H("pre1_used"), inc=1)
            t1e = prog.op("vector", lambda v: v.tensor_copy(ncsT[:], ps_pre[0:CH, 0:64]),
                          deps=[t1] + H("ncsT_used"), inc=1)
            hist["pre1_used"] = t1e
            if stage == 22:
                continue
            t2 = prog.op("tensor", lambda t: t.matmul(
                ps_pre[0:CH, 64:128], dtsp[:], identf[0:64, 0:64],
                start=True, stop=True),
                deps=[p2] + H("pre2_used"), inc=1)
            t2e = prog.op("vector", lambda v: v.tensor_copy(dtsT[:], ps_pre[0:CH, 64:128]),
                          deps=[t2] + H("dtsT_used"), inc=1)
            hist["pre2_used"] = t2e
            if stage == 23:
                continue
            par2 = (b * 16 + c) % 2
            t3 = prog.op("sync", lambda s, par2=par2: s.dma_start(
                tscr[par2, :, 0], tsum[:, 0]),
                deps=[p5] + H(("tscr", par2)), inc=16, dma=True, sem=f"sc_tb{par2}")
            t3e = prog.op("sync", lambda s, par2=par2: s.dma_start(
                tsrow[:], tscr[par2:par2 + 1, :, 0]),
                deps=[t3] + H("tsrow_used"), inc=16, dma=True, sem=f"sc_tr{par2}")
            hist[("tscr", par2)] = t3e
            hist["pre3_used"] = t3e
            if stage == 24:
                continue
            if stage == 27:
                continue
            t4 = prog.op("tensor", lambda t: t.matmul(
                ps_pre[0:128, 192:256], ac["oner"][:], tsrow[:], start=True, stop=True),
                deps=[t3e, ac["ready"]] + H("pre4_used"), inc=1)
            d1 = prog.op("vector", lambda v: v.tensor_add(
                dec[:], ncsT[:], ps_pre[0:128, 192:256]),
                deps=[t4, t1e] + H("dec_used"), inc=1)
            d2 = prog.op("scalar", lambda s: s.activation(dec[:], dec[:], AF.Exp),
                         deps=[d1], inc=1)
            d3 = prog.op("scalar", lambda s: s.activation(cdb[:], ps_pre[0:128, 192:256],
                                                          AF.Exp),
                         deps=[t4] + H("cdb_used"), inc=1)
            hist["pre4_used"] = d3
            hist["tsrow_used"] = t4
            if stage < 3:
                continue
            xt_evs = [None] * XG
            for gx in range(XG):
                tp = prog.op("tensor", lp(nc, lambda t, gx=gx: t.transpose(
                    ps_bt[:, gx * CH:(gx + 1) * CH], xs[:, gx * CH:(gx + 1) * CH],
                    ident[:])),
                    deps=([xlev[gx], ac["ready"]] + (H("psbt_used") if gx == 0 else [])),
                    inc=(1 if gx == XG - 1 else 0))
            xtw = prog.op("scalar", lambda s: s.copy(x_tm[:], ps_bt[:]),
                          deps=[tp] + H("xtm_used"), inc=1)
            for gx in range(XG):
                xt_evs[gx] = xtw
            bt_evs = [None] * XG
            for gx in range(XG):
                tp = prog.op("tensor", lp(nc, lambda t, gx=gx: t.transpose(
                    ps_bt[:, gx * CH:(gx + 1) * CH], bs[:, gx * CH:(gx + 1) * CH],
                    ident[:])),
                    deps=([blev[gx]] + ([xtw] if gx == 0 else [])),
                    inc=(1 if gx == XG - 1 else 0))
            btw = prog.op("scalar", lambda s: s.copy(b_tm[:], ps_bt[:]),
                          deps=[tp] + H("btm_used"), inc=1)
            hist["psbt_used"] = btw
            ddv = prog.op("vector", lambda v: v.tensor_tensor(
                ddec[:], dtsT[:], dec[:], ALU.mult),
                deps=[t2e, d2] + H("ddec_used"), inc=1)
            xdt_ev = {}
            for h in range(NH):
                gx, off = h // 8, (h % 2) * 64
                ea = prog.op("vector", lambda v, gx=gx, off=off, h=h: v.tensor_scalar_mul(
                    xdt[:, h * HD:(h + 1) * HD],
                    x_tm[:, gx * CH + off:gx * CH + off + HD], dtsT[:, h:h + 1]),
                    deps=[xt_evs[gx], t2e] + (H("xdt_used") if h == 0 else []), inc=1)
                eb = prog.op("scalar", lambda s, gx=gx, off=off, h=h: s.activation(
                    xdtd[:, h * HD:(h + 1) * HD],
                    x_tm[:, gx * CH + off:gx * CH + off + HD], AF.Copy,
                    scale=ddec[:, h:h + 1]),
                    deps=[xt_evs[gx], ddv] + (H("xdtd_used") if h == 0 else []), inc=1)
                xdt_ev[h] = (ea, eb)
            if stage < 4:
                continue
            y_evs = []
            s8a = s8b = s9 = s10 = None
            s3e = {}
            s4e = {}
            s6e = {}
            s7e = {}
            for hblk in range(2):
                hs0 = hblk * 32
                heads = range(hs0, hs0 + 32) if stage >= 5 else range(hs0, hs0 + 1)
                for h in heads:
                    hs = h % 32
                    s1 = prog.op("tensor", lambda t, h=h: t.matmul(
                        ps_csb[:], oh[:, h * 128:(h + 1) * 128], csbf[:],
                        start=True, stop=True),
                        deps=[p6, ohev] + H("csb_used") + H("csb_used2"), inc=1)
                    par = h % 2
                    s2 = prog.op("vector", lambda v, h=h, par=par: v.scalar_tensor_tensor(
                        lt[par][:], ps_csb[:], ncsT[:, h:h + 1], mbias[:],
                        ALU.add, ALU.min),
                        deps=[s1, t1e, mbev] + H(("lt_used", par)), inc=1)
                    hist["csb_used2"] = s2
                    s3 = prog.op("scalar", lambda s, hs=hs, par=par: s.activation(
                        lt_blk[:, hs * CH:(hs + 1) * CH], lt[par][:], AF.Exp),
                        deps=[s2] + H(("ltbu", hs)), inc=1)
                    hist[("lt_used", par)] = s3
                    s4 = prog.op("scalar", lambda s, hs=hs: s.activation(
                        ecs_blk[:, hs * CH:(hs + 1) * CH], ps_csb[:], AF.Exp),
                        deps=[s1] + H(("ecsu", hs)), inc=1)
                    hist["csb_used"] = s4
                    s3e[h] = s3
                    s4e[h] = s4
                for h in heads:
                    hs = h % 32
                    par = h % 2
                    gB = h // 8
                    gC = h // 2
                    s5 = prog.op("tensor", lambda t, gB=gB, gC=gC, par=par: t.matmul(
                        ps_g[par][:], bs[:, gB * CH:(gB + 1) * CH],
                        cs_t[:, gC * CH:(gC + 1) * CH], start=True, stop=True),
                        deps=[blev[gB], clev[gC]] + H(("g_used", par)), inc=1)
                    s6 = prog.op("vector", lambda v, hs=hs, par=par: v.tensor_tensor(
                        glt_blk[:, hs * CH:(hs + 1) * CH], ps_g[par][:],
                        lt_blk[:, hs * CH:(hs + 1) * CH], ALU.mult),
                        deps=[s5, s3e[h]] + H(("gltu", hs)), inc=1)
                    hist[("g_used", par)] = s6
                    hist[("ltbu", hs)] = s6
                    s7 = prog.op("gpsimd", lambda g, hs=hs, gC=gC: g.tensor_tensor(
                        cpr_blk[:, hs * CH:(hs + 1) * CH],
                        cs_t[:, gC * CH:(gC + 1) * CH],
                        ecs_blk[:, hs * CH:(hs + 1) * CH], ALU.mult),
                        deps=[s4e[h], clev[gC]] + H(("cpru", hs)), inc=1)
                    hist[("ecsu", hs)] = s7
                    s6e[h] = s6
                    s7e[h] = s7
                for h in heads:
                    hs = h % 32
                    par = h % 2
                    gB = h // 8
                    g2, off = h // 2, (h % 2) * 64
                    s8a = prog.op("tensor", lambda t, h=h, hs=hs, par=par: t.matmul(
                        ps_y[par][:], xdt[:, h * HD:(h + 1) * HD],
                        glt_blk[:, hs * CH:(hs + 1) * CH],
                        start=True, stop=False),
                        deps=[s6e[h], xdt_ev[h][0]] + H(("y_used", par)), inc=0)
                    s8b = prog.op("tensor", lambda t, h=h, hs=hs, par=par: t.matmul(
                        ps_y[par][:], carry_bf[:, h * HD:(h + 1) * HD],
                        cpr_blk[:, hs * CH:(hs + 1) * CH],
                        start=False, stop=True),
                        deps=[s7e[h], zcb] + H(("cbf_w", h)), inc=1)
                    hist[("gltu", hs)] = s8b
                    hist[("cpru", hs)] = s8b
                    s9 = prog.op("tensor", lambda t, h=h, gB=gB: t.matmul(
                        ps_st[:], b_tm[:, gB * CH:(gB + 1) * CH],
                        xdtd[:, h * HD:(h + 1) * HD], start=True, stop=True),
                        deps=[btw, xdt_ev[h][1]] + H("st_used"), inc=1)
                    s10 = prog.op("vector", lambda v, h=h: v.scalar_tensor_tensor(
                        carry_bf[:, h * HD:(h + 1) * HD],
                        carry_bf[:, h * HD:(h + 1) * HD],
                        cdb[:, h:h + 1], ps_st[:], ALU.mult, ALU.add),
                        deps=[s9, s8b, d3, zcb], inc=1)
                    hist["st_used"] = s10
                    hist[("cbf_w", h)] = s10
                    s11 = prog.op("vector", lambda v, h=h, g2=g2, gB=gB, off=off, par=par: v.scalar_tensor_tensor(
                        yall[off:off + 64, g2 * CH:(g2 + 1) * CH],
                        xs[off:off + 64, gB * CH:(gB + 1) * CH],
                        Db[off:off + 64, h:h + 1], ps_y[par][:], ALU.mult, ALU.add),
                        deps=[s8b, xlev[gB], db_ev] + H("yall_used"), inc=1)
                    hist[("y_used", par)] = s11
                    y_evs.append(s11)
            yst = None
            for q in range(8):
                yst = prog.op("scalar", lambda g, q=q, tlo=tlo: g.dma_start(
                    yTd[q * 512:(q + 1) * 512, tlo:tlo + CH].rearrange(
                        "(a p) c -> p a c", p=128),
                    yall[:, q * 512:(q + 1) * 512].rearrange(
                        "p (a c) -> p a c", c=CH)),
                    deps=[y_evs[-1]] + ([pb.get("yst")] if pb.get("yst") else []),
                    inc=16, dma=True, sem="sc_y")
            hist["yst"] = yst
            hist["yall_used"] = yst
            hist["tiles_used_pe"] = s9
            hist["tiles_used_ve"] = y_evs[-1]
            hist["tiles_used_po"] = s7e[NH - 1]
            hist["dts_used"] = p2
            hist["dtsp_used"] = p3
            hist["dA_used"] = p5
            hist["cs_used"] = p7
            hist["tsum_used"] = t3
            hist["csbf_used"] = s1
            hist["ncs_used"] = t1
            hist["ncsT_used"] = s2
            hist["dtsT_used"] = xdt_ev[NH - 1][0]
            hist["dec_used"] = ddv
            hist["ddec_used"] = xdt_ev[NH - 1][1]
            hist["cdb_used"] = s10
            hist["xdt_used"] = s8b
            hist["xdtd_used"] = s9
            hist["xtm_used"] = xdt_ev[NH - 1][0]
            hist["btm_used"] = s9
        hist["carry_used"] = hist.get(("cbf_w", NH - 1))
    return prog.barrier_all()


def emit_gating(nc, stack, prog, ac, cc, cinit_ev, yTd, zT, nw, ynT, entry=()):
    nwt = stack.enter_context(nc.sbuf_tensor("gt_nw", [128, 32], F32))
    lnw = prog.op("sync", lambda s: s.dma_start(
        nwt[:], nw[:, :]), deps=list(entry), inc=16, dma=True, sem="gt_nw")
    yb = [stack.enter_context(nc.sbuf_tensor(f"gt_y{i}", [128, 512], BF16)) for i in range(3)]
    zb = [stack.enter_context(nc.sbuf_tensor(f"gt_z{i}", [128, 512], BF16)) for i in range(3)]
    yz = [stack.enter_context(nc.sbuf_tensor(f"gt_yz{i}", [128, 512], BF16)) for i in range(3)]
    ysq = [stack.enter_context(nc.sbuf_tensor(f"gt_sq{i}", [128, 512], BF16)) for i in range(3)]
    rst = [stack.enter_context(nc.sbuf_tensor(f"gt_rs{i}", [1, 512], F32)) for i in range(3)]
    rsb = [stack.enter_context(nc.sbuf_tensor(f"gt_rsb{i}", [1, 512], BF16)) for i in range(3)]
    ob = [stack.enter_context(nc.sbuf_tensor(f"gt_o{i}", [128, 512], BF16)) for i in range(3)]
    ps_ss = [stack.enter_context(nc.psum_tensor(f"gt_ss{i}", [128, 512], F32))[0:1, :]
             for i in range(2)]
    ps_rb = [stack.enter_context(nc.psum_tensor(f"gt_rb{i}", [128, 512], F32)) for i in range(2)]
    hist = {}

    def H(key):
        v = hist.get(key)
        return [v] if v is not None else list(entry)

    it = 0
    for j in range(32):
        for ci in range(8):
            sl = it % 3
            lo = ci * 512
            ly = prog.op("sync", lambda s, j=j, lo=lo, sl=sl: s.dma_start(
                yb[sl][:], yTd[j * 128:(j + 1) * 128, lo:lo + 512]),
                deps=H(("yz", sl)), inc=16, dma=True, sem=f"gt_y{sl}")
            lz = prog.op("sync", lambda s, j=j, lo=lo, sl=sl: s.dma_start(
                zb[sl][:], zT[j * 128:(j + 1) * 128, lo:lo + 512]),
                deps=H(("sz", sl)), inc=16, dma=True, sem=f"gt_z{sl}")
            sz = prog.op("scalar", lambda s, sl=sl: s.activation(
                zb[sl][:], zb[sl][:], AF.Silu), deps=[lz], inc=1)
            hist[("sz", sl)] = sz
            yzv = prog.op("vector", lambda v, sl=sl: v.tensor_mul(
                yz[sl][:], yb[sl][:], zb[sl][:]), deps=[ly, sz] + H(("sq", sl)), inc=1)
            hist[("yz", sl)] = yzv
            sq = prog.op("scalar", lambda s, sl=sl: s.activation(
                ysq[sl][:], yz[sl][:], AF.Square), deps=[yzv] + H(("ssmm", sl)), inc=1)
            hist[("sq", sl)] = sq
            ssb = it % 2
            ssmm = prog.op("tensor", lambda t, sl=sl, ssb=ssb: t.matmul(
                ps_ss[ssb], ac["onec"][:], ysq[sl][:], start=True, stop=True),
                deps=[sq, ac["ready"]] + H(("ssp", ssb)), inc=1)
            hist[("ssmm", sl)] = ssmm
            r1 = prog.op("vector", lambda v, sl=sl, ssb=ssb: v.tensor_scalar(
                rst[sl][:], ps_ss[ssb], 1.0 / 128.0, LN_EPS, ALU.mult, ALU.add),
                deps=[ssmm] + H(("r2", sl)), inc=1)
            hist[("ssp", ssb)] = r1
            r2 = prog.op("scalar", lambda s, sl=sl: s.activation(
                rst[sl][:], rst[sl][:], AF.Sqrt), deps=[r1], inc=1)
            hist[("r2", sl)] = r2
            r3 = prog.op("vector", lp(nc, lambda v, sl=sl: v.reciprocal(
                rsb[sl][:], rst[sl][:]),
                reason="bf16 rstd broadcast operand (bf16-level tolerance)"),
                deps=[r2] + H(("bc", sl)), inc=1)
            bank = it % 2
            bc = prog.op("tensor", lambda t, sl=sl, bank=bank: t.matmul(
                ps_rb[bank][:], ac["onerb"][:], rsb[sl][:], start=True, stop=True),
                deps=[r3] + H(("bcp", bank)), inc=1)
            hist[("bc", sl)] = bc
            yn = prog.op("vector", lambda v, j=j, sl=sl, bank=bank: v.scalar_tensor_tensor(
                ob[sl][:], yz[sl][:], nwt[:, j:j + 1], ps_rb[bank][:],
                ALU.mult, ALU.mult),
                deps=[bc, lnw, yzv] + H(("st", sl)), inc=1)
            hist[("bcp", bank)] = yn
            st = prog.op("scalar", lambda g, j=j, lo=lo, sl=sl: g.dma_start(
                ynT[j * 128:(j + 1) * 128, lo:lo + 512], ob[sl][:]),
                deps=[yn], inc=16, dma=True, sem=f"gt_st{sl}")
            hist[("st", sl)] = st
            it += 1
    return prog.barrier_all()


def emit_wout(nc, stack, prog, woutT, ynT, s1T, entry=()):
    ktiles = 32
    ab = stack.enter_context(nc.sbuf_tensor("wo_act", [128, ktiles * 1024], BF16))
    res = {
        "wb": [stack.enter_context(nc.sbuf_tensor(f"wo_wb{i}", [128, ktiles * 128], BF16))
               for i in range(2)],
        "ps": [stack.enter_context(nc.psum_tensor(f"wo_ps{i}", [128, 512], F32))
               for i in range(4)],
        "ot": [stack.enter_context(nc.sbuf_tensor(f"wo_ot{i}", [128, 512], BF16))
               for i in range(4)],
    }
    barrier = list(entry)
    for c4 in range(4):
        tlo = c4 * 1024
        aev = None
        for kt in range(ktiles):
            aev = prog.op("scalar", lambda s, kt=kt, tlo=tlo: s.dma_start(
                ab[:, kt * 1024:(kt + 1) * 1024],
                ynT[kt * 128:(kt + 1) * 128, tlo:tlo + 1024]),
                deps=barrier, inc=16, dma=True, sem="wo_a")

        def out_store(mt, msz, ci, lo, sz, src, dep, tlo=tlo):
            g_ = (mt * 2 + ci) % 4
            return prog.op("sync", lambda g: g.dma_start(
                s1T[mt * 128:mt * 128 + msz, tlo + lo:tlo + lo + sz], src),
                deps=[dep], inc=16, dma=True, sem=f"wo_st{g_}")

        gpe, gep, gst = emit_dense_mm(
            nc, stack, prog, f"wo{c4}", wT=woutT, M_tot=D, K_tot=4096, n_tok=1024,
            act_tile=lambda kt, lo, sz: ab[:, kt * 1024 + lo:kt * 1024 + lo + sz],
            act_ready=aev, out_store=out_store, res=res, entry=barrier)
        res["guard"] = gpe[-1]
        res["guard_st"] = gst[-1]
        barrier = [gpe[-1]]
    return prog.barrier_all()



N_CTRL = 130156
N_SAMP = 16777216
NTILES = N_SAMP // 16384


def host_spline_tables():
    t = np.linspace(0.0, N_CTRL - 1.0, N_SAMP, dtype=np.float64)
    i = np.clip(np.floor(t).astype(np.int64), 0, N_CTRL - 2)
    f = (t - i).astype(np.float32).astype(np.float64)
    f2 = f * f
    f3 = f2 * f
    a = np.stack([
        0.5 * (-f + 2 * f2 - f3),
        0.5 * (2.0 - 5 * f2 + 3 * f3),
        0.5 * (f + 4 * f2 - 3 * f3),
        0.5 * (-f2 + f3),
    ], axis=1)
    idx = np.stack([np.clip(i + k, 0, N_CTRL - 1) for k in (-1, 0, 1, 2)],
                   axis=1) + 2
    n_ar = np.arange(N_SAMP, dtype=np.int64)
    m = (n_ar % 16384) // 128
    tau = n_ar // 16384
    base = i[::16384]
    j = idx - (base[tau] + m)[:, None]
    assert j.min() >= 0 and j.max() <= 5, (j.min(), j.max())
    btab = np.zeros((N_SAMP, 6), np.float32)
    np.put_along_axis(btab, j, a.astype(np.float32), axis=1)
    btab = btab.reshape(NTILES, 128, 128, 6).transpose(0, 1, 3, 2)
    slot_mask = (np.abs(btab) > 0).any(axis=(1, 3))
    return (np.ascontiguousarray(btab.astype(BF)), base.astype(np.int64),
            slot_mask)


def host_pp_ext(p):
    return np.concatenate([p[:1], p[:1], p, np.repeat(p[-1:], 8)]).astype(np.float32)


def emit_spline_recon(nc, stack, prog, btab, bases, slot_mask, pps, wouts,
                      entry=(), taus=None, res=None):
    if res is None:
        res = {}
        res["bt"] = [stack.enter_context(nc.sbuf_tensor(f"sp_bt{i}", [128, 768], BF16, side="right")) for i in range(2)]
        res["lt"] = {(w, s): stack.enter_context(nc.sbuf_tensor(f"sp_lt{w}_{s}", [128, 8], F32, side="right"))
                     for w in range(3) for s in range(2)}
        res["acc"] = [[stack.enter_context(nc.sbuf_tensor(f"sp_ac{i}_{p}", [128, 128], F32, side="right"))
                       for p in range(2)] for i in range(3)]
        res["accb"] = [[stack.enter_context(nc.sbuf_tensor(f"sp_ab{i}_{p}", [128, 128], BF16, side="right"))
                        for p in range(2)] for i in range(3)]
        res["strip"] = [
            [stack.enter_context(nc.sbuf_tensor(f"sp_st{w}_{i}", [128, 2048 if w < 2 else 8192],
                                                BF16, side="right")) for i in range(2)]
            for w in range(3)]
        res["ident"] = stack.enter_context(nc.sbuf_tensor("sp_id", [128, 128], BF16, side="right"))
        ident = res["ident"]
        e1 = prog.op("gpsimd", lambda g: g.memset(ident[:], 1.0),
                     deps=list(entry), inc=1, sem="R_cg")
        res["idev"] = prog.op("gpsimd", lambda g: g.affine_select(
            ident[:], ident[:], [[1, 128]], ALU.is_equal, 0.0,
            channel_multiplier=-1), deps=[e1], inc=1, sem="R_cg")
        res["hist"] = {}
        return res
    bt, lt, acc, accb = res["bt"], res["lt"], res["acc"], res["accb"]
    sbuf_strip, ident, idev = res["strip"], res["ident"], res["idev"]
    ps_tr = res["ps_tr"]
    hist = res["hist"]

    def H(k):
        v = hist.get(k)
        return [v] if v is not None else list(entry)

    evs_final = []
    if taus is None:
        taus = range(NTILES)
    for tau in taus:
        bslot = tau % 2
        lb = prog.op("sync", lambda s, tau=tau, bslot=bslot: s.dma_start(
            bt[bslot][:], btab[tau].rearrange("p j n -> p (j n)")),
            deps=H(("bt", bslot)), inc=16, dma=True, sem=f"R_b{bslot}")
        for w in range(3):
            lslot = tau % 2
            src = pps[w][bases[tau]:bases[tau] + 8].unsqueeze(0).broadcast_to([128, 8])
            src.ap[0] = [1, 128]
            lh = prog.op("scalar", lambda s, w=w, lslot=lslot, src=src: s.dma_start(
                lt[(w, lslot)][:], src),
                deps=H(("lt", w, lslot)), inc=16, dma=True, sem=f"R_l{w}{lslot}")
            slots = [j for j in range(6) if slot_mask[tau, j]]
            nA = (len(slots) + 1) // 2
            sA, sB = slots[:nA], slots[nA:]
            ap_ = tau % 2

            def B(j, bslot=bslot):
                return bt[bslot][:, j * 128:(j + 1) * 128]

            def L(j, w=w, lslot=lslot):
                return lt[(w, lslot)][:, j:j + 1]
            fe = prog.op("vector", lambda v, w=w, j=slots[0], ap_=ap_, B=B, L=L: v.tensor_scalar_mul(
                acc[w][ap_][:], B(j), L(j)),
                deps=[lb, lh] + H(("acc", w, ap_)), inc=1, sem="R_cv")
            for j in slots[1:-1]:
                fe = prog.op("vector", lambda v, w=w, j=j, ap_=ap_, B=B, L=L: v.scalar_tensor_tensor(
                    acc[w][ap_][:], B(j), L(j), acc[w][ap_][:], ALU.mult, ALU.add),
                    deps=[fe], inc=1, sem="R_cv")
            fe = prog.op("vector", lambda v, w=w, j=slots[-1], ap_=ap_, B=B, L=L: v.scalar_tensor_tensor(
                accb[w][ap_][:], B(j), L(j), acc[w][ap_][:], ALU.mult, ALU.add),
                deps=[fe] + H(("accb", w, ap_)), inc=1, sem="R_cv")
            hist[("lt", w, lslot)] = fe
            hist[("acc", w, ap_)] = fe
            if w == 2:
                hist[("bt", bslot)] = fe
            grp = tau % 8
            tp = prog.op("tensor", lp(nc, lambda t, w=w, grp=grp, ap_=ap_: t.transpose(
                ps_tr[w][:, grp * 128:(grp + 1) * 128], accb[w][ap_][:], ident[:])),
                deps=[fe, idev] + H(("pstr", w, grp)), inc=1, sem="R_cp")
            hist[("accb", w, ap_)] = tp
            if tau % 4 == 3:
                g4 = (tau // 4) % 2
                if w < 2:
                    sslot = (tau // 16) % 2
                    within = (tau - 3) % 16
                    ev = prog.op("vector", lambda v, w=w, g4=g4, sslot=sslot, within=within: v.tensor_copy(
                        _strip_dst_gu(sbuf_strip[w][sslot], within),
                        _tr_src(ps_tr[w], g4)),
                        deps=[tp] + H(("strip", w, sslot)), inc=1, sem="R_cv")
                else:
                    sslot = (tau // 64) % 2
                    within = (tau - 3) % 64
                    ev = prog.op("vector", lambda v, w=w, g4=g4, sslot=sslot, within=within: v.tensor_copy(
                        _strip_dst_d(sbuf_strip[w][sslot], within),
                        _tr_src(ps_tr[w], g4)),
                        deps=[tp] + H(("strip", w, sslot)), inc=1, sem="R_cv")
                hist[("pstr", w, (tau - 3) % 8)] = ev
                hist[("pstr", w, (tau - 2) % 8)] = ev
                hist[("pstr", w, (tau - 1) % 8)] = ev
                hist[("pstr", w, tau % 8)] = ev
                hist[("evac", w)] = ev
                per = 16 if w < 2 else 64
                if (tau + 1) % per == 0:
                    mt = tau // per
                    sev = prog.op("sync", lambda g, w=w, mt=mt, sslot=sslot: g.dma_start(
                        wouts[w][mt, :, :], sbuf_strip[w][sslot][:]),
                        deps=[ev], inc=16, dma=True, sem=f"R_o{w}{sslot}")
                    hist[("strip", w, sslot)] = sev
                    evs_final.append(sev)
    return res


def _tr_src(ps, g4):
    return ps[:, g4 * 512:(g4 + 1) * 512]


def _strip_dst_gu(buf, within):
    ap = buf[:, within * 8:within * 8 + 512].copy()
    ap.ap.pop()
    ap.ap.extend([[8, 4], [1, 8], [128, 16]])
    return ap


def _strip_dst_d(buf, within):
    ap = buf[:, within * 2:within * 2 + 512].copy()
    ap.ap.pop()
    ap.ap.extend([[2, 4], [1, 2], [128, 64]])
    return ap


def emit_spline_mm(nc, stack, prog, cc, cinit_ev, wgT, wuT, wdT, xn1, s1T,
                   interT, sg1T, entry=()):
    act, act_ev = emit_actT_load(nc, stack, prog, "sg", xn1, D // 128, T,
                                 entry=entry)
    ktiles = 16
    wgb = [stack.enter_context(nc.sbuf_tensor(f"sg_wg{i}", [128, 2048], BF16)) for i in range(2)]
    wub = [stack.enter_context(nc.sbuf_tensor(f"sg_wu{i}", [128, 2048], BF16)) for i in range(2)]
    sil = [stack.enter_context(nc.sbuf_tensor(f"sg_si{i}", [128, 512], BF16)) for i in range(4)]
    ot = [stack.enter_context(nc.sbuf_tensor(f"sg_ot{i}", [128, 512], BF16)) for i in range(4)]
    psA = [stack.enter_context(nc.psum_tensor(f"sg_psA{i}", [128, 512], F32)) for i in range(2)]
    psB = [stack.enter_context(nc.psum_tensor(f"sg_psB{i}", [128, 512], F32)) for i in range(2)]
    hist = {}

    def H(k):
        v = hist.get(k)
        return [v] if v is not None else list(entry)

    for mt in range(64):
        par = mt % 2
        lg = prog.op("sync", lambda s, mt=mt, par=par: s.dma_start(
            wgb[par][:], wgT[mt, :, :]), deps=H(("wg", par)), inc=16, dma=True,
            sem=f"sg_wg{par}")
        lu = prog.op("sync", lambda s, mt=mt, par=par: s.dma_start(
            wub[par][:], wuT[mt, :, :]), deps=H(("wu", par)), inc=16, dma=True,
            sem=f"sg_wu{par}")
        for ci in range(8):
            lo = ci * 512
            bank = ci % 2
            slot = ci % 4
            depsA = [lg, act_ev] + ([hist.get(("eA", bank))] if ("eA", bank) in hist else [])
            depsB = [lu, act_ev] + ([hist.get(("eB", bank))] if ("eB", bank) in hist else [])
            peA = peB = None
            for kt in range(ktiles):
                peA = prog.op("tensor", lambda t, kt=kt, par=par, bank=bank, lo=lo: t.matmul(
                    psA[bank][:], wgb[par][:, kt * 128:(kt + 1) * 128],
                    act(kt)[:, lo:lo + 512], start=(kt == 0), stop=(kt == ktiles - 1)),
                    deps=(depsA if kt == 0 else ()), inc=(1 if kt == ktiles - 1 else 0))
            for kt in range(ktiles):
                peB = prog.op("tensor", lambda t, kt=kt, par=par, bank=bank, lo=lo: t.matmul(
                    psB[bank][:], wub[par][:, kt * 128:(kt + 1) * 128],
                    act(kt)[:, lo:lo + 512], start=(kt == 0), stop=(kt == ktiles - 1)),
                    deps=(depsB if kt == 0 else ()), inc=(1 if kt == ktiles - 1 else 0))
            hist[("wg", par)] = peB
            hist[("wu", par)] = peB
            sv = prog.op("scalar", lambda s, bank=bank, slot=slot: s.activation(
                sil[slot][:], psA[bank][:], AF.Silu),
                deps=[peA] + H(("sil", slot)), inc=1)
            hist[("eA", bank)] = sv
            hv = prog.op("vector", lambda v, bank=bank, slot=slot: v.tensor_tensor(
                ot[slot][:], psB[bank][:], sil[slot][:], ALU.mult),
                deps=[peB, sv] + H(("st", slot)), inc=1)
            hist[("sil", slot)] = hv
            hist[("eB", bank)] = hv
            stv = prog.op("gpsimd", lambda g, mt=mt, lo=lo, slot=slot: g.dma_start(
                interT[mt * 128:(mt + 1) * 128, lo:lo + 512], ot[slot][:]),
                deps=[hv], inc=16, dma=True, sem=f"sg_st{slot}")
            hist[("st", slot)] = stv
    return prog.barrier_all()


def emit_spline_down(nc, stack, prog, cc, cinit_ev, wdT, interT, s1T, sg1T,
                     entry=()):
    ktiles = 64
    hb = stack.enter_context(nc.sbuf_tensor("sd_hb", [128, ktiles * 1024], BF16))
    wqb = [stack.enter_context(nc.sbuf_tensor(f"sd_wq{i}", [128, ktiles * 128], BF16)) for i in range(2)]
    s0b = [stack.enter_context(nc.sbuf_tensor(f"sd_s0{i}", [128, 512], BF16)) for i in range(4)]
    gb = [stack.enter_context(nc.sbuf_tensor(f"sd_g{i}", [128, 512], BF16)) for i in range(4)]
    ot = [stack.enter_context(nc.sbuf_tensor(f"sd_ot{i}", [128, 512], BF16)) for i in range(4)]
    ps = [stack.enter_context(nc.psum_tensor(f"sd_ps{i}", [128, 512], F32)) for i in range(4)]
    hist = {}

    def H(k):
        v = hist.get(k)
        return [v] if v is not None else list(entry)

    hfree = None
    for c4 in range(4):
        tlo = c4 * 1024
        hl = None
        for kt in range(ktiles):
            hl = prog.op("scalar", lambda s, kt=kt, tlo=tlo: s.dma_start(
                hb[:, kt * 1024:(kt + 1) * 1024],
                interT[kt * 128:(kt + 1) * 128, tlo:tlo + 1024]),
                deps=([hfree] if hfree else list(entry)) if kt == 0 else (),
                inc=16, dma=True, sem="sd_h")
        hready = hl
        g = 0
        pe = None
        for mt in range(16):
            par = mt % 2
            wl = prog.op("sync", lambda s, mt=mt, par=par: s.dma_start(
                wqb[par][:], wdT[mt, :, :]),
                deps=H(("wq", par)), inc=16, dma=True, sem=f"sd_w{par}")
            for half in range(2):
                lo = tlo + half * 512
                bank = g % 4
                slot = g % 4
                deps = [wl, hready] + ([hist.get(("ep", bank))] if ("ep", bank) in hist else [])
                for kt in range(ktiles):
                    pe = prog.op("tensor", lambda t, kt=kt, par=par, bank=bank, half=half: t.matmul(
                        ps[bank][:], wqb[par][:, kt * 128:(kt + 1) * 128],
                        hb[:, kt * 1024 + half * 512:kt * 1024 + half * 512 + 512],
                        start=(kt == 0), stop=(kt == ktiles - 1)),
                        deps=(deps if kt == 0 else ()), inc=(1 if kt == ktiles - 1 else 0))
                hist[("wq", par)] = pe
                sl = prog.op("sync", lambda gg, mt=mt, lo=lo, slot=slot: gg.dma_start(
                    s0b[slot][:], s1T[mt * 128:(mt + 1) * 128, lo:lo + 512]),
                    deps=H(("s0", slot)), inc=16, dma=True, sem=f"sd_s0{slot}")
                gv = prog.op("scalar", lambda s, bank=bank, slot=slot: s.activation(
                    gb[slot][:], ps[bank][:], AF.Silu),
                    deps=[pe] + H(("gb", slot)), inc=1)
                hist[("ep", bank)] = gv
                hv = prog.op("vector", lambda v, slot=slot: v.tensor_mul(
                    ot[slot][:], gb[slot][:], s0b[slot][:]),
                    deps=[gv, sl] + H(("stv", slot)), inc=1)
                hist[("gb", slot)] = hv
                hist[("s0", slot)] = hv
                stv = prog.op("scalar", lambda gg, mt=mt, lo=lo, slot=slot: gg.dma_start(
                    sg1T[mt * 128:(mt + 1) * 128, lo:lo + 512], ot[slot][:]),
                    deps=[hv], inc=16, dma=True, sem=f"sd_st{slot}")
                hist[("stv", slot)] = stv
                g += 1
        hfree = pe
    return prog.barrier_all()



def build_ln(n_tok=T, d=D):
    nc = bass.Bass("TRN2", target_bir_lowering=False)
    xin = nc.dram_tensor("xin", [n_tok, d], F32, kind="ExternalInput")
    xout = nc.dram_tensor("xout", [n_tok, d], BF16, kind="ExternalOutput")
    with ExitStack() as stack:
        block = stack.enter_context(nc.Block())
        ct, cc, cinit = emit_consts(nc, block, stack)
        prog = Prog(nc, stack, "ln")
        emit_ln(nc, stack, prog, cc, (cinit.h, cinit.v), xin[:], xout[:])
        prog.emit(block)
        prog.drain(block)
    return nc


def build_qkv():
    nc = bass.Bass("TRN2", target_bir_lowering=False)
    xn0 = nc.dram_tensor("xn0", [T, D], BF16, kind="ExternalInput")
    wT = nc.dram_tensor("WqkvT", [3 * D // 128, 128, D], F32, kind="ExternalInput")
    outT = nc.dram_tensor("qkvT", [3 * D, T], BF16, kind="ExternalOutput")
    with ExitStack() as stack:
        block = stack.enter_context(nc.Block())
        prog = Prog(nc, stack, "qkv")
        act, act_ev = emit_actT_load(nc, stack, prog, "qkv", xn0[:], D // 128, T)

        def out_store(mt, msz, ci, lo, sz, src, dep):
            g = mt * 8 + ci
            return prog.op("sync", lambda s: s.dma_start(
                outT[mt * 128:mt * 128 + msz, lo:lo + sz], src), deps=[dep],
                inc=16, dma=True, sem=f"qkv_st{g % 4}")

        emit_dense_mm(nc, stack, prog, "qkv", wT=wT[:], M_tot=3 * D, K_tot=D,
                      n_tok=T, act_tile=lambda kt, lo, sz: act(kt)[:, lo:lo + sz],
                      act_ready=act_ev, out_store=out_store)
        prog.emit(block)
        prog.drain(block)
    return nc


def build_l0ffn(debug=False):
    nc = bass.Bass("TRN2", target_bir_lowering=False)
    xn0 = nc.dram_tensor("xn0", [T, D], BF16, kind="ExternalInput")
    w1T = nc.dram_tensor("W1T", [44, 128, D], F32, kind="ExternalInput")
    w2T = nc.dram_tensor("W2T", [44, 128, D], F32, kind="ExternalInput")
    w3T = nc.dram_tensor("W3T", [16, 128, 5632], F32, kind="ExternalInput")
    s0T = nc.dram_tensor("s0T", [D, T], BF16, kind="ExternalInput")
    x = nc.dram_tensor("x", [T, D], F32, kind="ExternalInput")
    x1 = nc.dram_tensor("x1", [T, D], F32, kind="ExternalOutput")
    ikind = "ExternalOutput" if debug else "Internal"
    hidT = nc.dram_tensor("hidT", [5632, T], BF16, kind=ikind)
    w3q = nc.dram_tensor("w3q", [16, 128, 5632], BF16, kind=ikind)
    sgT = nc.dram_tensor("sgT", [D, T], BF16, kind=ikind)
    with ExitStack() as stack:
        block = stack.enter_context(nc.Block())
        ct, cc, cinit = emit_consts(nc, block, stack)
        cinit_ev = (cinit.h, cinit.v)
        prog = Prog(nc, stack, "bl")
        ac = emit_attn_consts(nc, stack, prog, pfx="blc")
        thrb, thr_ev = emit_gamma(nc, stack, prog, ac,
                                  [(w1T[:], 44, D), (w2T[:], 44, D), (w3T[:], 16, 5632)])
        with ExitStack() as s1:
            b1 = emit_bitlinear_hid(nc, s1, prog, ac, cc, cinit_ev, thrb, thr_ev,
                                    w1T[:], w2T[:], xn0[:], hidT[:], entry=[thr_ev])
        with ExitStack() as s2:
            b2 = emit_w3_quant(nc, s2, prog, thrb, thr_ev, w3T[:], w3q[:], entry=b1)
        with ExitStack() as s3:
            b3 = emit_w3_mm(nc, s3, prog, cc, cinit_ev, w3q[:], hidT[:], s0T[:],
                            sgT[:], entry=b2)
        with ExitStack() as s4:
            emit_residual(nc, s4, prog, x[:], sgT[:], x1[:], entry=b3)
        prog.emit(block)
        prog.drain(block)
    return nc


def build_l1attn(debug=False, upto="all"):
    nc = bass.Bass("TRN2", target_bir_lowering=False)
    xn1 = nc.dram_tensor("xn1", [T, D], BF16, kind="ExternalInput")
    winT = nc.dram_tensor("WinT", [81, 128, D], F32, kind="ExternalInput")
    convw = nc.dram_tensor("convw", [128, 192], F32, kind="ExternalInput")
    convb = nc.dram_tensor("convb", [128, 48], F32, kind="ExternalInput")
    dtbias = nc.dram_tensor("dtbias", [64, 1], F32, kind="ExternalInput")
    alog = nc.dram_tensor("alog", [64, 1], F32, kind="ExternalInput")
    dvec = nc.dram_tensor("dvec", [1, 64], F32, kind="ExternalInput")
    nw = nc.dram_tensor("nw", [128, 32], F32, kind="ExternalInput")
    woutT = nc.dram_tensor("WoutT", [16, 128, 4096], F32, kind="ExternalInput")
    s1T = nc.dram_tensor("s1T", [D, T], BF16, kind="ExternalOutput")
    ikind = "ExternalOutput" if debug else "Internal"
    zT = nc.dram_tensor("zT", [4096, T], BF16, kind=ikind)
    xbcT = nc.dram_tensor("xbcT", [6144, T], BF16, kind="Internal")
    convT = nc.dram_tensor("convT", [6144, T], BF16, kind=ikind)
    dtT = nc.dram_tensor("dtT", [64, T], F32, kind=ikind)
    yT = nc.dram_tensor("yT", [4096, T], BF16, kind=ikind)
    ynT = nc.dram_tensor("ynT", [4096, T], BF16, kind="Internal")
    with ExitStack() as stack:
        block = stack.enter_context(nc.Block())
        ct, cc, cinit = emit_consts(nc, block, stack)
        cinit_ev = (cinit.h, cinit.v)
        prog = Prog(nc, stack, "ma")
        ac = emit_attn_consts(nc, stack, prog, pfx="mac")
        with ExitStack() as s1:
            b1 = emit_inproj(nc, s1, prog, zT[:], xbcT[:], dtT[:], xn1[:], winT[:])
        if upto != "inproj":
            with ExitStack() as s2:
                b2 = emit_conv(nc, s2, prog, ac, xbcT[:], convT[:], convw[:], convb[:],
                               entry=b1)
        if upto not in ("inproj", "conv"):
            with ExitStack() as s3:
                b3 = emit_scan(nc, s3, prog, ac, cc, cinit_ev, convT[:], dtT[:], yT[:],
                               dtbias[:], alog[:], dvec[:], entry=b2)
        if upto not in ("inproj", "conv", "scan"):
            with ExitStack() as s4:
                b4 = emit_gating(nc, s4, prog, ac, cc, cinit_ev, yT[:], zT[:], nw[:],
                                 ynT[:], entry=b3)
            with ExitStack() as s5:
                emit_wout(nc, s5, prog, woutT[:], ynT[:], s1T[:], entry=b4)
        prog.emit(block)
        prog.drain(block)
    return nc


def build_scan_only(nck=2, nb=1, stage=9):
    nc = bass.Bass("TRN2", target_bir_lowering=False)
    convT = nc.dram_tensor("convT", [6144, T], BF16, kind="ExternalInput")
    dtT = nc.dram_tensor("dtT", [64, T], F32, kind="ExternalInput")
    dtbias = nc.dram_tensor("dtbias", [64, 1], F32, kind="ExternalInput")
    alog = nc.dram_tensor("alog", [64, 1], F32, kind="ExternalInput")
    dvec = nc.dram_tensor("dvec", [1, 64], F32, kind="ExternalInput")
    yT = nc.dram_tensor("yT", [4096, T], BF16, kind="ExternalOutput")
    with ExitStack() as stack:
        block = stack.enter_context(nc.Block())
        ct, cc, cinit = emit_consts(nc, block, stack)
        cinit_ev = (cinit.h, cinit.v)
        prog = Prog(nc, stack, "sc")
        ac = emit_attn_consts(nc, stack, prog, pfx="scc")
        emit_scan(nc, stack, prog, ac, cc, cinit_ev, convT[:], dtT[:], yT[:],
                  dtbias[:], alog[:], dvec[:], nck=nck, nb=nb, stage=stage)
        prog.emit(block)
        prog.drain(block)
    return nc


_SPLINE_CACHE = {}


def spline_tables_cached():
    if "t" not in _SPLINE_CACHE:
        _SPLINE_CACHE["t"] = host_spline_tables()
    return _SPLINE_CACHE["t"]


def build_l1ffn(debug=False):
    btab_np, bases, slot_mask = spline_tables_cached()
    nc = bass.Bass("TRN2", target_bir_lowering=False)
    xn1 = nc.dram_tensor("xn1", [T, D], BF16, kind="ExternalInput")
    ppg = nc.dram_tensor("ppg", [N_CTRL + 10], F32, kind="ExternalInput")
    ppu = nc.dram_tensor("ppu", [N_CTRL + 10], F32, kind="ExternalInput")
    ppd = nc.dram_tensor("ppd", [N_CTRL + 10], F32, kind="ExternalInput")
    btab = nc.dram_tensor("btab", [NTILES, 128, 6, 128], BF16, kind="ExternalInput")
    s1T = nc.dram_tensor("s1T", [D, T], BF16, kind="ExternalInput")
    x1 = nc.dram_tensor("x1", [T, D], F32, kind="ExternalInput")
    out = nc.dram_tensor("out", [T, D], F32, kind="ExternalOutput")
    ikind = "ExternalOutput" if debug else "Internal"
    wgT = nc.dram_tensor("wgT", [64, 128, 2048], BF16, kind=ikind)
    wuT = nc.dram_tensor("wuT", [64, 128, 2048], BF16, kind="Internal")
    wdT = nc.dram_tensor("wdT", [16, 128, 8192], BF16, kind=ikind)
    interT = nc.dram_tensor("interT", [8192, T], BF16, kind="Internal")
    sg1T = nc.dram_tensor("sg1T", [D, T], BF16, kind="Internal")
    with ExitStack() as stack:
        block = stack.enter_context(nc.Block())
        ct, cc, cinit = emit_consts(nc, block, stack)
        cinit_ev = (cinit.h, cinit.v)
        prog = Prog(nc, stack, "sp")
        with ExitStack() as s1:
            b1 = emit_spline_recon(nc, s1, prog, btab[:], list(bases),
                                   slot_mask, [ppg[:], ppu[:], ppd[:]],
                                   [wgT[:], wuT[:], wdT[:]])
        prog.phase_reset()
        with ExitStack() as s2:
            b2 = emit_spline_mm(nc, s2, prog, cc, cinit_ev, wgT[:], wuT[:],
                                wdT[:], xn1[:], s1T[:], interT[:], sg1T[:],
                                entry=b1)
        prog.phase_reset()
        with ExitStack() as s3:
            b3 = emit_spline_down(nc, s3, prog, cc, cinit_ev, wdT[:], interT[:],
                                  s1T[:], sg1T[:], entry=b2)
        prog.phase_reset()
        with ExitStack() as s4:
            emit_residual(nc, s4, prog, x1[:], sg1T[:], out[:], entry=b3)
        prog.emit(block)
        prog.drain(block)
    return nc


def build_attn():
    nc = bass.Bass("TRN2", target_bir_lowering=False)
    qkvT = nc.dram_tensor("qkvT", [3 * D, T], BF16, kind="ExternalInput")
    wprojT = nc.dram_tensor("WprojT", [16, 128, D], F32, kind="ExternalInput")
    s0T = nc.dram_tensor("s0T", [D, T], BF16, kind="ExternalOutput")
    with ExitStack() as stack:
        block = stack.enter_context(nc.Block())
        ct, cc, cinit = emit_consts(nc, block, stack)
        cinit_ev = (cinit.h, cinit.v)
        prog = Prog(nc, stack, "at")
        ac = emit_attn_consts(nc, stack, prog)
        emit_attn(nc, stack, prog, cc, cinit_ev, ac, qkvT[:], s0T[:], wprojT[:])
        prog.emit(block)
        prog.drain(block)
    return nc


def build_fused():
    btab_np, bases, slot_mask = spline_tables_cached()
    nc = bass.Bass("TRN2", target_bir_lowering=False)
    EI = "ExternalInput"
    x = nc.dram_tensor("x", [T, D], F32, kind=EI)
    wqkvT = nc.dram_tensor("WqkvT", [48, 128, D], F32, kind=EI)
    wprojT = nc.dram_tensor("WprojT", [16, 128, D], F32, kind=EI)
    w1T = nc.dram_tensor("W1T", [44, 128, D], F32, kind=EI)
    w2T = nc.dram_tensor("W2T", [44, 128, D], F32, kind=EI)
    w3T = nc.dram_tensor("W3T", [16, 128, 5632], F32, kind=EI)
    winT = nc.dram_tensor("WinT", [81, 128, D], F32, kind=EI)
    convw = nc.dram_tensor("convw", [128, 192], F32, kind=EI)
    convb = nc.dram_tensor("convb", [128, 48], F32, kind=EI)
    dtbias = nc.dram_tensor("dtbias", [64, 1], F32, kind=EI)
    alog = nc.dram_tensor("alog", [64, 1], F32, kind=EI)
    dvec = nc.dram_tensor("dvec", [1, 64], F32, kind=EI)
    nw = nc.dram_tensor("nw", [128, 32], F32, kind=EI)
    woutT = nc.dram_tensor("WoutT", [16, 128, 4096], F32, kind=EI)
    ppg = nc.dram_tensor("ppg", [N_CTRL + 10], F32, kind=EI)
    ppu = nc.dram_tensor("ppu", [N_CTRL + 10], F32, kind=EI)
    ppd = nc.dram_tensor("ppd", [N_CTRL + 10], F32, kind=EI)
    btab = nc.dram_tensor("btab", [NTILES, 128, 6, 128], BF16, kind=EI)
    out = nc.dram_tensor("out", [T, D], F32, kind="ExternalOutput")
    IN = "Internal"
    xn0 = nc.dram_tensor("xn0", [T, D], BF16, kind=IN)
    qkvT = nc.dram_tensor("qkvT", [3 * D, T], BF16, kind=IN)
    s0T = nc.dram_tensor("s0T", [D, T], BF16, kind=IN)
    hidT = nc.dram_tensor("hidT", [5632, T], BF16, kind=IN)
    w3q = nc.dram_tensor("w3q", [16, 128, 5632], BF16, kind=IN)
    sgT = nc.dram_tensor("sgT", [D, T], BF16, kind=IN)
    x1 = nc.dram_tensor("x1", [T, D], F32, kind=IN)
    xn1 = nc.dram_tensor("xn1", [T, D], BF16, kind=IN)
    zT = nc.dram_tensor("zT", [4096, T], BF16, kind=IN)
    xbcT = nc.dram_tensor("xbcT", [6144, T], BF16, kind=IN)
    convT = nc.dram_tensor("convT", [6144, T], BF16, kind=IN)
    dtT = nc.dram_tensor("dtT", [64, T], F32, kind=IN)
    yT = nc.dram_tensor("yT", [4096, T], BF16, kind=IN)
    ynT = nc.dram_tensor("ynT", [4096, T], BF16, kind=IN)
    s1T = nc.dram_tensor("s1T", [D, T], BF16, kind=IN)
    wgT = nc.dram_tensor("wgT", [64, 128, 2048], BF16, kind=IN)
    wuT = nc.dram_tensor("wuT", [64, 128, 2048], BF16, kind=IN)
    wdT = nc.dram_tensor("wdT", [16, 128, 8192], BF16, kind=IN)
    interT = nc.dram_tensor("interT", [8192, T], BF16, kind=IN)
    sg1T = nc.dram_tensor("sg1T", [D, T], BF16, kind=IN)

    with ExitStack() as stack:
        block = stack.enter_context(nc.Block())
        ct, cc, cinit = emit_consts(nc, block, stack)
        cie = (cinit.h, cinit.v)
        prog = Prog(nc, stack, "f")
        ac = emit_attn_consts(nc, stack, prog, pfx="fc")

        def nxt(b):
            prog.phase_reset()
            return b

        rstack = stack.enter_context(ExitStack())
        rres = emit_spline_recon(nc, rstack, prog, btab[:], list(bases),
                                 slot_mask, [ppg[:], ppu[:], ppd[:]],
                                 [wgT[:], wuT[:], wdT[:]], entry=[cie])

        def rchunk(lo, hi):
            prog.side_begin()
            emit_spline_recon(nc, rstack, prog, btab[:], list(bases),
                              slot_mask, [ppg[:], ppu[:], ppd[:]],
                              [wgT[:], wuT[:], wdT[:]], taus=range(lo, hi),
                              res=rres)
            return prog.side_end()

        with ExitStack() as s:
            prog.phase = "ln0"
            b = nxt(emit_ln(nc, s, prog, cc, cie, x[:], xn0[:], pfx="l0"))
        with ExitStack() as s:
            prog.phase = "gamma"
            thrb, thr_ev = emit_gamma(nc, stack, prog, ac,
                                      [(w1T[:], 44, D), (w2T[:], 44, D),
                                       (w3T[:], 16, 5632)], entry=b)
            b = nxt(prog.barrier_all())
        with ExitStack() as ps_seg:
            rres["ps_tr"] = [ps_seg.enter_context(
                nc.psum_tensor(f"rA{w}", [128, 1024], BF16)) for w in range(3)]
            prog.mark("A0")
            prog.phase = "qkv"
            with ExitStack() as s:
                qk_buf = s.enter_context(
                    nc.sbuf_tensor("qk_actT", [128, 16 * S], BF16))
                qk_res = {
                    "wb": [s.enter_context(nc.sbuf_tensor(
                        f"qk_wb{i}", [128, 16 * 128], BF16)) for i in range(2)],
                    "ps": [s.enter_context(nc.psum_tensor(
                        f"qk_ps{i}", [128, 512], F32)) for i in range(4)],
                    "ot": [s.enter_context(nc.sbuf_tensor(
                        f"qk_ot{i}", [128, 512], BF16)) for i in range(4)],
                }
                for hh in range(2):
                    act, aev = emit_actT_load(nc, s, prog, f"qk{hh}", xn0[:],
                                              D // 128, S, entry=b,
                                              tok_lo=hh * S, buf=qk_buf)

                    def qst(mt, msz, ci, lo, sz, src, dep, hh=hh):
                        g = mt * 4 + ci
                        return prog.op("sync", lambda ss: ss.dma_start(
                            qkvT[mt * 128:mt * 128 + msz,
                                 hh * S + lo:hh * S + lo + sz], src), deps=[dep],
                            inc=16, dma=True, sem=f"qk_st{g % 4}")
                    gpe, gep, gst = emit_dense_mm(
                        nc, s, prog, f"qk{hh}", wT=wqkvT[:], M_tot=3 * D,
                        K_tot=D, n_tok=S,
                        act_tile=lambda kt, lo, sz: act(kt)[:, lo:lo + sz],
                        act_ready=aev, out_store=qst, entry=b, res=qk_res)
                    qk_res["guard"] = gpe[-1]
                    qk_res["guard_st"] = gst[-1]
                    b = nxt(prog.barrier_all())
            prog.phase = "blhid"
            with ExitStack() as s:
                bl_bufs = None
                for hh in range(2):
                    bb, bl_bufs = emit_bitlinear_hid(
                        nc, s, prog, ac, cc, cie, thrb, thr_ev, w1T[:], w2T[:],
                        xn0[:], hidT[:], entry=b, tok_lo=hh * S, n_tok=S,
                        pfx="bl", bufs=bl_bufs)
                    b = nxt(bb)
            prog.mark("A1")
            prog.phase = "recon"
            prog.splice(rchunk(0, 360), "A0", "A1")
        b = b + prog.r_totals()
        with ExitStack() as s:
            prog.phase = "attn"
            emit_attn(nc, s, prog, cc, cie, ac, qkvT[:], s0T[:], wprojT[:],
                      entry=b)
            b = nxt(prog.barrier_all())
        with ExitStack() as ps_seg:
            rres["ps_tr"] = [ps_seg.enter_context(
                nc.psum_tensor(f"rB{w}", [128, 1024], BF16)) for w in range(3)]
            prog.mark("B0")
            with ExitStack() as s:
                prog.phase = "w3q"
                b = nxt(emit_w3_quant(nc, s, prog, thrb, thr_ev, w3T[:], w3q[:],
                                      entry=b))
            with ExitStack() as s:
                prog.phase = "w3mm"
                b = nxt(emit_w3_mm(nc, s, prog, cc, cie, w3q[:], hidT[:], s0T[:],
                                   sgT[:], entry=b))
            with ExitStack() as s:
                prog.phase = "resid0"
                b = nxt(emit_residual(nc, s, prog, x[:], sgT[:], x1[:], entry=b))
            with ExitStack() as s:
                prog.phase = "ln1"
                b = nxt(emit_ln(nc, s, prog, cc, cie, x1[:], xn1[:], pfx="l1",
                                entry=b))
            prog.phase = "inproj"
            with ExitStack() as s:
                ip_bufs = None
                for hh in range(2):
                    bb, ip_bufs = emit_inproj(nc, s, prog, zT[:], xbcT[:], dtT[:],
                                              xn1[:], winT[:], entry=b,
                                              tok_lo=hh * S, n_tok=S, pfx="ip",
                                              bufs=ip_bufs)
                    b = nxt(bb)
            with ExitStack() as s:
                prog.phase = "conv"
                b = nxt(emit_conv(nc, s, prog, ac, xbcT[:], convT[:], convw[:],
                                  convb[:], entry=b))
            prog.mark("B1")
            prog.phase = "recon"
            prog.splice(rchunk(360, 660), "B0", "B1")
        b = b + prog.r_totals()
        with ExitStack() as s:
            prog.phase = "scan"
            b = nxt(emit_scan(nc, s, prog, ac, cc, cie, convT[:], dtT[:], yT[:],
                              dtbias[:], alog[:], dvec[:], entry=b))
        with ExitStack() as ps_seg:
            rres["ps_tr"] = [ps_seg.enter_context(
                nc.psum_tensor(f"rC{w}", [128, 1024], BF16)) for w in range(3)]
            prog.mark("C0")
            with ExitStack() as s:
                prog.phase = "gating"
                b = nxt(emit_gating(nc, s, prog, ac, cc, cie, yT[:], zT[:], nw[:],
                                    ynT[:], entry=b))
            with ExitStack() as s:
                prog.phase = "wout"
                b = nxt(emit_wout(nc, s, prog, woutT[:], ynT[:], s1T[:], entry=b))
            prog.mark("C1")
            prog.phase = "recon"
            prog.splice(rchunk(660, NTILES), "C0", "C1")
        rstack.close()
        b = b + prog.r_totals()
        with ExitStack() as s:
            prog.phase = "splmm"
            b = nxt(emit_spline_mm(nc, s, prog, cc, cie, wgT[:], wuT[:], wdT[:],
                                   xn1[:], s1T[:], interT[:], sg1T[:], entry=b))
        with ExitStack() as s:
            prog.phase = "spldown"
            b = nxt(emit_spline_down(nc, s, prog, cc, cie, wdT[:], interT[:],
                                     s1T[:], sg1T[:], entry=b))
        with ExitStack() as s:
            prog.phase = "resid1"
            nxt(emit_residual(nc, s, prog, x1[:], sg1T[:], out[:], entry=b,
                              pfx="rs2"))
        prog.apply_splices()
        prog.emit(block)
        prog.drain(block)
    return nc


MULTI_LAUNCH = False
_NC_CACHE = {}


def _get_nc(name, builder):
    if name not in _NC_CACHE:
        _NC_CACHE[name] = builder()
    return _NC_CACHE[name]


def kernel(L0_attn_x, L0_attn_Wqkv, L0_attn_Wproj, L0_ffn_w1, L0_ffn_w2,
           L0_ffn_w3, L1_attn_W_in, L1_attn_conv_w, L1_attn_conv_b,
           L1_attn_dt_bias, L1_attn_A_log, L1_attn_D, L1_attn_norm_weight,
           L1_attn_W_out, L1_ffn_gate_cp, L1_ffn_up_cp, L1_ffn_down_cp):
    cores = list(range(N_CORES))
    f32 = np.float32

    def strips(w):
        return host_strips(np.ascontiguousarray(np.asarray(w, f32).T))

    x = np.ascontiguousarray(np.asarray(L0_attn_x, f32))
    xs = [x[2 * c:2 * c + 2].reshape(T, D) for c in cores]
    wqkv_s = strips(L0_attn_Wqkv)
    wproj_s = strips(L0_attn_Wproj)
    w1_s = strips(L0_ffn_w1)
    w2_s = strips(L0_ffn_w2)
    w3_s = strips(L0_ffn_w3)
    win_s = strips(L1_attn_W_in)
    wout_s = strips(L1_attn_W_out)
    convw = np.ascontiguousarray(np.asarray(L1_attn_conv_w, f32)
                                 .reshape(48, 128, 4).transpose(1, 0, 2).reshape(128, 192))
    convb = np.ascontiguousarray(np.asarray(L1_attn_conv_b, f32).reshape(48, 128).T)
    dtb = np.asarray(L1_attn_dt_bias, f32).reshape(64, 1)
    alog = np.asarray(L1_attn_A_log, f32).reshape(64, 1)
    dvec = np.asarray(L1_attn_D, f32).reshape(1, 64)
    nw = np.ascontiguousarray(np.asarray(L1_attn_norm_weight, f32).reshape(32, 128).T)
    ppg = host_pp_ext(np.asarray(L1_ffn_gate_cp, f32))
    ppu = host_pp_ext(np.asarray(L1_ffn_up_cp, f32))
    ppd = host_pp_ext(np.asarray(L1_ffn_down_cp, f32))
    btab, _bases, _sm = spline_tables_cached()

    if not MULTI_LAUNCH:
        nc = _get_nc("fused", build_fused)
        in_maps = [{
            "x": xs[c], "WqkvT": wqkv_s, "WprojT": wproj_s, "W1T": w1_s,
            "W2T": w2_s, "W3T": w3_s, "WinT": win_s, "convw": convw,
            "convb": convb, "dtbias": dtb, "alog": alog, "dvec": dvec,
            "nw": nw, "WoutT": wout_s, "ppg": ppg, "ppu": ppu, "ppd": ppd,
            "btab": btab} for c in cores]
        res = run_bass_kernel_spmd(nc, in_maps, core_ids=cores)
        return np.concatenate(
            [res.results[c]["out"].reshape(2, S, D) for c in cores], axis=0)

    def launch(name, builder, in_map_fn):
        nc = _get_nc(name, builder)
        res = run_bass_kernel_spmd(nc, [in_map_fn(c) for c in cores], core_ids=cores)
        return res.results

    r = launch("ln", build_ln, lambda c: {"xin": xs[c]})
    xn0 = [r[c]["xout"] for c in cores]
    r = launch("qkv", build_qkv, lambda c: {"xn0": xn0[c], "WqkvT": wqkv_s})
    qkvT = [r[c]["qkvT"] for c in cores]
    r = launch("attn", build_attn, lambda c: {"qkvT": qkvT[c], "WprojT": wproj_s})
    s0T = [r[c]["s0T"] for c in cores]
    r = launch("l0ffn", build_l0ffn, lambda c: {
        "xn0": xn0[c], "W1T": w1_s, "W2T": w2_s, "W3T": w3_s,
        "s0T": s0T[c], "x": xs[c]})
    x1 = [r[c]["x1"] for c in cores]
    r = launch("ln", build_ln, lambda c: {"xin": x1[c]})
    xn1 = [r[c]["xout"] for c in cores]
    r = launch("l1attn", build_l1attn, lambda c: {
        "xn1": xn1[c], "WinT": win_s, "convw": convw, "convb": convb,
        "dtbias": dtb, "alog": alog, "dvec": dvec, "nw": nw, "WoutT": wout_s})
    s1T = [r[c]["s1T"] for c in cores]
    r = launch("l1ffn", build_l1ffn, lambda c: {
        "xn1": xn1[c], "ppg": ppg, "ppu": ppu, "ppd": ppd, "btab": btab,
        "s1T": s1T[c], "x1": x1[c]})
    out = np.concatenate([r[c]["out"].reshape(2, S, D) for c in cores], axis=0)
    return out


# version 10 (final)
# speedup vs baseline: 1.0278x; 1.0278x over previous
import numpy as np
import ml_dtypes
import concourse.bass as bass
import concourse.mybir as mybir
from concourse.bass_utils import run_bass_kernel_spmd
from contextlib import ExitStack

F32 = mybir.dt.float32
BF16 = mybir.dt.bfloat16
AF = mybir.ActivationFunctionType
ALU = mybir.AluOpType
AX = mybir.AxisListType
BF = ml_dtypes.bfloat16

N_CORES = 8
B_LOC = 2
S = 2048
D = 2048
T = B_LOC * S
LN_EPS = 1e-5

MAGIC = 3 * 2.0 ** 22


class Sem:

    def __init__(self, nc, stack, name):
        self.h = stack.enter_context(nc.semaphore(name))
        self.v = 0

    def inc(self, inst, k=1):
        inst.then_inc(self.h, k)
        self.v += k
        return self.v


def cdiv(a, b):
    return (a + b - 1) // b


def lp(nc, fn, reason="bf16 transpose data movement (no accumulation)"):
    def wrapped(eng):
        with nc.allow_low_precision(reason=reason):
            return fn(eng)
    return wrapped



ENGINES = ("sync", "scalar", "vector", "tensor", "gpsimd")


class Prog:
    def __init__(self, nc, stack, pfx, pool_size=44):
        self.nc = nc
        self.stack = stack
        self.pfx = pfx
        self.ops = {e: [] for e in ENGINES}
        self.sems = {e: Sem(nc, stack, f"{pfx}_{e}") for e in ENGINES}
        self.cnt = {e: self.sems[e].v for e in ENGINES}
        self.pool = [Sem(nc, stack, f"{pfx}_d{i}") for i in range(28)]
        self.pool_sw = [Sem(nc, stack, f"{pfx}_s{i}") for i in range(10)]
        self.pool_cp = [Sem(nc, stack, f"{pfx}_c{i}") for i in range(6)]
        self.pool_r = [Sem(nc, stack, f"{pfx}_r{i}") for i in range(18)]
        self.binding = {}

    def phase_reset(self):
        self.binding = {k: v for k, v in self.binding.items()
                        if k.startswith("R_")}

    def _sem(self, name, kind="hw"):
        if name not in self.binding:
            pool = (self.pool_r if name.startswith("R_") else
                    {"hw": self.pool, "sw": self.pool_sw, "cp": self.pool_cp}[kind])
            used = set(id(s) for s in self.binding.values())
            free = [s for s in pool if id(s) not in used]
            assert free, f"slot sem pool exhausted ({kind}:{name})"
            self.binding[name] = free[0]
        return self.binding[name]

    phase = "init"

    def op(self, engine, fn, deps=(), inc=0, dma=False, sem=None):
        if dma and inc:
            inc = 16
            assert sem is not None, "DMA op with inc needs a slot semaphore name"
        if sem is not None and inc:
            kind = ("sw" if engine == "gpsimd" else "hw") if dma else "cp"
            s = self._sem(sem, kind)
            s.v += inc
            ev = (s.h, s.v)
            rec = [fn, tuple(d for d in deps if d is not None), inc, s.h,
                   self.phase]
            if getattr(self, "_side_mode", False):
                rec.append(self._side_gidx)
                self._side_gidx += 1
            else:
                self._g = getattr(self, "_g", 0) + 1
                rec.append(-self._g)
            self.ops[engine].append(tuple(rec))
            return ev
        self.cnt[engine] += inc
        ev = (self.sems[engine].h, self.cnt[engine]) if inc else None
        rec = [fn, tuple(d for d in deps if d is not None), inc, None,
               self.phase]
        if getattr(self, "_side_mode", False):
            rec.append(self._side_gidx)
            self._side_gidx += 1
        else:
            self._g = getattr(self, "_g", 0) + 1
            rec.append(-self._g)
        self.ops[engine].append(tuple(rec))
        return ev

    def mark(self, label):
        self.marks = getattr(self, "marks", {})
        self.marks[label] = {e: len(self.ops[e]) for e in ENGINES}
        self.marks[label]["_g"] = getattr(self, "_g", 0)

    def side_begin(self):
        self._main_ops = self.ops
        self.ops = {e: [] for e in ENGINES}
        self._side_mode = True
        self._side_gidx = 0

    def side_end(self):
        side = self.ops
        self.ops = self._main_ops
        self._side_mode = False
        return side

    def splice(self, side, label_a, label_b):
        self._splices = getattr(self, "_splices", [])
        self._splices.append((side, label_a, label_b))

    def apply_splices(self):
        for side, la, lb in sorted(
                getattr(self, "_splices", []),
                key=lambda t: -self.marks[t[1]]["vector"]):
            total = max(1, sum(len(v) for v in side.values()))
            a_g, b_g = self.marks[la]["_g"], self.marks[lb]["_g"]
            for e in ENGINES:
                s = side[e]
                if not s:
                    continue
                a, b = self.marks[la][e], self.marks[lb][e]
                main = self.ops[e]
                n = len(s)
                out = main[:a]
                si = 0
                for k in range(a, b):
                    g_main = -main[k][5]
                    while si < n and a_g + (s[si][5] * (b_g - a_g)) // total <= g_main:
                        out.append(s[si][:5])
                        si += 1
                    out.append(main[k])
                while si < n:
                    out.append(s[si][:5])
                    si += 1
                out.extend(main[b:])
                self.ops[e] = out
                for lbl, m in self.marks.items():
                    if m[e] >= b:
                        m[e] += n

    def r_totals(self):
        return [(s.h, s.v) for k, s in self.binding.items()
                if k.startswith("R_") and s.v > 0]

    def barrier_all(self):
        rset = set(id(s) for k, s in self.binding.items() if k.startswith("R_"))
        evs = [(self.sems[e].h, self.cnt[e]) for e in ENGINES
               if self.cnt[e] > self.sems[e].v]
        evs += [(s.h, s.v) for s in self.pool + self.pool_sw + self.pool_cp
                if s.v > 0 and id(s) not in rset]
        return evs

    def emit(self, block):
        watermark = {e: {} for e in ENGINES}

        def make(engine):
            ops = self.ops[engine]
            mysem = self.sems[engine]
            wm = watermark[engine]

            def body(eng):
                cur = [None, 0]
                for rec in ops:
                    fn, deps, inc, semh, phase = rec[:5]
                    if phase != cur[0]:
                        if cur[0] is not None:
                            self.nc.leave_named_scope(cur[0], cur[1], False)
                        cur[1] = self.nc.enter_named_scope(phase, False)[0]
                        cur[0] = phase
                    for (h, val) in deps:
                        if wm.get(h, 0) < val:
                            eng.wait_ge(h, val)
                            wm[h] = val
                    inst = fn(eng)
                    if inc:
                        inst.then_inc(semh if semh is not None else mysem.h, inc)
                if cur[0] is not None:
                    self.nc.leave_named_scope(cur[0], cur[1], False)
            return body

        for engine in ENGINES:
            if not self.ops[engine]:
                continue
            getattr(block, engine)(make(engine))
        for e in ENGINES:
            self.sems[e].v = self.cnt[e]

    def drain(self, block):
        vals = [(self.sems[e].h, self.cnt[e]) for e in ENGINES if self.cnt[e] > 0]
        vals += [(s.h, s.v) for s in
                 self.pool + self.pool_sw + self.pool_cp + self.pool_r
                 if s.v > 0]

        def body(eng):
            for h, v in vals:
                eng.wait_ge(h, v)
        block.sync(body)



CONST_COLS = {
    "eps_ln": (0, LN_EPS),
    "inv_d": (1, 1.0 / D),
    "one": (2, 1.0),
    "qk_scale": (3, 1.0 / np.sqrt(128.0)),
    "eps_gn": (4, 1e-5),
    "inv_128": (5, 1.0 / 128.0),
    "half": (6, 0.5),
}


def emit_consts(nc, block, stack):
    ct = stack.enter_context(nc.sbuf_tensor("consts", [128, 16], F32))
    cinit = Sem(nc, stack, "cinit")

    @block.gpsimd
    def _(gpsimd):
        items = list(CONST_COLS.items())
        for name, (col, val) in items[:-1]:
            gpsimd.memset(ct[:, col:col + 1], float(val))
        _, (col, val) = items[-1]
        cinit.inc(gpsimd.memset(ct[:, col:col + 1], float(val)))

    def cc(name):
        return ct[:, CONST_COLS[name][0]:CONST_COLS[name][0] + 1]

    return ct, cc, cinit


def emit_ln(nc, stack, prog, cc, cinit_ev, xin, xout, pfx="ln", n_tok=T, d=D,
            entry=()):
    n_tiles = n_tok // 128
    NB = 3
    xt = [stack.enter_context(nc.sbuf_tensor(f"{pfx}_x{i}", [128, d], F32)) for i in range(NB)]
    sq = [stack.enter_context(nc.sbuf_tensor(f"{pfx}_sq{i}", [128, d], F32)) for i in range(NB)]
    ot = [stack.enter_context(nc.sbuf_tensor(f"{pfx}_o{i}", [128, d], BF16)) for i in range(NB)]
    st = [stack.enter_context(nc.sbuf_tensor(f"{pfx}_st{i}", [128, 8], F32)) for i in range(NB)]
    hist = {}

    def H(k):
        v = hist.get(k)
        return [v] if v is not None else list(entry)

    for i in range(n_tiles):
        b = i % NB
        lx = prog.op("sync", lambda s, i=i, b=b: s.dma_start(
            xt[b][:], xin[i * 128:(i + 1) * 128, :]),
            deps=H(("xt", b)), inc=16, dma=True, sem=f"{pfx}x{b}")
        sA = prog.op("vector", lambda v, b=b: v.reduce_sum(
            st[b][:, 0:1], xt[b][:], axis=AX.X), deps=[lx] + H(("stw", b)), inc=1)
        sB = prog.op("scalar", lambda s, b=b: s.activation(
            sq[b][:], xt[b][:], AF.Square), deps=[lx] + H(("sq", b)), inc=1)
        sC = prog.op("vector", lambda v, b=b: v.reduce_sum(
            st[b][:, 1:2], sq[b][:], axis=AX.X), deps=[sB], inc=1)
        hist[("sq", b)] = sC
        sD = prog.op("scalar", lambda s, b=b: s.activation(
            st[b][:, 3:4], st[b][:, 0:1], AF.Square, scale=cc("inv_d")),
            deps=[sA, cinit_ev], inc=1)
        sE = prog.op("vector", lambda v, b=b: v.tensor_scalar_mul(
            st[b][:, 2:3], st[b][:, 0:1], 1.0 / d), deps=[sA], inc=1)
        sF = prog.op("vector", lambda v, b=b: v.scalar_tensor_tensor(
            st[b][:, 4:5], st[b][:, 1:2], 1.0 / d, st[b][:, 3:4],
            ALU.mult, ALU.subtract), deps=[sC, sD], inc=1)
        sG = prog.op("scalar", lambda s, b=b: s.activation(
            st[b][:, 5:6], st[b][:, 4:5], AF.Sqrt, bias=cc("eps_ln")),
            deps=[sF], inc=1)
        sH = prog.op("vector", lambda v, b=b: v.reciprocal(
            st[b][:, 6:7], st[b][:, 5:6]), deps=[sG], inc=1)
        sI = prog.op("vector", lambda v, b=b: v.tensor_scalar(
            ot[b][:], xt[b][:], st[b][:, 2:3], st[b][:, 6:7],
            ALU.subtract, ALU.mult), deps=[sH] + H(("ot", b)), inc=1)
        hist[("xt", b)] = sI
        hist[("stw", b)] = sI
        so = prog.op("scalar", lambda g, i=i, b=b: g.dma_start(
            xout[i * 128:(i + 1) * 128, :], ot[b][:]), deps=[sI], inc=16,
            dma=True, sem=f"{pfx}o{b}")
        hist[("ot", b)] = so
    return prog.barrier_all()


def host_strips(wT, pad_to=None):
    K, M = wT.shape
    mtiles = cdiv(M, 128)
    if M % 128:
        wT = np.concatenate([wT, np.zeros((K, mtiles * 128 - M), wT.dtype)], axis=1)
    ktiles = K // 128
    return np.ascontiguousarray(
        wT.reshape(ktiles, 128, mtiles, 128).transpose(2, 1, 0, 3).reshape(mtiles, 128, K))


def emit_actT_load(nc, stack, prog, pfx, src, d_tiles, n_tok, entry=(),
                   tok_lo=0, buf=None):
    res = buf if buf is not None else stack.enter_context(
        nc.sbuf_tensor(f"{pfx}_actT", [128, d_tiles * n_tok], BF16))
    ev = None
    for kt in range(d_tiles):
        ev = prog.op("sync", lambda s, kt=kt: s.dma_start(
            res[:, kt * n_tok:(kt + 1) * n_tok],
            src[tok_lo:tok_lo + n_tok, kt * 128:(kt + 1) * 128], transpose=True),
            deps=(list(entry) if kt == 0 else ()), inc=16, dma=True,
            sem=f"{pfx}_at")
    return (lambda kt: res[:, kt * n_tok:(kt + 1) * n_tok]), ev


def emit_dense_mm(nc, stack, prog, pfx, *, wT, M_tot, K_tot, n_tok,
                  act_tile, act_ready, out_store, NCH=512,
                  epilogue=None, w_queue="gpsimd", n_banks=4, res=None,
                  entry=()):
    ktiles = K_tot // 128
    mtiles = cdiv(M_tot, 128)
    nch = cdiv(n_tok, NCH)
    if res is None:
        wb = [stack.enter_context(nc.sbuf_tensor(f"{pfx}_wb{i}", [128, ktiles * 128], BF16))
              for i in range(2)]
        ps = [stack.enter_context(nc.psum_tensor(f"{pfx}_ps{i}", [128, NCH], F32))
              for i in range(n_banks)]
        ot = [stack.enter_context(nc.sbuf_tensor(f"{pfx}_ot{i}", [128, NCH], BF16))
              for i in range(n_banks)]
    else:
        wb, ps, ot = res["wb"], res["ps"], res["ot"]
        n_banks = len(ps)

    guard = res.get("guard") if res else None
    wload = [None] * mtiles
    group_pe = []
    group_ep = []
    store_ev = []
    g = 0
    for mt in range(mtiles):
        msz = min(128, M_tot - mt * 128)
        par = mt % 2
        pe_dep = group_pe[(mt - 2) * nch + nch - 1] if mt >= 2 else guard
        wload[mt] = prog.op(w_queue, lambda e, mt=mt, par=par: e.dma_start(
            wb[par][:], wT[mt, :, :]),
            deps=[pe_dep] + (list(entry) if mt < 2 else []), inc=16, dma=True,
            sem=f"{pfx}_w{par}")
        for ci in range(nch):
            lo = ci * NCH
            sz = min(NCH, n_tok - lo)
            bank = g % n_banks
            deps = [wload[mt], act_ready]
            if g >= n_banks:
                deps.append(group_ep[g - n_banks])
            elif guard is not None:
                deps.append(guard)
            pe_ev = None
            for kt in range(ktiles):
                pe_ev = prog.op("tensor", lambda t, kt=kt, par=par, msz=msz, bank=bank, lo=lo, sz=sz: t.matmul(
                    ps[bank][0:msz, 0:sz], wb[par][:, kt * 128:kt * 128 + msz],
                    act_tile(kt, lo, sz),
                    start=(kt == 0), stop=(kt == ktiles - 1)),
                    deps=(deps if kt == 0 else ()),
                    inc=(1 if kt == ktiles - 1 else 0))
            group_pe.append(pe_ev)
            st_dep = store_ev[g - n_banks] if g >= n_banks else (
                res.get("guard_st") if res else None)
            if epilogue is None:
                ep = prog.op("scalar", lambda s, bank=bank, msz=msz, sz=sz: s.copy(
                    ot[bank][0:msz, 0:sz], ps[bank][0:msz, 0:sz]),
                    deps=[pe_ev, st_dep], inc=1)
            else:
                ep = epilogue(ps[bank][0:msz, 0:sz], ot[bank][0:msz, 0:sz], [pe_ev, st_dep])
            group_ep.append(ep)
            sev = out_store(mt, msz, ci, lo, sz, ot[bank][0:msz, 0:sz], ep)
            store_ev.append(sev)
            g += 1
    return group_pe, group_ep, store_ev


def emit_attn_consts(nc, stack, prog, pfx="ac"):
    ident = stack.enter_context(nc.sbuf_tensor(f"{pfx}_id", [128, 128], BF16))
    onec = stack.enter_context(nc.sbuf_tensor(f"{pfx}_onec", [128, 1], BF16))
    oner = stack.enter_context(nc.sbuf_tensor(f"{pfx}_oner", [1, 128], F32))
    onerb = stack.enter_context(nc.sbuf_tensor(f"{pfx}_onerb", [1, 128], BF16))
    masks = stack.enter_context(nc.sbuf_tensor(f"{pfx}_mask", [128, 4 * 512], BF16))
    onecf = stack.enter_context(nc.sbuf_tensor(f"{pfx}_onecf", [128, 1], F32))
    identf = stack.enter_context(nc.sbuf_tensor(f"{pfx}_idf", [128, 128], F32))
    e1 = prog.op("gpsimd", lambda g: g.memset(ident[:], 1.0), inc=1)
    prog.op("gpsimd", lambda g: g.memset(onecf[:], 1.0), inc=1)
    e1f = prog.op("gpsimd", lambda g: g.memset(identf[:], 1.0), inc=1)
    prog.op("gpsimd", lambda g: g.affine_select(
        identf[:], identf[:], [[1, 128]], ALU.is_equal, 0.0, channel_multiplier=-1),
        deps=[e1f], inc=1)
    e2 = prog.op("gpsimd", lambda g: g.affine_select(
        ident[:], ident[:], [[1, 128]], ALU.is_equal, 0.0, channel_multiplier=-1),
        deps=[e1], inc=1)
    prog.op("gpsimd", lambda g: g.memset(onec[:], 1.0), inc=1)
    prog.op("gpsimd", lambda g: g.memset(oner[:], 1.0), inc=1)
    prog.op("gpsimd", lambda g: g.memset(onerb[:], 1.0), inc=1)
    e3 = prog.op("gpsimd", lambda g: g.memset(masks[:], 1.0), inc=1)
    ev = e3
    for r in range(4):
        ev = prog.op("gpsimd", lambda g, r=r: g.affine_select(
            masks[:, r * 512:(r + 1) * 512], masks[:, r * 512:(r + 1) * 512],
            [[1, 512]], ALU.is_ge, 0.0, base=-128 * r, channel_multiplier=-1),
            deps=[e3 if r == 0 else ev], inc=1)
    ready = prog.barrier_all()
    return dict(ident=ident, identf=identf, onec=onec, oner=oner, onerb=onerb,
                onecf=onecf, masks=masks, ready=ev)


def emit_attn(nc, stack, prog, cc, cinit_ev, ac, qkvT, s0T, wprojT, entry=()):
    NH = 16
    QC = 4
    qt = [stack.enter_context(nc.sbuf_tensor(f"at_q{i}", [128, S], BF16)) for i in range(2)]
    kt_ = [stack.enter_context(nc.sbuf_tensor(f"at_k{i}", [128, S], BF16)) for i in range(2)]
    vt = [stack.enter_context(nc.sbuf_tensor(f"at_v{i}", [128, S], BF16)) for i in range(2)]
    vtm = [stack.enter_context(nc.sbuf_tensor(f"at_vtm{i}", [128, S], BF16)) for i in range(2)]
    pT = stack.enter_context(nc.sbuf_tensor("at_p", [128, 16 * 512], BF16))
    rec = stack.enter_context(nc.sbuf_tensor("at_rec", [1, 512], F32))
    recb = stack.enter_context(nc.sbuf_tensor("at_recb", [128, 512], F32))
    att = stack.enter_context(nc.sbuf_tensor("at_att", [128, NH * S], BF16))

    ps_s = [stack.enter_context(nc.psum_tensor(f"at_pss{i}", [128, 512], F32)) for i in range(2)]
    ps_av = [stack.enter_context(nc.psum_tensor(f"at_psav{i}", [128, 512], F32)) for i in range(2)]
    ps_db = stack.enter_context(nc.psum_tensor("at_psdb", [128, 512], F32))
    ps_vt = stack.enter_context(nc.psum_tensor("at_psvt", [128, 1024], BF16))

    ident, onec, oner, masks = ac["ident"], ac["onec"], ac["oner"], ac["masks"]
    acr = ac["ready"]

    hist = {}
    last_exp = {}
    last_av = {}
    last_av_pe = None
    vtm_evac = {}
    last_recip = None
    last_bevac = None
    att_ev_final = None
    proj_pe = None
    proj_res = None

    for b in range(B_LOC):
        for h in range(NH):
            gh = b * NH + h
            par = gh % 2
            prev_done = hist.get(("head_done", gh - 2))
            extra_entry = list(entry) if gh < 2 else []
            lq = prog.op("sync", lambda s, h=h, b=b, par=par: s.dma_start(
                qt[par][:], qkvT[h * 128:(h + 1) * 128, b * S:(b + 1) * S]),
                deps=[prev_done] + extra_entry, inc=16, dma=True, sem=f"at_q{par}")
            lk = prog.op("sync", lambda s, h=h, b=b, par=par: s.dma_start(
                kt_[par][:], qkvT[2048 + h * 128:2048 + (h + 1) * 128, b * S:(b + 1) * S]),
                deps=[prev_done] + extra_entry, inc=16, dma=True, sem=f"at_k{par}")
            lv = prog.op("sync", lambda s, h=h, b=b, par=par: s.dma_start(
                vt[par][:], qkvT[4096 + h * 128:4096 + (h + 1) * 128, b * S:(b + 1) * S]),
                deps=[prev_done] + extra_entry, inc=16, dma=True, sem=f"at_v{par}")
            vtm_ready = []
            for half in range(2):
                pe_ev = None
                for j in range(8):
                    ktile = half * 8 + j
                    deps = ()
                    if j == 0:
                        deps = [lv, acr, vtm_evac.get(half),
                                hist.get(("vtm_used", gh - 2))]
                        if half == 1:
                            deps.append(vtm_ready[0])
                    pe_ev = prog.op("tensor", lp(nc, lambda t, par=par, j=j, ktile=ktile: t.transpose(
                        ps_vt[:, j * 128:(j + 1) * 128],
                        vt[par][:, ktile * 128:(ktile + 1) * 128], ident[:])),
                        deps=deps, inc=(1 if j == 7 else 0))
                ev = prog.op("scalar", lambda s, par=par, half=half: s.copy(
                    vtm[par][:, half * 1024:(half + 1) * 1024], ps_vt[:]),
                    deps=[pe_ev, hist.get(("vtm_used", gh - 2))], inc=1)
                vtm_evac[half] = ev
                vtm_ready.append(ev)

            for qc in range(QC):
                ktmax = 4 * (qc + 1)
                exp_evs = []
                for ktk in range(ktmax):
                    bank = ktk % 2
                    deps = [lq, lk, last_exp.get(bank)]
                    pe_ev = prog.op("tensor", lambda t, par=par, ktk=ktk, qc=qc, bank=bank: t.matmul(
                        ps_s[bank][:], kt_[par][:, ktk * 128:(ktk + 1) * 128],
                        qt[par][:, qc * 512:(qc + 1) * 512], start=True, stop=True),
                        deps=deps, inc=1)
                    eev = prog.op("scalar", lambda s, ktk=ktk, bank=bank: s.activation(
                        pT[:, ktk * 512:(ktk + 1) * 512], ps_s[bank][:],
                        AF.Exp, scale=cc("qk_scale")),
                        deps=[pe_ev, cinit_ev, last_av_pe], inc=1)
                    last_exp[bank] = eev
                    if ktk >= 4 * qc:
                        r = ktk - 4 * qc
                        eev = prog.op("vector", lambda v, ktk=ktk, r=r: v.tensor_mul(
                            pT[:, ktk * 512:(ktk + 1) * 512],
                            pT[:, ktk * 512:(ktk + 1) * 512],
                            masks[:, r * 512:(r + 1) * 512]),
                            deps=[eev, acr], inc=1)
                    exp_evs.append(eev)
                d_pe = None
                for ktk in range(ktmax):
                    deps = [exp_evs[ktk]]
                    if ktk == 0:
                        deps += [last_recip, last_bevac, acr]
                    d_pe = prog.op("tensor", lambda t, ktk=ktk, ktmax=ktmax: t.matmul(
                        ps_db[0:1, :], onec[:], pT[:, ktk * 512:(ktk + 1) * 512],
                        start=(ktk == 0), stop=(ktk == ktmax - 1)),
                        deps=deps, inc=(1 if ktk == ktmax - 1 else 0))
                last_recip = prog.op("vector", lambda v: v.reciprocal(rec[:], ps_db[0:1, :]),
                                     deps=[d_pe], inc=1)
                b_pe = prog.op("tensor", lambda t: t.matmul(
                    ps_db[:], oner[:], rec[:], start=True, stop=True),
                    deps=[last_recip, last_bevac], inc=1)
                last_bevac = prog.op("scalar", lambda s: s.copy(recb[:], ps_db[:]),
                                     deps=[b_pe], inc=1)
                avbank = qc % 2
                av_pe = None
                for ktk in range(ktmax):
                    deps = [vtm_ready[0], vtm_ready[1], exp_evs[ktk]]
                    if ktk == 0:
                        deps.append(last_av.get(avbank))
                    av_pe = prog.op("tensor", lambda t, par=par, ktk=ktk, avbank=avbank: t.matmul(
                        ps_av[avbank][:], vtm[par][:, ktk * 128:(ktk + 1) * 128],
                        pT[:, ktk * 512:(ktk + 1) * 512],
                        start=(ktk == 0), stop=(ktk == ktmax - 1)),
                        deps=deps, inc=(1 if ktk == ktmax - 1 else 0))
                last_av_pe = av_pe
                deps = [av_pe, last_bevac]
                if proj_pe is not None:
                    deps.append(proj_pe[15 * 4 + qc])
                am = prog.op("vector", lambda v, h=h, qc=qc, avbank=avbank: v.tensor_tensor(
                    att[:, h * S + qc * 512: h * S + (qc + 1) * 512],
                    ps_av[avbank][:], recb[:], ALU.mult),
                    deps=deps, inc=1)
                last_av[avbank] = am
                att_ev_final = am
            hist[("head_done", gh)] = att_ev_final
            hist[("vtm_used", gh)] = last_av_pe

        if proj_res is None:
            proj_res = {
                "wb": [stack.enter_context(nc.sbuf_tensor(f"pj_wb{i}", [128, 16 * 128], BF16))
                       for i in range(2)],
                "ps": [stack.enter_context(nc.psum_tensor(f"pj_ps{i}", [128, 512], F32))
                       for i in range(2)],
                "ot": [stack.enter_context(nc.sbuf_tensor(f"pj_ot{i}", [128, 512], BF16))
                       for i in range(2)],
            }
        proj_res["guard"] = att_ev_final

        def out_store(mt, msz, ci, lo, sz, src, dep, b=b):
            g = mt * 4 + ci
            return prog.op("sync", lambda s: s.dma_start(
                s0T[mt * 128:mt * 128 + msz, b * S + lo:b * S + lo + sz], src),
                deps=[dep], inc=16, dma=True, sem=f"pj_st{g % 2}")

        proj_pe, proj_ep, proj_st = emit_dense_mm(
            nc, stack, prog, f"proj{b}", wT=wprojT, M_tot=D, K_tot=D, n_tok=S,
            act_tile=lambda kt, lo, sz: att[:, kt * S + lo:lo + kt * S + sz],
            act_ready=att_ev_final, out_store=out_store, res=proj_res,
            entry=entry)
        proj_res["guard_st"] = proj_st[-1]



def emit_gamma(nc, stack, prog, ac, wTs, entry=()):
    nw = len(wTs)
    maxC = max(c for _, _, c in wTs)
    parts = stack.enter_context(nc.sbuf_tensor("gm_parts", [128, 64], F32))
    tot = stack.enter_context(nc.sbuf_tensor("gm_tot", [128, nw], F32))
    thr = stack.enter_context(nc.sbuf_tensor("gm_thr", [1, nw], F32))
    thrb = stack.enter_context(nc.sbuf_tensor("gm_thrb", [128, 2 * nw], F32))

    onecf, onerf = ac["onecf"], ac["oner"]
    acr = ac["ready"]
    ready = None
    with ExitStack() as local:
        buf = [local.enter_context(nc.sbuf_tensor(f"gm_b{i}", [128, maxC], F32))
               for i in range(3)]
        ps_g = local.enter_context(nc.psum_tensor("gm_ps", [128, 16], F32))
        bufap = [b[:] for b in buf]
        last_red = {}
        for w, (ap, nrb, C) in enumerate(wTs):
            rev = None
            for rb in range(nrb):
                par = rb % 3
                dep = [last_red.get(par)] if par in last_red else list(entry)
                src = ap[rb, :, 0:C] if ap.ndim == 3 else ap[rb * 128:(rb + 1) * 128, 0:C]
                dst = bufap[par][:, 0:C] if C != maxC else bufap[par]
                lev = prog.op("scalar", lambda s, src=src, dst=dst: s.dma_start(dst, src),
                              deps=dep, inc=16, dma=True, sem=f"gm_l{par}")
                pdst = parts[:, rb:rb + 1]
                rev = prog.op("vector", lambda v, dst=dst, pdst=pdst: v.reduce_sum(
                    pdst, dst, axis=AX.X, apply_absolute_value=True),
                    deps=[lev], inc=1)
                last_red[par] = rev
            sev = prog.op("vector", lambda v, nrb=nrb, w=w: v.reduce_sum(
                tot[:, w:w + 1], parts[:, 0:nrb], axis=AX.X),
                deps=[rev], inc=1)
            pg1 = ps_g[0:1, w:w + 1]
            pg2 = ps_g[0:128, 8 + w:9 + w]
            mm1 = prog.op("tensor", lambda t, pg1=pg1, w=w: t.matmul(
                pg1, onecf[:], tot[:, w:w + 1], start=True, stop=True),
                deps=[sev, acr], inc=1)
            N = nrb * 128 * C
            tev = prog.op("vector", lambda v, pg1=pg1, w=w, N=N: v.tensor_scalar(
                thr[:, w:w + 1], pg1, 0.5 / N, 0.5e-8, ALU.mult, ALU.add),
                deps=[mm1], inc=1)
            mm2 = prog.op("tensor", lambda t, pg2=pg2, w=w: t.matmul(
                pg2, onerf[:], thr[:, w:w + 1], start=True, stop=True),
                deps=[tev], inc=1)
            cev = prog.op("vector", lambda v, pg2=pg2, w=w: v.tensor_copy(
                thrb[:, 2 * w:2 * w + 1], pg2),
                deps=[mm2], inc=1)
            ready = prog.op("vector", lambda v, w=w: v.tensor_scalar_mul(
                thrb[:, 2 * w + 1:2 * w + 2], thrb[:, 2 * w:2 * w + 1], -1.0),
                deps=[cev], inc=1)
    return thrb, ready


def emit_quant(nc, stack, prog, wf32_ap, sgn_ap, msk_ap, wq_ap, thrb, wi, deps):
    e1 = prog.op("vector", lambda v: v.tensor_scalar(
        sgn_ap, wf32_ap, thrb[:, 2 * wi:2 * wi + 1], None, ALU.is_gt),
        deps=deps, inc=1)
    e2 = prog.op("vector", lambda v: v.tensor_scalar(
        msk_ap, wf32_ap, thrb[:, 2 * wi + 1:2 * wi + 2], None, ALU.is_lt),
        deps=[e1] + list(deps), inc=1)
    e3 = prog.op("vector", lambda v: v.tensor_sub(wq_ap, sgn_ap, msk_ap),
                 deps=[e1, e2], inc=1)
    return e3


def emit_bitlinear_hid(nc, stack, prog, ac, cc, cinit_ev, thrb, thr_ev,
                       w1T, w2T, xn0, hidT, entry=(), tok_lo=0, n_tok=T,
                       pfx="bl", bufs=None):
    H = 5632
    ktiles = 16
    mtiles = H // 128
    nch = n_tok // 512
    if bufs is None:
        bufs = {}
        bufs["actT"] = stack.enter_context(
            nc.sbuf_tensor(f"{pfx}_actT", [128, (D // 128) * n_tok], BF16))
        bufs["wf"] = [stack.enter_context(nc.sbuf_tensor(f"{pfx}_wf{i}", [128, 2048], F32)) for i in range(2)]
        bufs["sgn"] = [stack.enter_context(nc.sbuf_tensor(f"{pfx}_sg{i}", [128, 2048], BF16)) for i in range(2)]
        bufs["msk"] = [stack.enter_context(nc.sbuf_tensor(f"{pfx}_mk{i}", [128, 2048], BF16)) for i in range(2)]
        bufs["wq1"] = [stack.enter_context(nc.sbuf_tensor(f"{pfx}_q1{i}", [128, 2048], BF16)) for i in range(2)]
        bufs["wq2"] = [stack.enter_context(nc.sbuf_tensor(f"{pfx}_q2{i}", [128, 2048], BF16)) for i in range(2)]
        bufs["sig"] = [stack.enter_context(nc.sbuf_tensor(f"{pfx}_si{i}", [128, 512], BF16)) for i in range(4)]
        bufs["ot"] = [stack.enter_context(nc.sbuf_tensor(f"{pfx}_ot{i}", [128, 512], BF16)) for i in range(4)]
        bufs["psA"] = [stack.enter_context(nc.psum_tensor(f"{pfx}_psA{i}", [128, 512], F32)) for i in range(2)]
        bufs["psB"] = [stack.enter_context(nc.psum_tensor(f"{pfx}_psB{i}", [128, 512], F32)) for i in range(2)]
    act, act_ev = emit_actT_load(nc, stack, prog, f"{pfx}{tok_lo // S}", xn0,
                                 D // 128, n_tok, entry=entry, tok_lo=tok_lo,
                                 buf=bufs["actT"])
    wf, sgn, msk = bufs["wf"], bufs["sgn"], bufs["msk"]
    wq1, wq2, sig, ot = bufs["wq1"], bufs["wq2"], bufs["sig"], bufs["ot"]
    psA, psB = bufs["psA"], bufs["psB"]

    qev = {}
    mm_last = {}
    epA_hist = [None, None]
    epB_hist = [None, None]
    sig_hist = [None] * 4
    st_hist = [None] * 4
    for mt in range(mtiles):
        par = mt % 2
        for which, wT, wq in ((0, w1T, wq1), (1, w2T, wq2)):
            wpar = (2 * mt + which) % 2
            ldep = [qev[(0, wpar)] if (0, wpar) in qev else None,
                    qev[(1, wpar)] if (1, wpar) in qev else None]
            ldep = [d for d in ldep if d] or list(entry)
            lev = prog.op("sync", lambda s, wT=wT, mt=mt, wpar=wpar: s.dma_start(
                wf[wpar][:], wT[mt, :, :]),
                deps=ldep + [mm_last.get((which, par))], inc=16, dma=True,
                sem=f"{pfx}_w{wpar}")
            qev[(which, wpar)] = emit_quant(
                nc, stack, prog, wf[wpar][:], sgn[wpar][:], msk[wpar][:],
                wq[par][:], thrb, which,
                [lev, thr_ev, mm_last.get((which, par))])
        for ci in range(nch):
            lo = ci * 512
            bank = ci % 2
            slot = ci % 4
            depsA = [qev[(0, (2 * mt) % 2)], act_ev]
            depsB = [qev[(1, (2 * mt + 1) % 2)], act_ev]
            if epA_hist[bank] is not None:
                depsA.append(epA_hist[bank])
            if epB_hist[bank] is not None:
                depsB.append(epB_hist[bank])
            peA = None
            peB = None
            for kt in range(ktiles):
                peA = prog.op("tensor", lambda t, kt=kt, par=par, bank=bank, lo=lo: t.matmul(
                    psA[bank][:], wq1[par][:, kt * 128:(kt + 1) * 128],
                    act(kt)[:, lo:lo + 512], start=(kt == 0), stop=(kt == ktiles - 1)),
                    deps=(depsA if kt == 0 else ()), inc=(1 if kt == ktiles - 1 else 0))
            for kt in range(ktiles):
                peB = prog.op("tensor", lambda t, kt=kt, par=par, bank=bank, lo=lo: t.matmul(
                    psB[bank][:], wq2[par][:, kt * 128:(kt + 1) * 128],
                    act(kt)[:, lo:lo + 512], start=(kt == 0), stop=(kt == ktiles - 1)),
                    deps=(depsB if kt == 0 else ()), inc=(1 if kt == ktiles - 1 else 0))
            mm_last[(0, par)] = peB
            mm_last[(1, par)] = peB
            sev = prog.op("scalar", lambda s, bank=bank, slot=slot: s.activation(
                sig[slot][:], psA[bank][:], AF.Sigmoid),
                deps=[peA, sig_hist[slot]], inc=1)
            epA_hist[bank] = sev
            hev = prog.op("vector", lambda v, bank=bank, slot=slot: v.tensor_tensor(
                ot[slot][:], psB[bank][:], sig[slot][:], ALU.mult),
                deps=[peB, sev, st_hist[slot]], inc=1)
            sig_hist[slot] = hev
            epB_hist[bank] = hev
            stv = prog.op("scalar", lambda g, mt=mt, lo=lo, slot=slot: g.dma_start(
                hidT[mt * 128:(mt + 1) * 128, tok_lo + lo:tok_lo + lo + 512],
                ot[slot][:]),
                deps=[hev], inc=16, dma=True, sem=f"{pfx}_st{slot}")
            st_hist[slot] = stv
    return prog.barrier_all(), bufs


def emit_w3_quant(nc, stack, prog, thrb, thr_ev, w3T, w3q, entry=()):
    H = 5632
    ktiles = H // 128
    wf = [stack.enter_context(nc.sbuf_tensor(f"w3_wf{i}", [128, ktiles * 128], F32)) for i in range(2)]
    sgn = [stack.enter_context(nc.sbuf_tensor(f"w3_sg{i}", [128, ktiles * 128], BF16)) for i in range(2)]
    msk = [stack.enter_context(nc.sbuf_tensor(f"w3_mk{i}", [128, ktiles * 128], BF16)) for i in range(2)]
    wq = [stack.enter_context(nc.sbuf_tensor(f"w3_q{i}", [128, ktiles * 128], BF16)) for i in range(2)]
    qhist = {}
    shist = {}
    for mt in range(16):
        par = mt % 2
        lev = prog.op("sync", lambda s, mt=mt, par=par: s.dma_start(
            wf[par][:], w3T[mt, :, :]),
            deps=[qhist.get(par)] if par in qhist else list(entry), inc=16,
            dma=True, sem=f"w3_l{par}")
        qv = emit_quant(nc, stack, prog, wf[par][:], sgn[par][:], msk[par][:],
                        wq[par][:], thrb, 2, [lev, thr_ev, shist.get(par)])
        qhist[par] = qv
        sv = prog.op("scalar", lambda g, par=par, mt=mt: g.dma_start(
            w3q[mt, :, :], wq[par][:]),
            deps=[qv], inc=16, dma=True, sem=f"w3_s{par}")
        shist[par] = sv
    return prog.barrier_all()


def emit_w3_mm(nc, stack, prog, cc, cinit_ev, w3q, hidT, s0T, sgT, entry=()):
    H = 5632
    ktiles = H // 128
    hb = stack.enter_context(nc.sbuf_tensor("w3m_hb", [128, ktiles * 1024], BF16))
    wqb = [stack.enter_context(nc.sbuf_tensor(f"w3m_wq{i}", [128, ktiles * 128], BF16)) for i in range(2)]
    s0b = [stack.enter_context(nc.sbuf_tensor(f"w3m_s0{i}", [128, 512], BF16)) for i in range(4)]
    gb = [stack.enter_context(nc.sbuf_tensor(f"w3m_g{i}", [128, 512], BF16)) for i in range(4)]
    ot = [stack.enter_context(nc.sbuf_tensor(f"w3m_ot{i}", [128, 512], BF16)) for i in range(4)]
    ps = [stack.enter_context(nc.psum_tensor(f"w3m_ps{i}", [128, 512], F32)) for i in range(4)]

    hload = [None] * ktiles
    wload = {}
    g = 0
    ep_hist = {}
    st_hist = {}
    sg_hist = {}
    s0_hist = {}
    mm_hist = {}
    hfree = None
    for c4 in range(4):
        tlo = c4 * 1024
        for kt in range(ktiles):
            hload[kt] = prog.op("scalar", lambda s, kt=kt, tlo=tlo: s.dma_start(
                hb[:, kt * 1024:(kt + 1) * 1024],
                hidT[kt * 128:(kt + 1) * 128, tlo:tlo + 1024]),
                deps=[hfree] if hfree else list(entry), inc=16, dma=True,
                sem="w3m_h")
        hready = hload[-1]
        for mt in range(16):
            par = mt % 2
            wload[mt] = prog.op("sync", lambda s, mt=mt, par=par: s.dma_start(
                wqb[par][:], w3q[mt, :, :]),
                deps=[mm_hist.get(par)] if par in mm_hist else list(entry),
                inc=16, dma=True, sem=f"w3m_w{par}")
            for half in range(2):
                lo = tlo + half * 512
                bank = g % 4
                slot = g % 4
                deps = [wload[mt], hready]
                if bank in ep_hist:
                    deps.append(ep_hist[bank])
                pe = None
                for kt in range(ktiles):
                    pe = prog.op("tensor", lambda t, kt=kt, par=par, bank=bank, lo=lo: t.matmul(
                        ps[bank][:], wqb[par][:, kt * 128:(kt + 1) * 128],
                        hb[:, kt * 1024 + (lo % 1024):kt * 1024 + (lo % 1024) + 512],
                        start=(kt == 0), stop=(kt == ktiles - 1)),
                        deps=(deps if kt == 0 else ()), inc=(1 if kt == ktiles - 1 else 0))
                mm_hist[par] = pe
                sl = prog.op("sync", lambda gg, mt=mt, lo=lo, slot=slot: gg.dma_start(
                    s0b[slot][:], s0T[mt * 128:(mt + 1) * 128, lo:lo + 512]),
                    deps=[s0_hist.get(slot)] if slot in s0_hist else list(entry),
                    inc=16, dma=True, sem=f"w3m_s0{slot}")
                gv = prog.op("scalar", lambda s, bank=bank, slot=slot: s.activation(
                    gb[slot][:], ps[bank][:], AF.Silu),
                    deps=[pe, sg_hist.get(slot)], inc=1)
                ep_hist[bank] = gv
                hv = prog.op("vector", lambda v, slot=slot: v.tensor_mul(
                    ot[slot][:], gb[slot][:], s0b[slot][:]),
                    deps=[gv, sl, st_hist.get(slot)], inc=1)
                sg_hist[slot] = hv
                s0_hist[slot] = hv
                stv = prog.op("scalar", lambda gg, mt=mt, lo=lo, slot=slot: gg.dma_start(
                    sgT[mt * 128:(mt + 1) * 128, lo:lo + 512], ot[slot][:]),
                    deps=[hv], inc=16, dma=True, sem=f"w3m_st{slot}")
                st_hist[slot] = stv
                g += 1
        hfree = pe
    return prog.barrier_all()


def emit_residual(nc, stack, prog, x_in, sgT, x1_out, entry=(), n_tok=T, d=D,
                  pfx="rs"):
    n_tiles = n_tok // 128
    xb = [stack.enter_context(nc.sbuf_tensor(f"{pfx}_x{i}", [128, d], F32)) for i in range(3)]
    sb = [stack.enter_context(nc.sbuf_tensor(f"{pfx}_s{i}", [128, d], BF16)) for i in range(3)]
    ob = [stack.enter_context(nc.sbuf_tensor(f"{pfx}_o{i}", [128, d], F32)) for i in range(3)]
    add_h = {}
    st_h = {}
    for i in range(n_tiles):
        b = i % 3
        dep0 = [add_h.get(b)] if b in add_h else list(entry)
        lx = prog.op("sync", lambda s, i=i, b=b: s.dma_start(
            xb[b][:], x_in[i * 128:(i + 1) * 128, :]), deps=dep0, inc=16, dma=True,
            sem=f"{pfx}x{b}")
        ls = prog.op("sync", lambda s, i=i, b=b: s.dma_start(
            sb[b][:], sgT[0:d, i * 128:(i + 1) * 128], transpose=True),
            deps=dep0, inc=16, dma=True, sem=f"rs_s{b}")
        av = prog.op("vector", lambda v, b=b: v.tensor_add(ob[b][:], xb[b][:], sb[b][:]),
                     deps=[lx, ls, st_h.get(b)], inc=1)
        add_h[b] = av
        sv = prog.op("scalar", lambda g, i=i, b=b: g.dma_start(
            x1_out[i * 128:(i + 1) * 128, :], ob[b][:]), deps=[av], inc=16,
            dma=True, sem=f"{pfx}o{b}")
        st_h[b] = sv
    return prog.barrier_all()



MB = dict(D_IN=4096, D_XB=1024, D_ST=128, HD=64, NH=64, NG=32, XG=8,
          CH=128, NCK=16, CONV=4, CDIM=6144, DPROJ=10304)


def emit_inproj(nc, stack, prog, zT, xbcT, dtT, xn1, winT, entry=(),
                tok_lo=0, n_tok=T, pfx="ip", bufs=None):
    nch = n_tok // 512
    ktiles = D // 128
    if bufs is None:
        bufs = {}
        bufs["actT"] = stack.enter_context(
            nc.sbuf_tensor(f"{pfx}_actT", [128, (D // 128) * n_tok], BF16))
        bufs["dts"] = [stack.enter_context(nc.sbuf_tensor(f"{pfx}_dt{i}", [64, 512], F32)) for i in range(2)]
        bufs["wb"] = [stack.enter_context(nc.sbuf_tensor(f"{pfx}_wb{i}", [128, 16 * 128], BF16))
                      for i in range(2)]
        bufs["ps"] = [stack.enter_context(nc.psum_tensor(f"{pfx}_ps{i}", [128, 512], F32)) for i in range(4)]
        bufs["ot"] = [stack.enter_context(nc.sbuf_tensor(f"{pfx}_ot{i}", [128, 512], BF16)) for i in range(4)]
    act, act_ev = emit_actT_load(nc, stack, prog, f"{pfx}{tok_lo // S}", xn1,
                                 D // 128, n_tok, entry=entry, tok_lo=tok_lo,
                                 buf=bufs["actT"])
    dts = bufs["dts"]
    dt_state = {}

    def out_store(mt, msz, ci, lo, sz, src, dep):
        lo = lo + tok_lo
        if mt < 32:
            dst = zT[mt * 128:mt * 128 + msz, lo:lo + sz]
        elif mt < 80:
            dst = xbcT[(mt - 32) * 128:(mt - 32) * 128 + msz, lo:lo + sz]
        else:
            dst = dtT[0:64, lo:lo + sz]
        return prog.op("scalar", lambda g: g.dma_start(dst, src), deps=[dep],
                       inc=16, dma=True, sem=f"{pfx}_st{out_store.slot}")

    def epilogue(ps_ap, ot_ap, deps, _mt=[0]):
        mt, ci = epilogue.pos
        if mt == 80:
            slot = ci % 2
            ev = prog.op("scalar", lambda s, slot=slot: s.copy(dts[slot][:], ps_ap),
                         deps=list(deps) + [dt_state.get(slot)], inc=1)
            dt_state[slot] = ev
            epilogue.last_src = dts[slot][:]
            return ev
        ev = prog.op("scalar", lambda s: s.copy(ot_ap, ps_ap), deps=deps, inc=1)
        epilogue.last_src = None
        return ev

    mtiles = cdiv(MB["DPROJ"], 128)
    wb, ps, ot = bufs["wb"], bufs["ps"], bufs["ot"]
    group_pe, group_ep, store_ev = [], [], []
    g = 0
    for mt in range(mtiles):
        msz = min(128, MB["DPROJ"] - mt * 128)
        par = mt % 2
        pe_dep = group_pe[(mt - 2) * nch + nch - 1] if mt >= 2 else None
        wl = prog.op("gpsimd", lambda e, mt=mt, par=par: e.dma_start(
            wb[par][:], winT[mt, :, :]), deps=[pe_dep] if pe_dep else list(entry),
            inc=16, dma=True, sem=f"{pfx}_w{par}")
        for ci in range(nch):
            lo = ci * 512
            bank = g % 4
            deps = [wl, act_ev]
            if g >= 4:
                deps.append(group_ep[g - 4])
            pe = None
            for kt in range(ktiles):
                pe = prog.op("tensor", lambda t, kt=kt, par=par, msz=msz, bank=bank, lo=lo: t.matmul(
                    ps[bank][0:msz, :], wb[par][:, kt * 128:kt * 128 + msz],
                    act(kt)[:, lo:lo + 512],
                    start=(kt == 0), stop=(kt == ktiles - 1)),
                    deps=(deps if kt == 0 else ()), inc=(1 if kt == ktiles - 1 else 0))
            group_pe.append(pe)
            st_dep = store_ev[g - 4] if g >= 4 else None
            epilogue.pos = (mt, ci)
            ep = epilogue(ps[bank][0:msz, :], ot[bank][0:msz, :], [pe, st_dep])
            group_ep.append(ep)
            src = epilogue.last_src if epilogue.last_src is not None else ot[bank][0:msz, :]
            out_store.slot = g % 4
            sev = out_store(mt, msz, ci, lo, 512, src, ep)
            store_ev.append(sev)
            g += 1
    return prog.barrier_all(), bufs


def emit_conv(nc, stack, prog, ac, xbcT, convT, convw, convb, entry=()):
    cw = stack.enter_context(nc.sbuf_tensor("cv_w", [128, 48 * 4], F32))
    cb = stack.enter_context(nc.sbuf_tensor("cv_b", [128, 48], F32))
    xb = [stack.enter_context(nc.sbuf_tensor(f"cv_x{i}", [128, S + 3], BF16)) for i in range(3)]
    acc = [stack.enter_context(nc.sbuf_tensor(f"cv_a{i}", [128, S], F32)) for i in range(3)]
    ob = [stack.enter_context(nc.sbuf_tensor(f"cv_o{i}", [128, S], BF16)) for i in range(3)]
    lw = prog.op("sync", lambda s: s.dma_start(cw[:], convw[:, :]),
                 deps=list(entry), inc=16, dma=True, sem="cv_w")
    lb = prog.op("sync", lambda s: s.dma_start(cb[:], convb[:, :]),
                 deps=list(entry), inc=16, dma=True, sem="cv_b")
    hist = {}
    for it in range(96):
        ct_, b = divmod(it, 2)
        sl = it % 3
        dep0 = [hist.get(("st", sl))] if ("st", sl) in hist else list(entry)
        z0 = prog.op("gpsimd", lambda g, sl=sl: g.memset(xb[sl][:, 0:3], 0.0),
                     deps=dep0 + [hist.get(("a4", sl))], inc=1)
        lx = prog.op("sync", lambda s, ct_=ct_, b=b, sl=sl: s.dma_start(
            xb[sl][:, 3:S + 3],
            xbcT[ct_ * 128:(ct_ + 1) * 128, b * S:(b + 1) * S]),
            deps=dep0 + [hist.get(("a4", sl))], inc=16, dma=True, sem=f"cv_x{sl}")
        a1 = prog.op("vector", lambda v, ct_=ct_, sl=sl: v.tensor_scalar(
            acc[sl][:], xb[sl][:, 0:S], cw[:, ct_ * 4:ct_ * 4 + 1],
            cb[:, ct_:ct_ + 1], ALU.mult, ALU.add),
            deps=[lx, lw, lb, z0, hist.get(("sil", sl))], inc=1)
        a2 = prog.op("vector", lambda v, ct_=ct_, sl=sl: v.scalar_tensor_tensor(
            acc[sl][:], xb[sl][:, 1:S + 1], cw[:, ct_ * 4 + 1:ct_ * 4 + 2],
            acc[sl][:], ALU.mult, ALU.add), deps=[a1], inc=1)
        a3 = prog.op("vector", lambda v, ct_=ct_, sl=sl: v.scalar_tensor_tensor(
            acc[sl][:], xb[sl][:, 2:S + 2], cw[:, ct_ * 4 + 2:ct_ * 4 + 3],
            acc[sl][:], ALU.mult, ALU.add), deps=[a2], inc=1)
        a4 = prog.op("vector", lambda v, ct_=ct_, sl=sl: v.scalar_tensor_tensor(
            acc[sl][:], xb[sl][:, 3:S + 3], cw[:, ct_ * 4 + 3:ct_ * 4 + 4],
            acc[sl][:], ALU.mult, ALU.add), deps=[a3], inc=1)
        hist[("a4", sl)] = a4
        sil = prog.op("scalar", lambda s, sl=sl: s.activation(
            ob[sl][:], acc[sl][:], AF.Silu), deps=[a4, hist.get(("st", sl))], inc=1)
        hist[("sil", sl)] = sil
        st = prog.op("scalar", lambda g, ct_=ct_, b=b, sl=sl: g.dma_start(
            convT[ct_ * 128:(ct_ + 1) * 128, b * S:(b + 1) * S], ob[sl][:]),
            deps=[sil], inc=16, dma=True, sem=f"cv_o{sl}")
        hist[("st", sl)] = st
    return prog.barrier_all()


def emit_scan(nc, stack, prog, ac, cc, cinit_ev, convT, dtT, yTd,
              dt_bias, A_log, Dvec, entry=(), nck=None, nb=B_LOC, stage=9,
              tscr=None):
    NH, HD, NG, XG, CH = MB["NH"], MB["HD"], MB["NG"], MB["XG"], MB["CH"]
    ident = ac["ident"]
    identf = ac["identf"]

    oh = stack.enter_context(nc.sbuf_tensor("sc_oh", [64, 64 * 128], BF16))
    e1 = prog.op("gpsimd", lambda g: g.memset(oh[:], 1.0), deps=list(entry), inc=1)
    ohev = prog.op("gpsimd", lambda g: g.affine_select(
        oh[:], oh[:], [[1, 64], [0, 128]], ALU.is_equal, 0.0, channel_multiplier=-1),
        deps=[e1], inc=1)
    mbias = stack.enter_context(nc.sbuf_tensor("sc_mb", [128, 128], F32))
    e2 = prog.op("gpsimd", lambda g: g.memset(mbias[:], 0.0), deps=list(entry), inc=1)
    mbev = prog.op("gpsimd", lambda g: g.affine_select(
        mbias[:], mbias[:], [[1, 128]], ALU.is_ge, -1e30, channel_multiplier=-1),
        deps=[e2], inc=1)
    dtb = stack.enter_context(nc.sbuf_tensor("sc_dtb", [64, 1], F32))
    alog = stack.enter_context(nc.sbuf_tensor("sc_al", [64, 1], F32))
    negA = stack.enter_context(nc.sbuf_tensor("sc_na", [64, 1], F32))
    drow = stack.enter_context(nc.sbuf_tensor("sc_dr", [1, 64], F32))
    Db = stack.enter_context(nc.sbuf_tensor("sc_db", [128, 64], F32))
    l1 = prog.op("sync", lambda s: s.dma_start(dtb[:], dt_bias[:, :]),
                 deps=list(entry), inc=16, dma=True, sem="sc_c1")
    l2 = prog.op("sync", lambda s: s.dma_start(alog[:], A_log[:, :]),
                 deps=list(entry), inc=16, dma=True, sem="sc_c2")
    l3 = prog.op("sync", lambda s: s.dma_start(drow[:], Dvec[:, :]),
                 deps=list(entry), inc=16, dma=True, sem="sc_c3")
    e3 = prog.op("scalar", lambda s: s.activation(negA[:], alog[:], AF.Exp),
                 deps=[l2], inc=1)
    negA_ev = prog.op("vector", lambda v: v.tensor_scalar_mul(negA[:], negA[:], -1.0),
                      deps=[e3], inc=1)

    xs_p = [stack.enter_context(nc.sbuf_tensor(f"sc_xs{i}", [128, XG * CH], BF16))
            for i in range(2)]
    bs_p = [stack.enter_context(nc.sbuf_tensor(f"sc_bs{i}", [128, XG * CH], BF16))
            for i in range(2)]
    cs_p = [stack.enter_context(nc.sbuf_tensor(f"sc_cs{i}", [128, NG * CH], BF16))
            for i in range(2)]
    dts_p = [stack.enter_context(nc.sbuf_tensor(f"sc_dts{i}", [64, CH], F32))
             for i in range(2)]
    dtsp = stack.enter_context(nc.sbuf_tensor("sc_dtsp", [64, CH], F32))
    dA = stack.enter_context(nc.sbuf_tensor("sc_dA", [64, CH], F32))
    csum_ = stack.enter_context(nc.sbuf_tensor("sc_csum", [64, CH], F32))
    ncs = stack.enter_context(nc.sbuf_tensor("sc_ncs", [64, CH], F32))
    csbf = stack.enter_context(nc.sbuf_tensor("sc_csbf", [64, CH], BF16))
    tsum = stack.enter_context(nc.sbuf_tensor("sc_ts", [64, 1], F32))
    ncsT = stack.enter_context(nc.sbuf_tensor("sc_ncsT", [128, 64], F32))
    dtsT = stack.enter_context(nc.sbuf_tensor("sc_dtsT", [128, 64], F32))
    tsrow = stack.enter_context(nc.sbuf_tensor("sc_tsr", [1, 64], F32))
    dec = stack.enter_context(nc.sbuf_tensor("sc_dec", [128, 64], F32))
    cdb = stack.enter_context(nc.sbuf_tensor("sc_cdb", [128, 64], F32))
    ddec = stack.enter_context(nc.sbuf_tensor("sc_ddec", [128, 64], F32))
    x_tm = stack.enter_context(nc.sbuf_tensor("sc_xtm", [128, XG * CH], BF16))
    b_tm = stack.enter_context(nc.sbuf_tensor("sc_btm", [128, XG * CH], BF16))
    xdt = stack.enter_context(nc.sbuf_tensor("sc_xdt", [128, NH * HD], BF16))
    xdtd = stack.enter_context(nc.sbuf_tensor("sc_xdtd", [128, NH * HD], BF16))
    lt = [stack.enter_context(nc.sbuf_tensor(f"sc_lt{i}", [128, CH], F32)) for i in range(2)]
    lt_blk = stack.enter_context(nc.sbuf_tensor("sc_ltb", [128, 32 * CH], BF16))
    glt_blk = stack.enter_context(nc.sbuf_tensor("sc_gltb", [128, 32 * CH], BF16))
    cpr_blk = stack.enter_context(nc.sbuf_tensor("sc_cprb", [128, 32 * CH], BF16))
    ecs_blk = stack.enter_context(nc.sbuf_tensor("sc_ecsb", [128, 32 * CH], BF16))
    yall = stack.enter_context(nc.sbuf_tensor("sc_y", [128, NG * CH], BF16))
    carry_bf = stack.enter_context(nc.sbuf_tensor("sc_cbf", [128, NH * HD], BF16))

    ps_csb = stack.enter_context(nc.psum_tensor("sc_pcsb", [128, 512], F32))[:, 0:CH]
    ps_g = [stack.enter_context(nc.psum_tensor(f"sc_pg{i}", [128, 512], F32))[:, 0:CH]
            for i in range(2)]
    ps_y = [stack.enter_context(nc.psum_tensor(f"sc_py{i}", [128, 512], F32))[0:64, 0:CH]
            for i in range(2)]
    ps_st = stack.enter_context(nc.psum_tensor("sc_pst", [128, 512], F32))[:, 0:HD]
    ps_pre = stack.enter_context(nc.psum_tensor("sc_pre", [128, 512], F32))[:, 0:256]
    ps_bt = stack.enter_context(nc.psum_tensor("sc_pbt", [128, 1024], BF16))

    nck = MB["NCK"] if nck is None else nck
    if tscr is None:
        tscr = nc.dram_tensor(f"sc_tscr_{id(prog) % 997}", [2, 64, 1], F32,
                              kind="Internal")[:]
    hist = {}
    dmm = prog.op("tensor", lambda t: t.matmul(
        ps_pre[0:128, 0:64], ac["oner"][:], drow[:], start=True, stop=True),
        deps=[l3, ac["ready"]], inc=1)
    db_ev = prog.op("vector", lambda v: v.tensor_copy(Db[:], ps_pre[0:128, 0:64]),
                    deps=[dmm], inc=1)
    hist["pre1_used"] = db_ev

    def H(key):
        v = hist.get(key)
        return [v] if v is not None else list(entry)

    def chunk_body(b, c, cpar, zcb, xs, bs, cs_t, dts):
            tlo = b * S + c * CH
            pb = hist
            pdep = [pb.get(("tiles_used_pe", cpar)), pb.get(("tiles_used_ve", cpar)),
                    pb.get(("tiles_used_po", cpar))]
            pdep = [d for d in pdep if d is not None] or list(entry)
            for gx in range(XG):
                xl = prog.op("sync", lambda s, gx=gx, tlo=tlo: s.dma_start(
                    xs[:, gx * CH:(gx + 1) * CH],
                    convT[gx * 128:(gx + 1) * 128, tlo:tlo + CH]),
                    deps=pdep, inc=16, dma=True, sem=f"sc_lx{cpar}")
                bl = prog.op("sync", lambda s, gx=gx, tlo=tlo: s.dma_start(
                    bs[:, gx * CH:(gx + 1) * CH],
                    convT[1024 + gx * 128:1024 + (gx + 1) * 128, tlo:tlo + CH]),
                    deps=pdep, inc=16, dma=True, sem=f"sc_lb{cpar}")
            for gc in range(NG):
                cl = prog.op("scalar", lambda s, gc=gc, tlo=tlo: s.dma_start(
                    cs_t[:, gc * CH:(gc + 1) * CH],
                    convT[2048 + gc * 128:2048 + (gc + 1) * 128, tlo:tlo + CH]),
                    deps=pdep, inc=16, dma=True, sem=f"sc_lc{cpar}")
            xlev = [xl] * XG
            blev = [bl] * XG
            clev = [cl] * NG
            ldt = prog.op("sync", lambda s, tlo=tlo: s.dma_start(
                dts[:], dtT[0:64, tlo:tlo + CH]),
                deps=H(("dts_used", cpar)), inc=16, dma=True, sem=f"sc_ldt{cpar}")
            if stage < 2:
                return
            p1 = prog.op("scalar", lambda s: s.activation(
                dtsp[:], dts[:], AF.Exp, bias=dtb[:]),
                deps=[ldt, l1] + H("dtsp_used"), inc=1)
            p2 = prog.op("scalar", lambda s: s.activation(
                dtsp[:], dtsp[:], AF.Ln, bias=cc("one")[0:64, :]),
                deps=[p1, cinit_ev], inc=1)
            p3 = prog.op("vector", lambda v: v.tensor_scalar_mul(
                dA[:], dtsp[:], negA[:]), deps=[p2, negA_ev] + H("dA_used"), inc=1)
            p4 = prog.op("vector", lambda v: v.tensor_tensor_scan(
                csum_[:], dA[:], dA[:], 0.0, ALU.add, ALU.bypass),
                deps=[p3] + H("cs_used"), inc=1)
            p5 = prog.op("vector", lambda v: v.reduce_sum(
                tsum[:], dA[:], axis=AX.X), deps=[p3] + H("tsum_used"), inc=1)
            p6 = prog.op("gpsimd", lambda g: g.tensor_copy(csbf[:], csum_[:]),
                         deps=[p4] + H("csbf_used"), inc=1)
            p7 = prog.op("vector", lambda v: v.tensor_scalar_mul(ncs[:], csum_[:], -1.0),
                         deps=[p4] + H("ncs_used"), inc=1)
            if stage == 21:
                return
            t1 = prog.op("tensor", lambda t: t.matmul(
                ps_pre[0:CH, 0:64], ncs[:], identf[0:64, 0:64],
                start=True, stop=True),
                deps=[p7, ac["ready"]] + H("pre1_used"), inc=1)
            t1e = prog.op("vector", lambda v: v.tensor_copy(ncsT[:], ps_pre[0:CH, 0:64]),
                          deps=[t1] + H("ncsT_used"), inc=1)
            hist["pre1_used"] = t1e
            if stage == 22:
                return
            t2 = prog.op("tensor", lambda t: t.matmul(
                ps_pre[0:CH, 64:128], dtsp[:], identf[0:64, 0:64],
                start=True, stop=True),
                deps=[p2] + H("pre2_used"), inc=1)
            t2e = prog.op("vector", lambda v: v.tensor_copy(dtsT[:], ps_pre[0:CH, 64:128]),
                          deps=[t2] + H("dtsT_used"), inc=1)
            hist["pre2_used"] = t2e
            if stage == 23:
                return
            par2 = (b * 16 + c) % 2
            t3 = prog.op("sync", lambda s, par2=par2: s.dma_start(
                tscr[par2, :, 0], tsum[:, 0]),
                deps=[p5] + H(("tscr", par2)), inc=16, dma=True, sem=f"sc_tb{par2}")
            t3e = prog.op("sync", lambda s, par2=par2: s.dma_start(
                tsrow[:], tscr[par2:par2 + 1, :, 0]),
                deps=[t3] + H("tsrow_used"), inc=16, dma=True, sem=f"sc_tr{par2}")
            hist[("tscr", par2)] = t3e
            hist["pre3_used"] = t3e
            if stage == 24:
                return
            if stage == 27:
                return
            t4 = prog.op("tensor", lambda t: t.matmul(
                ps_pre[0:128, 192:256], ac["oner"][:], tsrow[:], start=True, stop=True),
                deps=[t3e, ac["ready"]] + H("pre4_used"), inc=1)
            d1 = prog.op("vector", lambda v: v.tensor_add(
                dec[:], ncsT[:], ps_pre[0:128, 192:256]),
                deps=[t4, t1e] + H("dec_used"), inc=1)
            d2 = prog.op("scalar", lambda s: s.activation(dec[:], dec[:], AF.Exp),
                         deps=[d1], inc=1)
            d3 = prog.op("scalar", lambda s: s.activation(cdb[:], ps_pre[0:128, 192:256],
                                                          AF.Exp),
                         deps=[t4] + H("cdb_used"), inc=1)
            hist["pre4_used"] = d3
            hist["tsrow_used"] = t4
            if stage < 3:
                return
            xt_evs = [None] * XG
            for gx in range(XG):
                tp = prog.op("tensor", lp(nc, lambda t, gx=gx: t.transpose(
                    ps_bt[:, gx * CH:(gx + 1) * CH], xs[:, gx * CH:(gx + 1) * CH],
                    ident[:])),
                    deps=([xlev[gx], ac["ready"]] + (H("psbt_used") if gx == 0 else [])),
                    inc=(1 if gx == XG - 1 else 0))
            xtw = prog.op("scalar", lambda s: s.copy(x_tm[:], ps_bt[:]),
                          deps=[tp] + H("xtm_used"), inc=1)
            for gx in range(XG):
                xt_evs[gx] = xtw
            bt_evs = [None] * XG
            for gx in range(XG):
                tp = prog.op("tensor", lp(nc, lambda t, gx=gx: t.transpose(
                    ps_bt[:, gx * CH:(gx + 1) * CH], bs[:, gx * CH:(gx + 1) * CH],
                    ident[:])),
                    deps=([blev[gx]] + ([xtw] if gx == 0 else [])),
                    inc=(1 if gx == XG - 1 else 0))
            btw = prog.op("scalar", lambda s: s.copy(b_tm[:], ps_bt[:]),
                          deps=[tp] + H("btm_used"), inc=1)
            hist["psbt_used"] = btw
            ddv = prog.op("vector", lambda v: v.tensor_tensor(
                ddec[:], dtsT[:], dec[:], ALU.mult),
                deps=[t2e, d2] + H("ddec_used"), inc=1)
            xdt_ev = {}
            for h in range(NH):
                gx, off = h // 8, (h % 2) * 64
                ea = prog.op("vector", lambda v, gx=gx, off=off, h=h: v.tensor_scalar_mul(
                    xdt[:, h * HD:(h + 1) * HD],
                    x_tm[:, gx * CH + off:gx * CH + off + HD], dtsT[:, h:h + 1]),
                    deps=[xt_evs[gx], t2e] + (H("xdt_used") if h == 0 else []), inc=1)
                eb = prog.op("scalar", lambda s, gx=gx, off=off, h=h: s.activation(
                    xdtd[:, h * HD:(h + 1) * HD],
                    x_tm[:, gx * CH + off:gx * CH + off + HD], AF.Copy,
                    scale=ddec[:, h:h + 1]),
                    deps=[xt_evs[gx], ddv] + (H("xdtd_used") if h == 0 else []), inc=1)
                xdt_ev[h] = (ea, eb)
            if stage < 4:
                return
            y_evs = []
            s8a = s8b = s9 = s10 = None
            s3e = {}
            s4e = {}
            s6e = {}
            s7e = {}
            for hblk in range(2):
                hs0 = hblk * 32
                heads = range(hs0, hs0 + 32) if stage >= 5 else range(hs0, hs0 + 1)
                for h in heads:
                    hs = h % 32
                    s1 = prog.op("tensor", lambda t, h=h: t.matmul(
                        ps_csb[:], oh[:, h * 128:(h + 1) * 128], csbf[:],
                        start=True, stop=True),
                        deps=[p6, ohev] + H("csb_used") + H("csb_used2"), inc=1)
                    par = h % 2
                    s2 = prog.op("vector", lambda v, h=h, par=par: v.scalar_tensor_tensor(
                        lt[par][:], ps_csb[:], ncsT[:, h:h + 1], mbias[:],
                        ALU.add, ALU.min),
                        deps=[s1, t1e, mbev] + H(("lt_used", par)), inc=1)
                    hist["csb_used2"] = s2
                    s3 = prog.op("scalar", lambda s, hs=hs, par=par: s.activation(
                        lt_blk[:, hs * CH:(hs + 1) * CH], lt[par][:], AF.Exp),
                        deps=[s2] + H(("ltbu", hs)), inc=1)
                    hist[("lt_used", par)] = s3
                    s4 = prog.op("scalar", lambda s, hs=hs: s.activation(
                        ecs_blk[:, hs * CH:(hs + 1) * CH], ps_csb[:], AF.Exp),
                        deps=[s1] + H(("ecsu", hs)), inc=1)
                    hist["csb_used"] = s4
                    s3e[h] = s3
                    s4e[h] = s4
                for h in heads:
                    hs = h % 32
                    par = h % 2
                    gB = h // 8
                    gC = h // 2
                    s5 = prog.op("tensor", lambda t, gB=gB, gC=gC, par=par: t.matmul(
                        ps_g[par][:], bs[:, gB * CH:(gB + 1) * CH],
                        cs_t[:, gC * CH:(gC + 1) * CH], start=True, stop=True),
                        deps=[blev[gB], clev[gC]] + H(("g_used", par)), inc=1)
                    s6 = prog.op("vector", lambda v, hs=hs, par=par: v.tensor_tensor(
                        glt_blk[:, hs * CH:(hs + 1) * CH], ps_g[par][:],
                        lt_blk[:, hs * CH:(hs + 1) * CH], ALU.mult),
                        deps=[s5, s3e[h]] + H(("gltu", hs)), inc=1)
                    hist[("g_used", par)] = s6
                    hist[("ltbu", hs)] = s6
                    s7 = prog.op("gpsimd", lambda g, hs=hs, gC=gC: g.tensor_tensor(
                        cpr_blk[:, hs * CH:(hs + 1) * CH],
                        cs_t[:, gC * CH:(gC + 1) * CH],
                        ecs_blk[:, hs * CH:(hs + 1) * CH], ALU.mult),
                        deps=[s4e[h], clev[gC]] + H(("cpru", hs)), inc=1)
                    hist[("ecsu", hs)] = s7
                    s6e[h] = s6
                    s7e[h] = s7
                for h in heads:
                    hs = h % 32
                    par = h % 2
                    gB = h // 8
                    g2, off = h // 2, (h % 2) * 64
                    s8a = prog.op("tensor", lambda t, h=h, hs=hs, par=par: t.matmul(
                        ps_y[par][:], xdt[:, h * HD:(h + 1) * HD],
                        glt_blk[:, hs * CH:(hs + 1) * CH],
                        start=True, stop=False),
                        deps=[s6e[h], xdt_ev[h][0]] + H(("y_used", par)), inc=0)
                    s8b = prog.op("tensor", lambda t, h=h, hs=hs, par=par: t.matmul(
                        ps_y[par][:], carry_bf[:, h * HD:(h + 1) * HD],
                        cpr_blk[:, hs * CH:(hs + 1) * CH],
                        start=False, stop=True),
                        deps=[s7e[h], zcb] + H(("cbf_w", h)), inc=1)
                    hist[("gltu", hs)] = s8b
                    hist[("cpru", hs)] = s8b
                    s9 = prog.op("tensor", lambda t, h=h, gB=gB: t.matmul(
                        ps_st[:], b_tm[:, gB * CH:(gB + 1) * CH],
                        xdtd[:, h * HD:(h + 1) * HD], start=True, stop=True),
                        deps=[btw, xdt_ev[h][1]] + H("st_used"), inc=1)
                    s10 = prog.op("vector", lambda v, h=h: v.scalar_tensor_tensor(
                        carry_bf[:, h * HD:(h + 1) * HD],
                        carry_bf[:, h * HD:(h + 1) * HD],
                        cdb[:, h:h + 1], ps_st[:], ALU.mult, ALU.add),
                        deps=[s9, s8b, d3, zcb], inc=1)
                    hist["st_used"] = s10
                    hist[("cbf_w", h)] = s10
                    s11 = prog.op("vector", lambda v, h=h, g2=g2, gB=gB, off=off, par=par: v.scalar_tensor_tensor(
                        yall[off:off + 64, g2 * CH:(g2 + 1) * CH],
                        xs[off:off + 64, gB * CH:(gB + 1) * CH],
                        Db[off:off + 64, h:h + 1], ps_y[par][:], ALU.mult, ALU.add),
                        deps=[s8b, xlev[gB], db_ev] + H("yall_used"), inc=1)
                    hist[("y_used", par)] = s11
                    y_evs.append(s11)
            yst = None
            for q in range(8):
                yst = prog.op("scalar", lambda g, q=q, tlo=tlo: g.dma_start(
                    yTd[q * 512:(q + 1) * 512, tlo:tlo + CH].rearrange(
                        "(a p) c -> p a c", p=128),
                    yall[:, q * 512:(q + 1) * 512].rearrange(
                        "p (a c) -> p a c", c=CH)),
                    deps=[y_evs[-1]] + ([pb.get("yst")] if pb.get("yst") else []),
                    inc=16, dma=True, sem="sc_y")
            hist["yst"] = yst
            hist["yall_used"] = yst
            hist[("tiles_used_pe", cpar)] = s9
            hist[("tiles_used_ve", cpar)] = y_evs[-1]
            hist[("tiles_used_po", cpar)] = s7e[NH - 1]
            hist[("dts_used", cpar)] = p1
            hist["dtsp_used"] = p3
            hist["dA_used"] = p5
            hist["cs_used"] = p7
            hist["tsum_used"] = t3
            hist["csbf_used"] = s1
            hist["ncs_used"] = t1
            hist["ncsT_used"] = s2
            hist["dtsT_used"] = xdt_ev[NH - 1][0]
            hist["dec_used"] = ddv
            hist["ddec_used"] = xdt_ev[NH - 1][1]
            hist["cdb_used"] = s10
            hist["xdt_used"] = s8b
            hist["xdtd_used"] = s9
            hist["xtm_used"] = xdt_ev[NH - 1][0]
            hist["btm_used"] = s9

    for b in range(nb):
        zcb = prog.op("gpsimd", lambda g: g.memset(carry_bf[:], 0.0),
                      deps=H("carry_used"), inc=1)
        for c in range(nck):
            cpar = (b * nck + c) % 2
            chunk_body(b, c, cpar, zcb, xs_p[cpar], bs_p[cpar],
                       cs_p[cpar], dts_p[cpar])
        hist["carry_used"] = hist.get(("cbf_w", NH - 1))
    return prog.barrier_all()


def emit_gating(nc, stack, prog, ac, cc, cinit_ev, yTd, zT, nw, ynT, entry=()):
    nwt = stack.enter_context(nc.sbuf_tensor("gt_nw", [128, 32], F32))
    lnw = prog.op("sync", lambda s: s.dma_start(
        nwt[:], nw[:, :]), deps=list(entry), inc=16, dma=True, sem="gt_nw")
    yb = [stack.enter_context(nc.sbuf_tensor(f"gt_y{i}", [128, 512], BF16)) for i in range(3)]
    zb = [stack.enter_context(nc.sbuf_tensor(f"gt_z{i}", [128, 512], BF16)) for i in range(3)]
    yz = [stack.enter_context(nc.sbuf_tensor(f"gt_yz{i}", [128, 512], BF16)) for i in range(3)]
    ysq = [stack.enter_context(nc.sbuf_tensor(f"gt_sq{i}", [128, 512], BF16)) for i in range(3)]
    rst = [stack.enter_context(nc.sbuf_tensor(f"gt_rs{i}", [1, 512], F32)) for i in range(3)]
    rsb = [stack.enter_context(nc.sbuf_tensor(f"gt_rsb{i}", [1, 512], BF16)) for i in range(3)]
    ob = [stack.enter_context(nc.sbuf_tensor(f"gt_o{i}", [128, 512], BF16)) for i in range(3)]
    ps_ss = [stack.enter_context(nc.psum_tensor(f"gt_ss{i}", [128, 512], F32))[0:1, :]
             for i in range(2)]
    ps_rb = [stack.enter_context(nc.psum_tensor(f"gt_rb{i}", [128, 512], F32)) for i in range(2)]
    hist = {}

    def H(key):
        v = hist.get(key)
        return [v] if v is not None else list(entry)

    it = 0
    for j in range(32):
        for ci in range(8):
            sl = it % 3
            lo = ci * 512
            ly = prog.op("sync", lambda s, j=j, lo=lo, sl=sl: s.dma_start(
                yb[sl][:], yTd[j * 128:(j + 1) * 128, lo:lo + 512]),
                deps=H(("yz", sl)), inc=16, dma=True, sem=f"gt_y{sl}")
            lz = prog.op("sync", lambda s, j=j, lo=lo, sl=sl: s.dma_start(
                zb[sl][:], zT[j * 128:(j + 1) * 128, lo:lo + 512]),
                deps=H(("sz", sl)), inc=16, dma=True, sem=f"gt_z{sl}")
            sz = prog.op("scalar", lambda s, sl=sl: s.activation(
                zb[sl][:], zb[sl][:], AF.Silu), deps=[lz], inc=1)
            hist[("sz", sl)] = sz
            yzv = prog.op("vector", lambda v, sl=sl: v.tensor_mul(
                yz[sl][:], yb[sl][:], zb[sl][:]), deps=[ly, sz] + H(("sq", sl)), inc=1)
            hist[("yz", sl)] = yzv
            sq = prog.op("scalar", lambda s, sl=sl: s.activation(
                ysq[sl][:], yz[sl][:], AF.Square), deps=[yzv] + H(("ssmm", sl)), inc=1)
            hist[("sq", sl)] = sq
            ssb = it % 2
            ssmm = prog.op("tensor", lambda t, sl=sl, ssb=ssb: t.matmul(
                ps_ss[ssb], ac["onec"][:], ysq[sl][:], start=True, stop=True),
                deps=[sq, ac["ready"]] + H(("ssp", ssb)), inc=1)
            hist[("ssmm", sl)] = ssmm
            r1 = prog.op("vector", lambda v, sl=sl, ssb=ssb: v.tensor_scalar(
                rst[sl][:], ps_ss[ssb], 1.0 / 128.0, LN_EPS, ALU.mult, ALU.add),
                deps=[ssmm] + H(("r2", sl)), inc=1)
            hist[("ssp", ssb)] = r1
            r2 = prog.op("scalar", lambda s, sl=sl: s.activation(
                rst[sl][:], rst[sl][:], AF.Sqrt), deps=[r1], inc=1)
            hist[("r2", sl)] = r2
            r3 = prog.op("vector", lp(nc, lambda v, sl=sl: v.reciprocal(
                rsb[sl][:], rst[sl][:]),
                reason="bf16 rstd broadcast operand (bf16-level tolerance)"),
                deps=[r2] + H(("bc", sl)), inc=1)
            bank = it % 2
            bc = prog.op("tensor", lambda t, sl=sl, bank=bank: t.matmul(
                ps_rb[bank][:], ac["onerb"][:], rsb[sl][:], start=True, stop=True),
                deps=[r3] + H(("bcp", bank)), inc=1)
            hist[("bc", sl)] = bc
            yn = prog.op("vector", lambda v, j=j, sl=sl, bank=bank: v.scalar_tensor_tensor(
                ob[sl][:], yz[sl][:], nwt[:, j:j + 1], ps_rb[bank][:],
                ALU.mult, ALU.mult),
                deps=[bc, lnw, yzv] + H(("st", sl)), inc=1)
            hist[("bcp", bank)] = yn
            st = prog.op("scalar", lambda g, j=j, lo=lo, sl=sl: g.dma_start(
                ynT[j * 128:(j + 1) * 128, lo:lo + 512], ob[sl][:]),
                deps=[yn], inc=16, dma=True, sem=f"gt_st{sl}")
            hist[("st", sl)] = st
            it += 1
    return prog.barrier_all()


def emit_wout(nc, stack, prog, woutT, ynT, s1T, entry=()):
    ktiles = 32
    ab = stack.enter_context(nc.sbuf_tensor("wo_act", [128, ktiles * 1024], BF16))
    res = {
        "wb": [stack.enter_context(nc.sbuf_tensor(f"wo_wb{i}", [128, ktiles * 128], BF16))
               for i in range(2)],
        "ps": [stack.enter_context(nc.psum_tensor(f"wo_ps{i}", [128, 512], F32))
               for i in range(4)],
        "ot": [stack.enter_context(nc.sbuf_tensor(f"wo_ot{i}", [128, 512], BF16))
               for i in range(4)],
    }
    barrier = list(entry)
    for c4 in range(4):
        tlo = c4 * 1024
        aev = None
        for kt in range(ktiles):
            aev = prog.op("scalar", lambda s, kt=kt, tlo=tlo: s.dma_start(
                ab[:, kt * 1024:(kt + 1) * 1024],
                ynT[kt * 128:(kt + 1) * 128, tlo:tlo + 1024]),
                deps=barrier, inc=16, dma=True, sem="wo_a")

        def out_store(mt, msz, ci, lo, sz, src, dep, tlo=tlo):
            g_ = (mt * 2 + ci) % 4
            return prog.op("sync", lambda g: g.dma_start(
                s1T[mt * 128:mt * 128 + msz, tlo + lo:tlo + lo + sz], src),
                deps=[dep], inc=16, dma=True, sem=f"wo_st{g_}")

        gpe, gep, gst = emit_dense_mm(
            nc, stack, prog, f"wo{c4}", wT=woutT, M_tot=D, K_tot=4096, n_tok=1024,
            act_tile=lambda kt, lo, sz: ab[:, kt * 1024 + lo:kt * 1024 + lo + sz],
            act_ready=aev, out_store=out_store, res=res, entry=barrier)
        res["guard"] = gpe[-1]
        res["guard_st"] = gst[-1]
        barrier = [gpe[-1]]
    return prog.barrier_all()



N_CTRL = 130156
N_SAMP = 16777216
NTILES = N_SAMP // 16384


def host_spline_tables():
    t = np.linspace(0.0, N_CTRL - 1.0, N_SAMP, dtype=np.float64)
    i = np.clip(np.floor(t).astype(np.int64), 0, N_CTRL - 2)
    f = (t - i).astype(np.float32).astype(np.float64)
    f2 = f * f
    f3 = f2 * f
    a = np.stack([
        0.5 * (-f + 2 * f2 - f3),
        0.5 * (2.0 - 5 * f2 + 3 * f3),
        0.5 * (f + 4 * f2 - 3 * f3),
        0.5 * (-f2 + f3),
    ], axis=1)
    idx = np.stack([np.clip(i + k, 0, N_CTRL - 1) for k in (-1, 0, 1, 2)],
                   axis=1) + 2
    n_ar = np.arange(N_SAMP, dtype=np.int64)
    m = (n_ar % 16384) // 128
    tau = n_ar // 16384
    base = i[::16384]
    j = idx - (base[tau] + m)[:, None]
    assert j.min() >= 0 and j.max() <= 5, (j.min(), j.max())
    btab = np.zeros((N_SAMP, 6), np.float32)
    np.put_along_axis(btab, j, a.astype(np.float32), axis=1)
    btab = btab.reshape(NTILES, 128, 128, 6).transpose(0, 1, 3, 2)
    slot_mask = (np.abs(btab) > 0).any(axis=(1, 3))
    return (np.ascontiguousarray(btab.astype(BF)), base.astype(np.int64),
            slot_mask)


def host_pp_ext(p):
    return np.concatenate([p[:1], p[:1], p, np.repeat(p[-1:], 8)]).astype(np.float32)


def emit_spline_recon(nc, stack, prog, btab, bases, slot_mask, pps, wouts,
                      entry=(), taus=None, res=None):
    if res is None:
        res = {}
        res["bt"] = [stack.enter_context(nc.sbuf_tensor(f"sp_bt{i}", [128, 768], BF16, side="right")) for i in range(2)]
        res["lt"] = {(w, s): stack.enter_context(nc.sbuf_tensor(f"sp_lt{w}_{s}", [128, 8], F32, side="right"))
                     for w in range(3) for s in range(2)}
        res["acc"] = [[stack.enter_context(nc.sbuf_tensor(f"sp_ac{i}_{p}", [128, 128], F32, side="right"))
                       for p in range(2)] for i in range(3)]
        res["accb"] = [[stack.enter_context(nc.sbuf_tensor(f"sp_ab{i}_{p}", [128, 128], BF16, side="right"))
                        for p in range(2)] for i in range(3)]
        res["strip"] = [
            [stack.enter_context(nc.sbuf_tensor(f"sp_st{w}_{i}", [128, 2048 if w < 2 else 8192],
                                                BF16, side="right")) for i in range(2)]
            for w in range(3)]
        res["ident"] = stack.enter_context(nc.sbuf_tensor("sp_id", [128, 128], BF16, side="right"))
        ident = res["ident"]
        e1 = prog.op("gpsimd", lambda g: g.memset(ident[:], 1.0),
                     deps=list(entry), inc=1, sem="R_cg")
        res["idev"] = prog.op("gpsimd", lambda g: g.affine_select(
            ident[:], ident[:], [[1, 128]], ALU.is_equal, 0.0,
            channel_multiplier=-1), deps=[e1], inc=1, sem="R_cg")
        res["hist"] = {}
        return res
    bt, lt, acc, accb = res["bt"], res["lt"], res["acc"], res["accb"]
    sbuf_strip, ident, idev = res["strip"], res["ident"], res["idev"]
    ps_tr = res["ps_tr"]
    hist = res["hist"]

    def H(k):
        v = hist.get(k)
        return [v] if v is not None else list(entry)

    evs_final = []
    if taus is None:
        taus = range(NTILES)
    for tau in taus:
        bslot = tau % 2
        lb = prog.op("sync", lambda s, tau=tau, bslot=bslot: s.dma_start(
            bt[bslot][:], btab[tau].rearrange("p j n -> p (j n)")),
            deps=H(("bt", bslot)), inc=16, dma=True, sem=f"R_b{bslot}")
        for w in range(3):
            lslot = tau % 2
            src = pps[w][bases[tau]:bases[tau] + 8].unsqueeze(0).broadcast_to([128, 8])
            src.ap[0] = [1, 128]
            lh = prog.op("scalar", lambda s, w=w, lslot=lslot, src=src: s.dma_start(
                lt[(w, lslot)][:], src),
                deps=H(("lt", w, lslot)), inc=16, dma=True, sem=f"R_l{w}{lslot}")
            slots = [j for j in range(6) if slot_mask[tau, j]]
            nA = (len(slots) + 1) // 2
            sA, sB = slots[:nA], slots[nA:]
            ap_ = tau % 2

            def B(j, bslot=bslot):
                return bt[bslot][:, j * 128:(j + 1) * 128]

            def L(j, w=w, lslot=lslot):
                return lt[(w, lslot)][:, j:j + 1]
            fe = prog.op("vector", lambda v, w=w, j=slots[0], ap_=ap_, B=B, L=L: v.tensor_scalar_mul(
                acc[w][ap_][:], B(j), L(j)),
                deps=[lb, lh] + H(("acc", w, ap_)), inc=1, sem="R_cv")
            for j in slots[1:-1]:
                fe = prog.op("vector", lambda v, w=w, j=j, ap_=ap_, B=B, L=L: v.scalar_tensor_tensor(
                    acc[w][ap_][:], B(j), L(j), acc[w][ap_][:], ALU.mult, ALU.add),
                    deps=[fe], inc=1, sem="R_cv")
            fe = prog.op("vector", lambda v, w=w, j=slots[-1], ap_=ap_, B=B, L=L: v.scalar_tensor_tensor(
                accb[w][ap_][:], B(j), L(j), acc[w][ap_][:], ALU.mult, ALU.add),
                deps=[fe] + H(("accb", w, ap_)), inc=1, sem="R_cv")
            hist[("lt", w, lslot)] = fe
            hist[("acc", w, ap_)] = fe
            if w == 2:
                hist[("bt", bslot)] = fe
            grp = tau % 8
            tp = prog.op("tensor", lp(nc, lambda t, w=w, grp=grp, ap_=ap_: t.transpose(
                ps_tr[w][:, grp * 128:(grp + 1) * 128], accb[w][ap_][:], ident[:])),
                deps=[fe, idev] + H(("pstr", w, grp)), inc=1, sem="R_cp")
            hist[("accb", w, ap_)] = tp
            if tau % 4 == 3:
                g4 = (tau // 4) % 2
                if w < 2:
                    sslot = (tau // 16) % 2
                    within = (tau - 3) % 16
                    ev = prog.op("vector", lambda v, w=w, g4=g4, sslot=sslot, within=within: v.tensor_copy(
                        _strip_dst_gu(sbuf_strip[w][sslot], within),
                        _tr_src(ps_tr[w], g4)),
                        deps=[tp] + H(("strip", w, sslot)), inc=1, sem="R_cv")
                else:
                    sslot = (tau // 64) % 2
                    within = (tau - 3) % 64
                    ev = prog.op("vector", lambda v, w=w, g4=g4, sslot=sslot, within=within: v.tensor_copy(
                        _strip_dst_d(sbuf_strip[w][sslot], within),
                        _tr_src(ps_tr[w], g4)),
                        deps=[tp] + H(("strip", w, sslot)), inc=1, sem="R_cv")
                hist[("pstr", w, (tau - 3) % 8)] = ev
                hist[("pstr", w, (tau - 2) % 8)] = ev
                hist[("pstr", w, (tau - 1) % 8)] = ev
                hist[("pstr", w, tau % 8)] = ev
                hist[("evac", w)] = ev
                per = 16 if w < 2 else 64
                if (tau + 1) % per == 0:
                    mt = tau // per
                    sev = prog.op("sync", lambda g, w=w, mt=mt, sslot=sslot: g.dma_start(
                        wouts[w][mt, :, :], sbuf_strip[w][sslot][:]),
                        deps=[ev], inc=16, dma=True, sem=f"R_o{w}{sslot}")
                    hist[("strip", w, sslot)] = sev
                    evs_final.append(sev)
    return res


def _tr_src(ps, g4):
    return ps[:, g4 * 512:(g4 + 1) * 512]


def _strip_dst_gu(buf, within):
    ap = buf[:, within * 8:within * 8 + 512].copy()
    ap.ap.pop()
    ap.ap.extend([[8, 4], [1, 8], [128, 16]])
    return ap


def _strip_dst_d(buf, within):
    ap = buf[:, within * 2:within * 2 + 512].copy()
    ap.ap.pop()
    ap.ap.extend([[2, 4], [1, 2], [128, 64]])
    return ap


def emit_spline_mm(nc, stack, prog, cc, cinit_ev, wgT, wuT, wdT, xn1, s1T,
                   interT, sg1T, entry=()):
    act, act_ev = emit_actT_load(nc, stack, prog, "sg", xn1, D // 128, T,
                                 entry=entry)
    ktiles = 16
    wgb = [stack.enter_context(nc.sbuf_tensor(f"sg_wg{i}", [128, 2048], BF16)) for i in range(2)]
    wub = [stack.enter_context(nc.sbuf_tensor(f"sg_wu{i}", [128, 2048], BF16)) for i in range(2)]
    sil = [stack.enter_context(nc.sbuf_tensor(f"sg_si{i}", [128, 512], BF16)) for i in range(4)]
    ot = [stack.enter_context(nc.sbuf_tensor(f"sg_ot{i}", [128, 512], BF16)) for i in range(4)]
    psA = [stack.enter_context(nc.psum_tensor(f"sg_psA{i}", [128, 512], F32)) for i in range(2)]
    psB = [stack.enter_context(nc.psum_tensor(f"sg_psB{i}", [128, 512], F32)) for i in range(2)]
    hist = {}

    def H(k):
        v = hist.get(k)
        return [v] if v is not None else list(entry)

    for mt in range(64):
        par = mt % 2
        lg = prog.op("sync", lambda s, mt=mt, par=par: s.dma_start(
            wgb[par][:], wgT[mt, :, :]), deps=H(("wg", par)), inc=16, dma=True,
            sem=f"sg_wg{par}")
        lu = prog.op("sync", lambda s, mt=mt, par=par: s.dma_start(
            wub[par][:], wuT[mt, :, :]), deps=H(("wu", par)), inc=16, dma=True,
            sem=f"sg_wu{par}")
        for ci in range(8):
            lo = ci * 512
            bank = ci % 2
            slot = ci % 4
            depsA = [lg, act_ev] + ([hist.get(("eA", bank))] if ("eA", bank) in hist else [])
            depsB = [lu, act_ev] + ([hist.get(("eB", bank))] if ("eB", bank) in hist else [])
            peA = peB = None
            for kt in range(ktiles):
                peA = prog.op("tensor", lambda t, kt=kt, par=par, bank=bank, lo=lo: t.matmul(
                    psA[bank][:], wgb[par][:, kt * 128:(kt + 1) * 128],
                    act(kt)[:, lo:lo + 512], start=(kt == 0), stop=(kt == ktiles - 1)),
                    deps=(depsA if kt == 0 else ()), inc=(1 if kt == ktiles - 1 else 0))
            for kt in range(ktiles):
                peB = prog.op("tensor", lambda t, kt=kt, par=par, bank=bank, lo=lo: t.matmul(
                    psB[bank][:], wub[par][:, kt * 128:(kt + 1) * 128],
                    act(kt)[:, lo:lo + 512], start=(kt == 0), stop=(kt == ktiles - 1)),
                    deps=(depsB if kt == 0 else ()), inc=(1 if kt == ktiles - 1 else 0))
            hist[("wg", par)] = peB
            hist[("wu", par)] = peB
            sv = prog.op("scalar", lambda s, bank=bank, slot=slot: s.activation(
                sil[slot][:], psA[bank][:], AF.Silu),
                deps=[peA] + H(("sil", slot)), inc=1)
            hist[("eA", bank)] = sv
            hv = prog.op("vector", lambda v, bank=bank, slot=slot: v.tensor_tensor(
                ot[slot][:], psB[bank][:], sil[slot][:], ALU.mult),
                deps=[peB, sv] + H(("st", slot)), inc=1)
            hist[("sil", slot)] = hv
            hist[("eB", bank)] = hv
            stv = prog.op("gpsimd", lambda g, mt=mt, lo=lo, slot=slot: g.dma_start(
                interT[mt * 128:(mt + 1) * 128, lo:lo + 512], ot[slot][:]),
                deps=[hv], inc=16, dma=True, sem=f"sg_st{slot}")
            hist[("st", slot)] = stv
    return prog.barrier_all()


def emit_spline_down(nc, stack, prog, cc, cinit_ev, wdT, interT, s1T, sg1T,
                     entry=()):
    ktiles = 64
    hb = stack.enter_context(nc.sbuf_tensor("sd_hb", [128, ktiles * 1024], BF16))
    wqb = [stack.enter_context(nc.sbuf_tensor(f"sd_wq{i}", [128, ktiles * 128], BF16)) for i in range(2)]
    s0b = [stack.enter_context(nc.sbuf_tensor(f"sd_s0{i}", [128, 512], BF16)) for i in range(4)]
    gb = [stack.enter_context(nc.sbuf_tensor(f"sd_g{i}", [128, 512], BF16)) for i in range(4)]
    ot = [stack.enter_context(nc.sbuf_tensor(f"sd_ot{i}", [128, 512], BF16)) for i in range(4)]
    ps = [stack.enter_context(nc.psum_tensor(f"sd_ps{i}", [128, 512], F32)) for i in range(4)]
    hist = {}

    def H(k):
        v = hist.get(k)
        return [v] if v is not None else list(entry)

    hfree = None
    for c4 in range(4):
        tlo = c4 * 1024
        hl = None
        for kt in range(ktiles):
            hl = prog.op("scalar", lambda s, kt=kt, tlo=tlo: s.dma_start(
                hb[:, kt * 1024:(kt + 1) * 1024],
                interT[kt * 128:(kt + 1) * 128, tlo:tlo + 1024]),
                deps=([hfree] if hfree else list(entry)) if kt == 0 else (),
                inc=16, dma=True, sem="sd_h")
        hready = hl
        g = 0
        pe = None
        for mt in range(16):
            par = mt % 2
            wl = prog.op("sync", lambda s, mt=mt, par=par: s.dma_start(
                wqb[par][:], wdT[mt, :, :]),
                deps=H(("wq", par)), inc=16, dma=True, sem=f"sd_w{par}")
            for half in range(2):
                lo = tlo + half * 512
                bank = g % 4
                slot = g % 4
                deps = [wl, hready] + ([hist.get(("ep", bank))] if ("ep", bank) in hist else [])
                for kt in range(ktiles):
                    pe = prog.op("tensor", lambda t, kt=kt, par=par, bank=bank, half=half: t.matmul(
                        ps[bank][:], wqb[par][:, kt * 128:(kt + 1) * 128],
                        hb[:, kt * 1024 + half * 512:kt * 1024 + half * 512 + 512],
                        start=(kt == 0), stop=(kt == ktiles - 1)),
                        deps=(deps if kt == 0 else ()), inc=(1 if kt == ktiles - 1 else 0))
                hist[("wq", par)] = pe
                sl = prog.op("sync", lambda gg, mt=mt, lo=lo, slot=slot: gg.dma_start(
                    s0b[slot][:], s1T[mt * 128:(mt + 1) * 128, lo:lo + 512]),
                    deps=H(("s0", slot)), inc=16, dma=True, sem=f"sd_s0{slot}")
                gv = prog.op("scalar", lambda s, bank=bank, slot=slot: s.activation(
                    gb[slot][:], ps[bank][:], AF.Silu),
                    deps=[pe] + H(("gb", slot)), inc=1)
                hist[("ep", bank)] = gv
                hv = prog.op("vector", lambda v, slot=slot: v.tensor_mul(
                    ot[slot][:], gb[slot][:], s0b[slot][:]),
                    deps=[gv, sl] + H(("stv", slot)), inc=1)
                hist[("gb", slot)] = hv
                hist[("s0", slot)] = hv
                stv = prog.op("scalar", lambda gg, mt=mt, lo=lo, slot=slot: gg.dma_start(
                    sg1T[mt * 128:(mt + 1) * 128, lo:lo + 512], ot[slot][:]),
                    deps=[hv], inc=16, dma=True, sem=f"sd_st{slot}")
                hist[("stv", slot)] = stv
                g += 1
        hfree = pe
    return prog.barrier_all()



def build_ln(n_tok=T, d=D):
    nc = bass.Bass("TRN2", target_bir_lowering=False)
    xin = nc.dram_tensor("xin", [n_tok, d], F32, kind="ExternalInput")
    xout = nc.dram_tensor("xout", [n_tok, d], BF16, kind="ExternalOutput")
    with ExitStack() as stack:
        block = stack.enter_context(nc.Block())
        ct, cc, cinit = emit_consts(nc, block, stack)
        prog = Prog(nc, stack, "ln")
        emit_ln(nc, stack, prog, cc, (cinit.h, cinit.v), xin[:], xout[:])
        prog.emit(block)
        prog.drain(block)
    return nc


def build_qkv():
    nc = bass.Bass("TRN2", target_bir_lowering=False)
    xn0 = nc.dram_tensor("xn0", [T, D], BF16, kind="ExternalInput")
    wT = nc.dram_tensor("WqkvT", [3 * D // 128, 128, D], F32, kind="ExternalInput")
    outT = nc.dram_tensor("qkvT", [3 * D, T], BF16, kind="ExternalOutput")
    with ExitStack() as stack:
        block = stack.enter_context(nc.Block())
        prog = Prog(nc, stack, "qkv")
        act, act_ev = emit_actT_load(nc, stack, prog, "qkv", xn0[:], D // 128, T)

        def out_store(mt, msz, ci, lo, sz, src, dep):
            g = mt * 8 + ci
            return prog.op("sync", lambda s: s.dma_start(
                outT[mt * 128:mt * 128 + msz, lo:lo + sz], src), deps=[dep],
                inc=16, dma=True, sem=f"qkv_st{g % 4}")

        emit_dense_mm(nc, stack, prog, "qkv", wT=wT[:], M_tot=3 * D, K_tot=D,
                      n_tok=T, act_tile=lambda kt, lo, sz: act(kt)[:, lo:lo + sz],
                      act_ready=act_ev, out_store=out_store)
        prog.emit(block)
        prog.drain(block)
    return nc


def build_l0ffn(debug=False):
    nc = bass.Bass("TRN2", target_bir_lowering=False)
    xn0 = nc.dram_tensor("xn0", [T, D], BF16, kind="ExternalInput")
    w1T = nc.dram_tensor("W1T", [44, 128, D], F32, kind="ExternalInput")
    w2T = nc.dram_tensor("W2T", [44, 128, D], F32, kind="ExternalInput")
    w3T = nc.dram_tensor("W3T", [16, 128, 5632], F32, kind="ExternalInput")
    s0T = nc.dram_tensor("s0T", [D, T], BF16, kind="ExternalInput")
    x = nc.dram_tensor("x", [T, D], F32, kind="ExternalInput")
    x1 = nc.dram_tensor("x1", [T, D], F32, kind="ExternalOutput")
    ikind = "ExternalOutput" if debug else "Internal"
    hidT = nc.dram_tensor("hidT", [5632, T], BF16, kind=ikind)
    w3q = nc.dram_tensor("w3q", [16, 128, 5632], BF16, kind=ikind)
    sgT = nc.dram_tensor("sgT", [D, T], BF16, kind=ikind)
    with ExitStack() as stack:
        block = stack.enter_context(nc.Block())
        ct, cc, cinit = emit_consts(nc, block, stack)
        cinit_ev = (cinit.h, cinit.v)
        prog = Prog(nc, stack, "bl")
        ac = emit_attn_consts(nc, stack, prog, pfx="blc")
        thrb, thr_ev = emit_gamma(nc, stack, prog, ac,
                                  [(w1T[:], 44, D), (w2T[:], 44, D), (w3T[:], 16, 5632)])
        with ExitStack() as s1:
            b1 = emit_bitlinear_hid(nc, s1, prog, ac, cc, cinit_ev, thrb, thr_ev,
                                    w1T[:], w2T[:], xn0[:], hidT[:], entry=[thr_ev])
        with ExitStack() as s2:
            b2 = emit_w3_quant(nc, s2, prog, thrb, thr_ev, w3T[:], w3q[:], entry=b1)
        with ExitStack() as s3:
            b3 = emit_w3_mm(nc, s3, prog, cc, cinit_ev, w3q[:], hidT[:], s0T[:],
                            sgT[:], entry=b2)
        with ExitStack() as s4:
            emit_residual(nc, s4, prog, x[:], sgT[:], x1[:], entry=b3)
        prog.emit(block)
        prog.drain(block)
    return nc


def build_l1attn(debug=False, upto="all"):
    nc = bass.Bass("TRN2", target_bir_lowering=False)
    xn1 = nc.dram_tensor("xn1", [T, D], BF16, kind="ExternalInput")
    winT = nc.dram_tensor("WinT", [81, 128, D], F32, kind="ExternalInput")
    convw = nc.dram_tensor("convw", [128, 192], F32, kind="ExternalInput")
    convb = nc.dram_tensor("convb", [128, 48], F32, kind="ExternalInput")
    dtbias = nc.dram_tensor("dtbias", [64, 1], F32, kind="ExternalInput")
    alog = nc.dram_tensor("alog", [64, 1], F32, kind="ExternalInput")
    dvec = nc.dram_tensor("dvec", [1, 64], F32, kind="ExternalInput")
    nw = nc.dram_tensor("nw", [128, 32], F32, kind="ExternalInput")
    woutT = nc.dram_tensor("WoutT", [16, 128, 4096], F32, kind="ExternalInput")
    s1T = nc.dram_tensor("s1T", [D, T], BF16, kind="ExternalOutput")
    ikind = "ExternalOutput" if debug else "Internal"
    zT = nc.dram_tensor("zT", [4096, T], BF16, kind=ikind)
    xbcT = nc.dram_tensor("xbcT", [6144, T], BF16, kind="Internal")
    convT = nc.dram_tensor("convT", [6144, T], BF16, kind=ikind)
    dtT = nc.dram_tensor("dtT", [64, T], F32, kind=ikind)
    yT = nc.dram_tensor("yT", [4096, T], BF16, kind=ikind)
    ynT = nc.dram_tensor("ynT", [4096, T], BF16, kind="Internal")
    with ExitStack() as stack:
        block = stack.enter_context(nc.Block())
        ct, cc, cinit = emit_consts(nc, block, stack)
        cinit_ev = (cinit.h, cinit.v)
        prog = Prog(nc, stack, "ma")
        ac = emit_attn_consts(nc, stack, prog, pfx="mac")
        with ExitStack() as s1:
            b1 = emit_inproj(nc, s1, prog, zT[:], xbcT[:], dtT[:], xn1[:], winT[:])
        if upto != "inproj":
            with ExitStack() as s2:
                b2 = emit_conv(nc, s2, prog, ac, xbcT[:], convT[:], convw[:], convb[:],
                               entry=b1)
        if upto not in ("inproj", "conv"):
            with ExitStack() as s3:
                b3 = emit_scan(nc, s3, prog, ac, cc, cinit_ev, convT[:], dtT[:], yT[:],
                               dtbias[:], alog[:], dvec[:], entry=b2)
        if upto not in ("inproj", "conv", "scan"):
            with ExitStack() as s4:
                b4 = emit_gating(nc, s4, prog, ac, cc, cinit_ev, yT[:], zT[:], nw[:],
                                 ynT[:], entry=b3)
            with ExitStack() as s5:
                emit_wout(nc, s5, prog, woutT[:], ynT[:], s1T[:], entry=b4)
        prog.emit(block)
        prog.drain(block)
    return nc


def build_scan_only(nck=2, nb=1, stage=9):
    nc = bass.Bass("TRN2", target_bir_lowering=False)
    convT = nc.dram_tensor("convT", [6144, T], BF16, kind="ExternalInput")
    dtT = nc.dram_tensor("dtT", [64, T], F32, kind="ExternalInput")
    dtbias = nc.dram_tensor("dtbias", [64, 1], F32, kind="ExternalInput")
    alog = nc.dram_tensor("alog", [64, 1], F32, kind="ExternalInput")
    dvec = nc.dram_tensor("dvec", [1, 64], F32, kind="ExternalInput")
    yT = nc.dram_tensor("yT", [4096, T], BF16, kind="ExternalOutput")
    with ExitStack() as stack:
        block = stack.enter_context(nc.Block())
        ct, cc, cinit = emit_consts(nc, block, stack)
        cinit_ev = (cinit.h, cinit.v)
        prog = Prog(nc, stack, "sc")
        ac = emit_attn_consts(nc, stack, prog, pfx="scc")
        emit_scan(nc, stack, prog, ac, cc, cinit_ev, convT[:], dtT[:], yT[:],
                  dtbias[:], alog[:], dvec[:], nck=nck, nb=nb, stage=stage)
        prog.emit(block)
        prog.drain(block)
    return nc


_SPLINE_CACHE = {}


def spline_tables_cached():
    if "t" not in _SPLINE_CACHE:
        _SPLINE_CACHE["t"] = host_spline_tables()
    return _SPLINE_CACHE["t"]


def build_l1ffn(debug=False):
    btab_np, bases, slot_mask = spline_tables_cached()
    nc = bass.Bass("TRN2", target_bir_lowering=False)
    xn1 = nc.dram_tensor("xn1", [T, D], BF16, kind="ExternalInput")
    ppg = nc.dram_tensor("ppg", [N_CTRL + 10], F32, kind="ExternalInput")
    ppu = nc.dram_tensor("ppu", [N_CTRL + 10], F32, kind="ExternalInput")
    ppd = nc.dram_tensor("ppd", [N_CTRL + 10], F32, kind="ExternalInput")
    btab = nc.dram_tensor("btab", [NTILES, 128, 6, 128], BF16, kind="ExternalInput")
    s1T = nc.dram_tensor("s1T", [D, T], BF16, kind="ExternalInput")
    x1 = nc.dram_tensor("x1", [T, D], F32, kind="ExternalInput")
    out = nc.dram_tensor("out", [T, D], F32, kind="ExternalOutput")
    ikind = "ExternalOutput" if debug else "Internal"
    wgT = nc.dram_tensor("wgT", [64, 128, 2048], BF16, kind=ikind)
    wuT = nc.dram_tensor("wuT", [64, 128, 2048], BF16, kind="Internal")
    wdT = nc.dram_tensor("wdT", [16, 128, 8192], BF16, kind=ikind)
    interT = nc.dram_tensor("interT", [8192, T], BF16, kind="Internal")
    sg1T = nc.dram_tensor("sg1T", [D, T], BF16, kind="Internal")
    with ExitStack() as stack:
        block = stack.enter_context(nc.Block())
        ct, cc, cinit = emit_consts(nc, block, stack)
        cinit_ev = (cinit.h, cinit.v)
        prog = Prog(nc, stack, "sp")
        with ExitStack() as s1:
            b1 = emit_spline_recon(nc, s1, prog, btab[:], list(bases),
                                   slot_mask, [ppg[:], ppu[:], ppd[:]],
                                   [wgT[:], wuT[:], wdT[:]])
        prog.phase_reset()
        with ExitStack() as s2:
            b2 = emit_spline_mm(nc, s2, prog, cc, cinit_ev, wgT[:], wuT[:],
                                wdT[:], xn1[:], s1T[:], interT[:], sg1T[:],
                                entry=b1)
        prog.phase_reset()
        with ExitStack() as s3:
            b3 = emit_spline_down(nc, s3, prog, cc, cinit_ev, wdT[:], interT[:],
                                  s1T[:], sg1T[:], entry=b2)
        prog.phase_reset()
        with ExitStack() as s4:
            emit_residual(nc, s4, prog, x1[:], sg1T[:], out[:], entry=b3)
        prog.emit(block)
        prog.drain(block)
    return nc


def build_attn():
    nc = bass.Bass("TRN2", target_bir_lowering=False)
    qkvT = nc.dram_tensor("qkvT", [3 * D, T], BF16, kind="ExternalInput")
    wprojT = nc.dram_tensor("WprojT", [16, 128, D], F32, kind="ExternalInput")
    s0T = nc.dram_tensor("s0T", [D, T], BF16, kind="ExternalOutput")
    with ExitStack() as stack:
        block = stack.enter_context(nc.Block())
        ct, cc, cinit = emit_consts(nc, block, stack)
        cinit_ev = (cinit.h, cinit.v)
        prog = Prog(nc, stack, "at")
        ac = emit_attn_consts(nc, stack, prog)
        emit_attn(nc, stack, prog, cc, cinit_ev, ac, qkvT[:], s0T[:], wprojT[:])
        prog.emit(block)
        prog.drain(block)
    return nc


def build_fused():
    btab_np, bases, slot_mask = spline_tables_cached()
    nc = bass.Bass("TRN2", target_bir_lowering=False)
    EI = "ExternalInput"
    x = nc.dram_tensor("x", [T, D], F32, kind=EI)
    wqkvT = nc.dram_tensor("WqkvT", [48, 128, D], F32, kind=EI)
    wprojT = nc.dram_tensor("WprojT", [16, 128, D], F32, kind=EI)
    w1T = nc.dram_tensor("W1T", [44, 128, D], F32, kind=EI)
    w2T = nc.dram_tensor("W2T", [44, 128, D], F32, kind=EI)
    w3T = nc.dram_tensor("W3T", [16, 128, 5632], F32, kind=EI)
    winT = nc.dram_tensor("WinT", [81, 128, D], F32, kind=EI)
    convw = nc.dram_tensor("convw", [128, 192], F32, kind=EI)
    convb = nc.dram_tensor("convb", [128, 48], F32, kind=EI)
    dtbias = nc.dram_tensor("dtbias", [64, 1], F32, kind=EI)
    alog = nc.dram_tensor("alog", [64, 1], F32, kind=EI)
    dvec = nc.dram_tensor("dvec", [1, 64], F32, kind=EI)
    nw = nc.dram_tensor("nw", [128, 32], F32, kind=EI)
    woutT = nc.dram_tensor("WoutT", [16, 128, 4096], F32, kind=EI)
    ppg = nc.dram_tensor("ppg", [N_CTRL + 10], F32, kind=EI)
    ppu = nc.dram_tensor("ppu", [N_CTRL + 10], F32, kind=EI)
    ppd = nc.dram_tensor("ppd", [N_CTRL + 10], F32, kind=EI)
    btab = nc.dram_tensor("btab", [NTILES, 128, 6, 128], BF16, kind=EI)
    out = nc.dram_tensor("out", [T, D], F32, kind="ExternalOutput")
    IN = "Internal"
    xn0 = nc.dram_tensor("xn0", [T, D], BF16, kind=IN)
    qkvT = nc.dram_tensor("qkvT", [3 * D, T], BF16, kind=IN)
    s0T = nc.dram_tensor("s0T", [D, T], BF16, kind=IN)
    hidT = nc.dram_tensor("hidT", [5632, T], BF16, kind=IN)
    w3q = nc.dram_tensor("w3q", [16, 128, 5632], BF16, kind=IN)
    sgT = nc.dram_tensor("sgT", [D, T], BF16, kind=IN)
    x1 = nc.dram_tensor("x1", [T, D], F32, kind=IN)
    xn1 = nc.dram_tensor("xn1", [T, D], BF16, kind=IN)
    zT = nc.dram_tensor("zT", [4096, T], BF16, kind=IN)
    xbcT = nc.dram_tensor("xbcT", [6144, T], BF16, kind=IN)
    convT = nc.dram_tensor("convT", [6144, T], BF16, kind=IN)
    dtT = nc.dram_tensor("dtT", [64, T], F32, kind=IN)
    yT = nc.dram_tensor("yT", [4096, T], BF16, kind=IN)
    ynT = nc.dram_tensor("ynT", [4096, T], BF16, kind=IN)
    s1T = nc.dram_tensor("s1T", [D, T], BF16, kind=IN)
    wgT = nc.dram_tensor("wgT", [64, 128, 2048], BF16, kind=IN)
    wuT = nc.dram_tensor("wuT", [64, 128, 2048], BF16, kind=IN)
    wdT = nc.dram_tensor("wdT", [16, 128, 8192], BF16, kind=IN)
    interT = nc.dram_tensor("interT", [8192, T], BF16, kind=IN)
    sg1T = nc.dram_tensor("sg1T", [D, T], BF16, kind=IN)

    with ExitStack() as stack:
        block = stack.enter_context(nc.Block())
        ct, cc, cinit = emit_consts(nc, block, stack)
        cie = (cinit.h, cinit.v)
        prog = Prog(nc, stack, "f")
        ac = emit_attn_consts(nc, stack, prog, pfx="fc")

        def nxt(b):
            prog.phase_reset()
            return b

        rstack = stack.enter_context(ExitStack())
        rres = emit_spline_recon(nc, rstack, prog, btab[:], list(bases),
                                 slot_mask, [ppg[:], ppu[:], ppd[:]],
                                 [wgT[:], wuT[:], wdT[:]], entry=[cie])

        def rchunk(lo, hi):
            prog.side_begin()
            emit_spline_recon(nc, rstack, prog, btab[:], list(bases),
                              slot_mask, [ppg[:], ppu[:], ppd[:]],
                              [wgT[:], wuT[:], wdT[:]], taus=range(lo, hi),
                              res=rres)
            return prog.side_end()

        with ExitStack() as s:
            prog.phase = "ln0"
            b = nxt(emit_ln(nc, s, prog, cc, cie, x[:], xn0[:], pfx="l0"))
        with ExitStack() as s:
            prog.phase = "gamma"
            thrb, thr_ev = emit_gamma(nc, stack, prog, ac,
                                      [(w1T[:], 44, D), (w2T[:], 44, D),
                                       (w3T[:], 16, 5632)], entry=b)
            b = nxt(prog.barrier_all())
        with ExitStack() as ps_seg:
            rres["ps_tr"] = [ps_seg.enter_context(
                nc.psum_tensor(f"rA{w}", [128, 1024], BF16)) for w in range(3)]
            prog.mark("A0")
            prog.phase = "qkv"
            with ExitStack() as s:
                qk_buf = s.enter_context(
                    nc.sbuf_tensor("qk_actT", [128, 16 * S], BF16))
                qk_res = {
                    "wb": [s.enter_context(nc.sbuf_tensor(
                        f"qk_wb{i}", [128, 16 * 128], BF16)) for i in range(2)],
                    "ps": [s.enter_context(nc.psum_tensor(
                        f"qk_ps{i}", [128, 512], F32)) for i in range(4)],
                    "ot": [s.enter_context(nc.sbuf_tensor(
                        f"qk_ot{i}", [128, 512], BF16)) for i in range(4)],
                }
                for hh in range(2):
                    act, aev = emit_actT_load(nc, s, prog, f"qk{hh}", xn0[:],
                                              D // 128, S, entry=b,
                                              tok_lo=hh * S, buf=qk_buf)

                    def qst(mt, msz, ci, lo, sz, src, dep, hh=hh):
                        g = mt * 4 + ci
                        return prog.op("sync", lambda ss: ss.dma_start(
                            qkvT[mt * 128:mt * 128 + msz,
                                 hh * S + lo:hh * S + lo + sz], src), deps=[dep],
                            inc=16, dma=True, sem=f"qk_st{g % 4}")
                    gpe, gep, gst = emit_dense_mm(
                        nc, s, prog, f"qk{hh}", wT=wqkvT[:], M_tot=3 * D,
                        K_tot=D, n_tok=S,
                        act_tile=lambda kt, lo, sz: act(kt)[:, lo:lo + sz],
                        act_ready=aev, out_store=qst, entry=b, res=qk_res)
                    qk_res["guard"] = gpe[-1]
                    qk_res["guard_st"] = gst[-1]
                    b = nxt(prog.barrier_all())
            prog.phase = "blhid"
            with ExitStack() as s:
                bl_bufs = None
                for hh in range(2):
                    bb, bl_bufs = emit_bitlinear_hid(
                        nc, s, prog, ac, cc, cie, thrb, thr_ev, w1T[:], w2T[:],
                        xn0[:], hidT[:], entry=b, tok_lo=hh * S, n_tok=S,
                        pfx="bl", bufs=bl_bufs)
                    b = nxt(bb)
            prog.mark("A1")
            prog.phase = "recon"
            prog.splice(rchunk(0, 360), "A0", "A1")
        b = b + prog.r_totals()
        with ExitStack() as s:
            prog.phase = "attn"
            emit_attn(nc, s, prog, cc, cie, ac, qkvT[:], s0T[:], wprojT[:],
                      entry=b)
            b = nxt(prog.barrier_all())
        with ExitStack() as ps_seg:
            rres["ps_tr"] = [ps_seg.enter_context(
                nc.psum_tensor(f"rB{w}", [128, 1024], BF16)) for w in range(3)]
            prog.mark("B0")
            with ExitStack() as s:
                prog.phase = "w3q"
                b = nxt(emit_w3_quant(nc, s, prog, thrb, thr_ev, w3T[:], w3q[:],
                                      entry=b))
            with ExitStack() as s:
                prog.phase = "w3mm"
                b = nxt(emit_w3_mm(nc, s, prog, cc, cie, w3q[:], hidT[:], s0T[:],
                                   sgT[:], entry=b))
            with ExitStack() as s:
                prog.phase = "resid0"
                b = nxt(emit_residual(nc, s, prog, x[:], sgT[:], x1[:], entry=b))
            with ExitStack() as s:
                prog.phase = "ln1"
                b = nxt(emit_ln(nc, s, prog, cc, cie, x1[:], xn1[:], pfx="l1",
                                entry=b))
            prog.phase = "inproj"
            with ExitStack() as s:
                ip_bufs = None
                for hh in range(2):
                    bb, ip_bufs = emit_inproj(nc, s, prog, zT[:], xbcT[:], dtT[:],
                                              xn1[:], winT[:], entry=b,
                                              tok_lo=hh * S, n_tok=S, pfx="ip",
                                              bufs=ip_bufs)
                    b = nxt(bb)
            with ExitStack() as s:
                prog.phase = "conv"
                b = nxt(emit_conv(nc, s, prog, ac, xbcT[:], convT[:], convw[:],
                                  convb[:], entry=b))
            prog.mark("B1")
            prog.phase = "recon"
            prog.splice(rchunk(360, 660), "B0", "B1")
        b = b + prog.r_totals()
        with ExitStack() as s:
            prog.phase = "scan"
            b = nxt(emit_scan(nc, s, prog, ac, cc, cie, convT[:], dtT[:], yT[:],
                              dtbias[:], alog[:], dvec[:], entry=b))
        with ExitStack() as ps_seg:
            rres["ps_tr"] = [ps_seg.enter_context(
                nc.psum_tensor(f"rC{w}", [128, 1024], BF16)) for w in range(3)]
            prog.mark("C0")
            with ExitStack() as s:
                prog.phase = "gating"
                b = nxt(emit_gating(nc, s, prog, ac, cc, cie, yT[:], zT[:], nw[:],
                                    ynT[:], entry=b))
            with ExitStack() as s:
                prog.phase = "wout"
                b = nxt(emit_wout(nc, s, prog, woutT[:], ynT[:], s1T[:], entry=b))
            prog.mark("C1")
            prog.phase = "recon"
            prog.splice(rchunk(660, NTILES), "C0", "C1")
        rstack.close()
        b = b + prog.r_totals()
        with ExitStack() as s:
            prog.phase = "splmm"
            b = nxt(emit_spline_mm(nc, s, prog, cc, cie, wgT[:], wuT[:], wdT[:],
                                   xn1[:], s1T[:], interT[:], sg1T[:], entry=b))
        with ExitStack() as s:
            prog.phase = "spldown"
            b = nxt(emit_spline_down(nc, s, prog, cc, cie, wdT[:], interT[:],
                                     s1T[:], sg1T[:], entry=b))
        with ExitStack() as s:
            prog.phase = "resid1"
            nxt(emit_residual(nc, s, prog, x1[:], sg1T[:], out[:], entry=b,
                              pfx="rs2"))
        prog.apply_splices()
        prog.emit(block)
        prog.drain(block)
    return nc


MULTI_LAUNCH = False
_NC_CACHE = {}


def _get_nc(name, builder):
    if name not in _NC_CACHE:
        _NC_CACHE[name] = builder()
    return _NC_CACHE[name]


def kernel(L0_attn_x, L0_attn_Wqkv, L0_attn_Wproj, L0_ffn_w1, L0_ffn_w2,
           L0_ffn_w3, L1_attn_W_in, L1_attn_conv_w, L1_attn_conv_b,
           L1_attn_dt_bias, L1_attn_A_log, L1_attn_D, L1_attn_norm_weight,
           L1_attn_W_out, L1_ffn_gate_cp, L1_ffn_up_cp, L1_ffn_down_cp):
    cores = list(range(N_CORES))
    f32 = np.float32

    def strips(w):
        return host_strips(np.ascontiguousarray(np.asarray(w, f32).T))

    x = np.ascontiguousarray(np.asarray(L0_attn_x, f32))
    xs = [x[2 * c:2 * c + 2].reshape(T, D) for c in cores]
    wqkv_s = strips(L0_attn_Wqkv)
    wproj_s = strips(L0_attn_Wproj)
    w1_s = strips(L0_ffn_w1)
    w2_s = strips(L0_ffn_w2)
    w3_s = strips(L0_ffn_w3)
    win_s = strips(L1_attn_W_in)
    wout_s = strips(L1_attn_W_out)
    convw = np.ascontiguousarray(np.asarray(L1_attn_conv_w, f32)
                                 .reshape(48, 128, 4).transpose(1, 0, 2).reshape(128, 192))
    convb = np.ascontiguousarray(np.asarray(L1_attn_conv_b, f32).reshape(48, 128).T)
    dtb = np.asarray(L1_attn_dt_bias, f32).reshape(64, 1)
    alog = np.asarray(L1_attn_A_log, f32).reshape(64, 1)
    dvec = np.asarray(L1_attn_D, f32).reshape(1, 64)
    nw = np.ascontiguousarray(np.asarray(L1_attn_norm_weight, f32).reshape(32, 128).T)
    ppg = host_pp_ext(np.asarray(L1_ffn_gate_cp, f32))
    ppu = host_pp_ext(np.asarray(L1_ffn_up_cp, f32))
    ppd = host_pp_ext(np.asarray(L1_ffn_down_cp, f32))
    btab, _bases, _sm = spline_tables_cached()

    if not MULTI_LAUNCH:
        nc = _get_nc("fused", build_fused)
        in_maps = [{
            "x": xs[c], "WqkvT": wqkv_s, "WprojT": wproj_s, "W1T": w1_s,
            "W2T": w2_s, "W3T": w3_s, "WinT": win_s, "convw": convw,
            "convb": convb, "dtbias": dtb, "alog": alog, "dvec": dvec,
            "nw": nw, "WoutT": wout_s, "ppg": ppg, "ppu": ppu, "ppd": ppd,
            "btab": btab} for c in cores]
        res = run_bass_kernel_spmd(nc, in_maps, core_ids=cores)
        return np.concatenate(
            [res.results[c]["out"].reshape(2, S, D) for c in cores], axis=0)

    def launch(name, builder, in_map_fn):
        nc = _get_nc(name, builder)
        res = run_bass_kernel_spmd(nc, [in_map_fn(c) for c in cores], core_ids=cores)
        return res.results

    r = launch("ln", build_ln, lambda c: {"xin": xs[c]})
    xn0 = [r[c]["xout"] for c in cores]
    r = launch("qkv", build_qkv, lambda c: {"xn0": xn0[c], "WqkvT": wqkv_s})
    qkvT = [r[c]["qkvT"] for c in cores]
    r = launch("attn", build_attn, lambda c: {"qkvT": qkvT[c], "WprojT": wproj_s})
    s0T = [r[c]["s0T"] for c in cores]
    r = launch("l0ffn", build_l0ffn, lambda c: {
        "xn0": xn0[c], "W1T": w1_s, "W2T": w2_s, "W3T": w3_s,
        "s0T": s0T[c], "x": xs[c]})
    x1 = [r[c]["x1"] for c in cores]
    r = launch("ln", build_ln, lambda c: {"xin": x1[c]})
    xn1 = [r[c]["xout"] for c in cores]
    r = launch("l1attn", build_l1attn, lambda c: {
        "xn1": xn1[c], "WinT": win_s, "convw": convw, "convb": convb,
        "dtbias": dtb, "alog": alog, "dvec": dvec, "nw": nw, "WoutT": wout_s})
    s1T = [r[c]["s1T"] for c in cores]
    r = launch("l1ffn", build_l1ffn, lambda c: {
        "xn1": xn1[c], "ppg": ppg, "ppu": ppu, "ppd": ppd, "btab": btab,
        "s1T": s1T[c], "x1": x1[c]})
    out = np.concatenate([r[c]["out"].reshape(2, S, D) for c in cores], axis=0)
    return out
